# version 15 (final)
# baseline (speedup 1.0000x reference)
import numpy as np
import ml_dtypes
from contextlib import ExitStack

import concourse.bass as bass
import concourse.mybir as mybir
from concourse.bass_utils import run_bass_kernel_spmd

F32 = mybir.dt.float32
BF16 = mybir.dt.bfloat16
U8 = mybir.dt.uint8
AF = mybir.ActivationFunctionType
ALU = mybir.AluOpType
AX = mybir.AxisListType
BF = ml_dtypes.bfloat16

T, D, H, KH = 2048, 1024, 16, 64
NCORE = 8
CH = 128
NCH = T // CH
NDB = D // 128
NTC = T // 128
DI = 2048
NDI = DI // 128
DS = 16
DTR = 64
FF = 4096
NFB = FF // 128
E = 8
TSL = 512
MTS = 1024
LNEPS = 1e-5
GNEPS = 1e-5 * 64.0
RMSEPS = 1e-6

UNFUSED = False
STOP_AFTER = None

SEGS = ("l0a", "moe", "mamba", "l1f")


def host_constants():
    c = {}
    c["c_ident"] = np.eye(128, dtype=np.float32).astype(BF)
    tau = np.arange(128)[:, None]
    tt = np.arange(128)[None, :]
    c["c_Lf"] = -(tau <= tt).astype(np.float32)
    c["c_Lb"] = (tau > tt).astype(np.float32)
    c["c_mask_f"] = (tau <= tt).astype(np.uint8)
    c["c_mask_b"] = (tau > tt).astype(np.uint8)
    return c


def _colpack(v, nb):
    return np.ascontiguousarray(np.asarray(v, np.float32).reshape(nb, 128).T)


def shard_inputs(ins):
    f32 = lambda a: np.ascontiguousarray(np.asarray(a, dtype=np.float32))
    cm = {}
    for nm, key in (("w_Wr", "L0_attn_Wr"), ("w_Wk", "L0_attn_Wk"),
                    ("w_Wv", "L0_attn_Wv"), ("w_Wg", "L0_attn_Wg"),
                    ("w_Wo", "L0_attn_Wo"), ("w_gate", "L0_attn_gate_w"),
                    ("w_maa_w1", "L0_attn_maa_w1"), ("w_decay_w1", "L0_attn_decay_w1"),
                    ("w_decay_w2", "L0_attn_decay_w2")):
        cm[nm] = f32(ins[key])
    cm["w_maa_w2"] = f32(ins["L0_attn_maa_w2"])
    cm["v_td_row"] = f32(ins["L0_attn_time_decay"]).reshape(1, D)
    cm["v_convw"] = _colpack(f32(ins["L0_attn_conv_w"]).reshape(D, 3).T.reshape(3 * D), 3 * NDB) \
        if False else np.ascontiguousarray(
            f32(ins["L0_attn_conv_w"]).reshape(D, 3).reshape(NDB, 128, 3).transpose(1, 0, 2).reshape(128, NDB * 3))
    for nm in ("maa_x", "maa_w", "maa_k", "maa_v", "maa_r", "maa_g"):
        cm["r_" + nm] = f32(ins["L0_attn_" + nm]).reshape(1, D)
    cm["c_ln_w"] = _colpack(ins["L0_attn_ln_w"], NDB)
    cm["c_maa_x"] = _colpack(ins["L0_attn_maa_x"], NDB)
    cm["r_ln_w"] = f32(ins["L0_attn_ln_w"]).reshape(1, D)
    cm["r_ln_b"] = f32(ins["L0_attn_ln_b"]).reshape(1, D)
    cm["r_gate_b"] = f32(ins["L0_attn_gate_b"]).reshape(1, D)
    cm["r_rms_w"] = f32(ins["L0_ffn_rms_w"]).reshape(1, D)
    cm["w_moe_gateT"] = f32(ins["L0_ffn_gate_w"]).T.copy()
    cm["w_moe_W1"] = np.ascontiguousarray(
        f32(ins["L0_ffn_W1"]).reshape(E, D, 4, 1024).transpose(0, 2, 1, 3))
    cm["w_moe_W2"] = np.ascontiguousarray(
        f32(ins["L0_ffn_W2"]).reshape(E, FF, 8, 128).transpose(0, 2, 1, 3))
    cm["c_moe_b1"] = np.ascontiguousarray(
        f32(ins["L0_ffn_b1"]).reshape(E, NFB, 128).transpose(0, 2, 1))
    cm["w_moe_b2"] = f32(ins["L0_ffn_b2"])
    cm["r_l1n_w"] = f32(ins["L1_attn_norm_w"]).reshape(1, D)
    cm["r_l1n_b"] = f32(ins["L1_attn_norm_b"]).reshape(1, D)
    cm["w_WinT"] = f32(ins["L1_attn_W_in"]).T.copy()
    cm["c_mconv_w"] = np.ascontiguousarray(
        f32(ins["L1_attn_conv_w"]).reshape(NDI, 128, 4).transpose(1, 0, 2).reshape(128, NDI * 4))
    cm["c_mconv_b"] = _colpack(ins["L1_attn_conv_b"], NDI)
    cm["w_WxT"] = np.ascontiguousarray(np.pad(f32(ins["L1_attn_W_x"]).T, ((0, 0), (0, 32))))
    cm["w_WdtT"] = f32(ins["L1_attn_W_dt"]).T.copy()
    cm["c_bdt"] = _colpack(ins["L1_attn_b_dt"], NDI)
    cm["c_Alog"] = np.ascontiguousarray(
        f32(ins["L1_attn_A_log"]).reshape(NDI, 128, DS).transpose(1, 0, 2).reshape(128, NDI * DS))
    cm["c_mD"] = _colpack(ins["L1_attn_D"], NDI)
    cm["w_WoutT"] = f32(ins["L1_attn_W_out"]).T.copy()
    cm["w_fb_T"] = f32(ins["L1_ffn_fast_b"]).T.copy()
    cm["w_fc"] = f32(ins["L1_ffn_w_fc"])
    cm["w_fcT"] = f32(ins["L1_ffn_w_fc"]).T.copy()
    cm["w_proj"] = f32(ins["L1_ffn_w_proj"])
    cm["w_projT"] = f32(ins["L1_ffn_w_proj"]).T.copy()
    cm["w_lrv"] = f32(ins["L1_ffn_v"])
    cm["w_lruT"] = f32(ins["L1_ffn_u"]).T.copy()
    cm["v_lrgate"] = f32(ins["L1_ffn_gate"]).reshape(1, 1)
    cm.update(host_constants())
    x = f32(ins["L0_attn_x"])
    return [dict(cm, x0=x[b]) for b in range(NCORE)]


IN_SPECS = {
    "x0": ([T, D], F32),
    "w_Wr": ([D, D], F32), "w_Wk": ([D, D], F32), "w_Wv": ([D, D], F32),
    "w_Wg": ([D, D], F32), "w_Wo": ([D, D], F32), "w_gate": ([D, D], F32),
    "w_maa_w1": ([D, 160], F32), "w_maa_w2": ([5, 32, D], F32),
    "w_decay_w1": ([D, 64], F32), "w_decay_w2": ([64, D], F32),
    "v_td_row": ([1, D], F32), "v_convw": ([128, NDB * 3], F32),
    "r_maa_x": ([1, D], F32), "r_maa_w": ([1, D], F32), "r_maa_k": ([1, D], F32),
    "r_maa_v": ([1, D], F32), "r_maa_r": ([1, D], F32), "r_maa_g": ([1, D], F32),
    "c_ln_w": ([128, NDB], F32), "c_maa_x": ([128, NDB], F32), "r_ln_w": ([1, D], F32), "r_ln_b": ([1, D], F32),
    "r_gate_b": ([1, D], F32),
    "r_rms_w": ([1, D], F32), "w_moe_gateT": ([D, E], F32),
    "w_moe_W1": ([E, 4, D, 1024], F32), "w_moe_W2": ([E, 8, FF, 128], F32),
    "c_moe_b1": ([E, 128, NFB], F32), "w_moe_b2": ([E, D], F32),
    "r_l1n_w": ([1, D], F32), "r_l1n_b": ([1, D], F32),
    "w_WinT": ([D, 2 * DI], F32), "c_mconv_w": ([128, NDI * 4], F32),
    "c_mconv_b": ([128, NDI], F32),
    "w_WxT": ([DI, 128], F32), "w_WdtT": ([DTR, DI], F32),
    "c_bdt": ([128, NDI], F32), "c_Alog": ([128, NDI * DS], F32),
    "c_mD": ([128, NDI], F32), "w_WoutT": ([DI, D], F32),
    "w_fb_T": ([16, 16], F32), "w_fc": ([FF, D], F32), "w_fcT": ([D, FF], F32),
    "w_proj": ([D, FF], F32), "w_projT": ([FF, D], F32),
    "w_lrv": ([16, D], F32), "w_lruT": ([16, FF], F32), "v_lrgate": ([1, 1], F32),
    "c_ident": ([128, 128], BF16), "c_Lf": ([128, 128], F32), "c_Lb": ([128, 128], F32),
    "c_mask_f": ([128, 128], U8), "c_mask_b": ([128, 128], U8),
}

SCRATCH = {
    "s_xn0tm": ([NTC, 128, D], BF16),
    "s_r": ([NCH, 128, D], BF16),
    "s_k": ([NCH, 128, D], BF16),
    "s_v": ([NTC, 128, D], BF16),
    "s_g": ([NTC, 128, D], BF16),
    "s_sig": ([NTC, 128, D], BF16),
    "s_attn": ([NTC, 128, D], BF16),
    "s_ffn": ([NTC, 128, D], BF16),
    "s_x1": ([NTC, 128, D], F32),
    "s_xn1": ([NTC, 128, D], BF16),
    "s_xn1f": ([NDB, 128, T], BF16),
    "s_xmf": ([NDB, 128, T], BF16),
    "s_u": ([NDI, 128, T], BF16),
    "s_z": ([NDI, 128, T], BF16),
    "s_delta": ([NDI, 128, T], BF16),
    "s_du": ([NDI, 128, T], BF16),
    "s_yac": ([NDI, 128, T], BF16),
    "s_mamba": ([NTC, 128, D], BF16),
    "s_ffn2": ([NTC, 128, D], BF16),
}

SEG_IO = {
    "l0a": ([], ["s_xn0tm", "s_attn", "s_sig", "s_g", "s_r", "s_k", "s_v"]),
    "moe": (["s_xn0tm", "s_attn"], ["s_ffn", "s_x1", "s_xn1", "s_xn1f", "s_xmf"]),
    "mamba": (["s_xmf"], ["s_u", "s_z", "s_delta", "s_du", "s_yac", "s_mamba"]),
    "l1f": (["s_xn1f", "s_xn1", "s_x1", "s_mamba"], []),
}


class Ctx:
    def __init__(self, nc):
        self.nc = nc
        self.stack = ExitStack()
        self.ostack = self.stack
        self._sems = {}

    def init_sems(self):
        for n in ("pp", "cc", "qq", "dd", "eT", "eV", "eS", "eG", "eY"):
            self.sem(n)

    def sem(self, name):
        if name not in self._sems:
            h = self.ostack.enter_context(self.nc.semaphore(name))
            self._sems[name] = [h, 0]
        return self._sems[name]

    def uname(self, base):
        if not hasattr(self, "_uc"):
            self._uc = 0
        self._uc += 1
        return f"{base}_{self._uc}"

    def setup_consts(self, vals=(-1.0, -0.5, LNEPS, GNEPS, RMSEPS)):
        nc = self.nc
        tile = self.ostack.enter_context(
            nc.sbuf_tensor("cstpool", [128, len(vals)], F32))
        with nc.Block() as block:
            def ms(v):
                last = None
                for i, val in enumerate(vals):
                    last = v.memset(tile[:, i:i + 1], float(val))
                self.inc("eV", last)
            block.vector(ms)
        for i, val in enumerate(vals):
            nc.const_aps.register(F32, float(val), tile[:, i:i + 1])

    def push(self):
        if not hasattr(self, "_saved"):
            self._saved = []
        self._saved.append(self.stack)
        self.stack = ExitStack()

    def pop(self):
        self.stack.close()
        self.stack = self._saved.pop()

    def scope(self):
        from contextlib import contextmanager

        @contextmanager
        def _s():
            old = self.stack
            self.stack = ExitStack()
            try:
                with self.stack:
                    yield
            finally:
                self.stack = old
        return _s()

    def inc(self, name, instr, k=1):
        s = self.sem(name)
        instr.then_inc(s[0], k)
        s[1] += k
        return instr

    def val(self, name):
        return self.sem(name)[1]

    def wait(self, eng, name, val=None):
        s = self.sem(name)
        eng.wait_ge(s[0], s[1] if val is None else val)


ENG_SEM = {"tensor": "eT", "vector": "eV", "scalar": "eS", "gpsimd": "eG",
           "sync": "eY"}


def _nop(e):
    return e.engine_nop() if hasattr(e, "engine_nop") else e.nop()


def ld3(dst2d, src2d, p=128):
    j = src2d.shape[0] // p
    return (dst2d.rearrange("p (j c) -> p j c", j=j),
            src2d.rearrange("(j p) c -> p j c", p=p))


class Seg:

    def __init__(self, c, block):
        self.c = c
        self.block = block

    def step(self, eng, deps, fn):
        c = self.c
        my = ENG_SEM[eng]
        snap = [(ENG_SEM[d], c.val(ENG_SEM[d])) for d in deps if c.val(ENG_SEM[d]) > 0]

        def body(e):
            for s, v in snap:
                e.wait_ge(c.sem(s)[0], v)
            fn(e, lambda instr, k=1: c.inc(my, instr, k))

        getattr(self.block, eng)(body)

    def barrier_wait(self, eng, engs=None):
        c = self.c

        def body(e):
            for d in (engs or ENG_SEM):
                if d == eng:
                    continue
                v = c.val(ENG_SEM[d])
                if v > 0:
                    e.wait_ge(c.sem(ENG_SEM[d])[0], v)

        getattr(self.block, eng)(body)


def pipe(sg, n, produce, consume, nslots, peng="tensor", ceng="vector", pwait=None):
    c = sg.c
    p0 = c.val("pp")
    q0 = c.val("cc")

    def _p(e):
        if pwait:
            for nm_, vv_ in pwait:
                e.wait_ge(c.sem(nm_)[0], vv_)
        for i in range(n):
            w = q0 + i - nslots + 1
            if w > 0:
                c.wait(e, "cc", w)
            c.inc("pp", produce(e, i, i % nslots))
        c.wait(e, "pp")
        c.inc(ENG_SEM[peng], _nop(e))

    def _c(e):
        for i in range(n):
            c.wait(e, "pp", p0 + i + 1)
            c.inc("cc", consume(e, i, i % nslots))
        c.wait(e, "cc")
        c.inc(ENG_SEM[ceng], _nop(e))

    getattr(sg.block, peng)(_p)
    getattr(sg.block, ceng)(_c)


def ln_rows_tm(sg, x_tile, out_tile, tmp, eps, extra=None):
    raise NotImplementedError


def seg_l0a(c, io, scr):
    nc = c.nc
    ident = c.stack.enter_context(nc.sbuf_tensor("ident", [128, 128], BF16))
    maskf = c.stack.enter_context(nc.sbuf_tensor("maskf", [128, 128], U8))
    maskb = c.stack.enter_context(nc.sbuf_tensor("maskb", [128, 128], U8))
    Lf = c.stack.enter_context(nc.sbuf_tensor("Lf", [128, 128], F32))
    Lb = c.stack.enter_context(nc.sbuf_tensor("Lb", [128, 128], F32))
    convw = c.stack.enter_context(nc.sbuf_tensor("convw", [128, NDB * 3], F32))
    cmaax = c.stack.enter_context(nc.sbuf_tensor("cmaax", [128, NDB], F32))
    ones1 = c.stack.enter_context(nc.sbuf_tensor("ones1", [1, 128], F32))
    ones1b = c.stack.enter_context(nc.sbuf_tensor("ones1b", [1, 128], BF16))
    rows = {}
    for nm in ("r_maa_x", "r_maa_w", "r_maa_k", "r_maa_v", "r_maa_r", "r_maa_g",
               "r_gate_b", "r_ln_w", "r_ln_b"):
        rows[nm] = c.stack.enter_context(nc.sbuf_tensor(nm + "_sb", [1, D], BF16))
    rows["v_td_row"] = c.stack.enter_context(nc.sbuf_tensor("v_td_row_sb", [1, D], F32))
    dw1 = c.stack.enter_context(nc.sbuf_tensor("dw1", [128, NDB * 64], BF16))
    dw2 = c.stack.enter_context(nc.sbuf_tensor("dw2", [64, D], BF16))
    mw1 = c.stack.enter_context(nc.sbuf_tensor("mw1", [128, NDB * 160], BF16))
    w2aug = c.stack.enter_context(nc.sbuf_tensor("w2aug", [33, D], BF16))

    dxf = c.stack.enter_context(nc.sbuf_tensor("dxf", [128, NDB * T], BF16))
    y_tm = dxf
    wbuf = c.stack.enter_context(nc.sbuf_tensor("wbuf", [128, NDB * D], BF16))
    tanhw = c.stack.enter_context(nc.sbuf_tensor("tanhw", [64, T], BF16))
    tdbc = c.stack.enter_context(nc.sbuf_tensor("tdbc", [128, D], F32))
    gbbc = c.stack.enter_context(nc.sbuf_tensor("gbbc", [128, D], F32))
    lnwbc = c.stack.enter_context(nc.sbuf_tensor("lnwbc", [128, D], BF16))
    lnbbc = c.stack.enter_context(nc.sbuf_tensor("lnbbc", [128, D], BF16))
    stage = c.stack.enter_context(nc.sbuf_tensor("stage", [128, 2 * D], BF16))

    with nc.Block() as block:
        sg = Seg(c, block)

        def ld(s, fin):
            fin(s.dma_start(ident[:, :], io["c_ident"]), 16)
            fin(s.dma_start(maskf[:, :], io["c_mask_f"]), 16)
            fin(s.dma_start(maskb[:, :], io["c_mask_b"]), 16)
            fin(s.dma_start(Lf[:, :], io["c_Lf"]), 16)
            fin(s.dma_start(Lb[:, :], io["c_Lb"]), 16)
            fin(s.dma_start(convw[:, :], io["v_convw"]), 16)
            fin(s.dma_start(cmaax[:, :], io["c_maa_x"]), 16)
            fin(s.dma_start(rows["v_td_row"][:, :], io["v_td_row"]), 16)
        sg.step("sync", [], ld)

        def ldg(g, fin):
            for nm in rows:
                if nm != "v_td_row":
                    fin(g.dma_start(rows[nm][:, :], io[nm]), 16)
            fin(g.dma_start(*ld3(dw1[:, :], io["w_decay_w1"])), 16)
            fin(g.dma_start(dw2[:, :], io["w_decay_w2"]), 16)
            fin(g.dma_start(*ld3(mw1[:, :], io["w_maa_w1"])), 16)
        sg.step("gpsimd", [], ldg)

        def ms(v, fin):
            fin(v.memset(ones1[:, :], 1.0))
            fin(v.memset(ones1b[:, :], 1.0))
        sg.step("vector", [], ms)

    c.push()
    xn0f = c.stack.enter_context(nc.sbuf_tensor("xn0f", [128, NDB * T], BF16))
    xxxf = c.stack.enter_context(nc.sbuf_tensor("xxxf", [128, NDB * T], BF16))
    xff = xn0f
    t5 = [c.stack.enter_context(nc.sbuf_tensor(f"t5_{f}", [33, T], BF16)) for f in range(5)]
    import os
    if os.environ.get("NOMS5") != "1":
        with nc.Block() as block:
            sg = Seg(c, block)

            def ms5(v, fin):
                last = None
                for f in range(5):
                    last = v.memset(t5[f][32:33, :], 1.0)
                fin(last)
            sg.step("vector", [], ms5)

    with c.scope(), nc.Block() as block:
        sg = Seg(c, block)
        xt = [c.stack.enter_context(nc.sbuf_tensor(f"xt{i}", [128, D], F32)) for i in range(2)]
        xnt = [c.stack.enter_context(nc.sbuf_tensor(f"xnt{i}", [128, D], BF16)) for i in range(2)]
        sqt = c.stack.enter_context(nc.sbuf_tensor("sqt", [128, D], BF16))
        col = c.stack.enter_context(nc.sbuf_tensor("col", [128, 8], F32))
        psT = [c.ps(f"psTB{i}", [128, 128], BF16) if False else
               c.stack.enter_context(nc.psum_tensor(f"psTB{i}", [128, 128], BF16)) for i in range(2)]

        for g in range(NTC):
            b = g % 2

            def ldx(s, fin, g=g, b=b):
                fin(s.dma_start(xt[b][:, :], io["x0"][128 * g:128 * (g + 1), :]), 16)
            sg.step("sync", ["vector", "scalar"], ldx)

            def sq_(s, fin, b=b):
                fin(s.activation(sqt[:, :], xt[b][:, :], AF.Square))
            sg.step("scalar", ["sync"], sq_)

            def stats(v, fin, b=b):
                v.tensor_reduce(col[:, 0:1], xt[b][:, :], AX.X, ALU.add)
                v.tensor_reduce(col[:, 1:2], sqt[:, :], AX.X, ALU.add)
                v.drain()
                v.tensor_scalar(col[:, 0:1], col[:, 0:1], 1.0 / D, None, ALU.mult)
                v.tensor_scalar(col[:, 1:2], col[:, 1:2], 1.0 / D, None, ALU.mult)
                v.drain()
                v.tensor_tensor(col[:, 2:3], col[:, 0:1], col[:, 0:1], ALU.mult)
                v.drain()
                fin(v.tensor_tensor(col[:, 2:3], col[:, 1:2], col[:, 2:3], ALU.subtract))
            sg.step("vector", ["scalar"], stats)

            def rstd(s, fin):
                s.activation(col[:, 3:4], col[:, 2:3], AF.Ln, bias=LNEPS)
                s.drain()
                fin(s.activation(col[:, 3:4], col[:, 3:4], AF.Exp, scale=-0.5))
            sg.step("scalar", ["vector"], rstd)

            def apply(v, fin, b=b):
                v.scalar_tensor_tensor(col[:, 4:5], col[:, 0:1], -1.0, col[:, 3:4],
                                       ALU.mult, ALU.mult)
                v.drain()
                fin(v.tensor_scalar(xnt[b][:, :], xt[b][:, :], col[:, 3:4], col[:, 4:5],
                                    ALU.mult, ALU.add))
            sg.step("vector", ["scalar", "tensor"], apply)

            def st(s, fin, g=g, b=b):
                fin(s.dma_start(scr["s_xn0tm"][g], xnt[b][:, :]), 16)
            sg.step("sync", ["vector"], st)

            def produce(t, i, sl, g=g, b=b):
                return t.transpose(psT[sl][:, :], xnt[b][:, 128 * i:128 * (i + 1)], ident[:, :])

            def consume(v, i, sl, g=g):
                return v.tensor_copy(xn0f[:, T * i + 128 * g:T * i + 128 * (g + 1)], psT[sl][:, :])

            sg.barrier_wait("tensor", ["vector", "sync"])
            pipe(sg, NDB, produce, consume, 2)

    if STOP_AFTER == "B":
        c.pop()
        return
    with nc.Block() as block:
        sg = Seg(c, block)

        def conv(v, fin):
            J = [(xn0f[:, T * j:T * (j + 1)], dxf[:, T * j:T * (j + 1)],
                  convw[:, 3 * j:3 * j + 1], convw[:, 3 * j + 1:3 * j + 2],
                  convw[:, 3 * j + 2:3 * j + 3], j) for j in range(NDB)]
            for x, dx, w0, w1, w2, j in J:
                v.tensor_scalar(dx, x, w1, None, ALU.mult)
            for x, dx, w0, w1, w2, j in J:
                v.tensor_tensor(dx, dx, x, ALU.subtract)
            for x, dx, w0, w1, w2, j in J:
                v.scalar_tensor_tensor(dx[:, 1:T], x[:, 0:T - 1], w0, dx[:, 1:T],
                                       ALU.mult, ALU.add)
            for x, dx, w0, w1, w2, j in J:
                v.scalar_tensor_tensor(dx[:, 0:T - 1], x[:, 1:T], w2, dx[:, 0:T - 1],
                                       ALU.mult, ALU.add)
            last = None
            for x, dx, w0, w1, w2, j in J:
                last = v.scalar_tensor_tensor(xxxf[:, T * j:T * (j + 1)], dx,
                                              cmaax[:, j:j + 1], x, ALU.mult, ALU.add)
            fin(last)
        sg.step("vector", [], conv)

    if STOP_AFTER == "C":
        c.pop()
        return
    with c.scope(), nc.Block() as block:
        sg = Seg(c, block)
        ps5 = [c.stack.enter_context(nc.psum_tensor(f"ps5_{i}", [128, 512], F32)) for i in range(2)]
        items = [(f, sl) for f in range(5) for sl in range(T // 512)]

        def produce(t, i, s):
            f, sl = items[i]
            last = None
            for jj in range(NDB):
                last = t.matmul(ps5[s][0:32, :],
                                mw1[:, 160 * jj + 32 * f:160 * jj + 32 * (f + 1)],
                                xxxf[:, T * jj + 512 * sl:T * jj + 512 * (sl + 1)],
                                start=(jj == 0), stop=(jj == NDB - 1))
            return last

        def consume(s, i, sl_):
            f, sl = items[i]
            return s.activation(t5[f][0:32, 512 * sl:512 * (sl + 1)], ps5[sl_][0:32, :], AF.Tanh)

        sg.barrier_wait("tensor", ["vector"])
        pipe(sg, len(items), produce, consume, 2, ceng="scalar")

    if STOP_AFTER == "D":
        c.pop()
        return
    fnames = [("r_maa_w", "w"), ("r_maa_k", "k"), ("r_maa_v", "v"),
              ("r_maa_r", "r"), ("r_maa_g", "g")]
    drowb = c.stack.enter_context(nc.sbuf_tensor("drowb", [1, D], BF16))
    for fi, (rname, kind) in enumerate(fnames):
        with c.scope(), nc.Block() as block:
            sg = Seg(c, block)

            def mkaug(v, fin, rname=rname, fi=fi):
                fin(v.tensor_tensor(drowb[:, :], rows[rname][:, :],
                                    rows["r_maa_x"][:, :], ALU.subtract))
            sg.step("vector", [], mkaug)

            def ldw2(g, fin, fi=fi):
                fin(g.dma_start(w2aug[0:32, :], io["w_maa_w2"][fi]), 16)
            sg.step("gpsimd", ["tensor"], ldw2)

            def cpaug(v, fin):
                v.drain()
                fin(v.tensor_copy(w2aug[32:33, :], drowb[:, :]))
            sg.step("vector", ["gpsimd"], cpaug)

            psm = [c.stack.enter_context(nc.psum_tensor(c.uname(f"psm{fi}_{i}"), [128, 512], F32))
                   for i in range(2)]
            items = [(j, sl) for j in range(NDB) for sl in range(T // 512)]

            def produce(t, i, s, fi=fi):
                j, sl = items[i]
                return t.matmul(psm[s][:, :], w2aug[:, 128 * j:128 * (j + 1)],
                                t5[fi][:, 512 * sl:512 * (sl + 1)], start=True, stop=True)

            def consume(v, i, s):
                j, sl = items[i]
                cols = slice(T * j + 512 * sl, T * j + 512 * (sl + 1))
                v.tensor_tensor(xff[:, cols], psm[s][:, :], dxf[:, cols], ALU.mult)
                return v.tensor_tensor(xff[:, cols], xff[:, cols], xxxf[:, cols], ALU.add)

            sg.barrier_wait("tensor", ["vector", "gpsimd"])
            pipe(sg, len(items), produce, consume, 2)

        emit_l0_proj(c, io, scr, kind, xff, wbuf, tanhw, dw1, ident, stage)

    c.pop()
    if STOP_AFTER == "E":
        return

    emit_bcast_row(c, ones1b, rows["r_gate_b"], gbbc)
    emit_l0_proj(c, io, scr, "sig", dxf, wbuf, tanhw, dw1, ident, stage, gbbc=gbbc)
    emit_bcast_row(c, ones1, rows["v_td_row"], tdbc)
    emit_bcast_row(c, ones1b, rows["r_ln_w"], lnwbc)
    emit_bcast_row(c, ones1b, rows["r_ln_b"], lnbbc)

    if STOP_AFTER == "F":
        return
    emit_rwkv_sweeps(c, io, scr, tanhw, dw2, tdbc, Lf, Lb, maskf, maskb, ident, y_tm)

    if STOP_AFTER == "G":
        return
    emit_l0_post(c, io, scr, y_tm, lnwbc, lnbbc, ident, wbuf, stage)


def pipe3(sg, n, produce, consume, store, nps, nst, peng="tensor", ceng="vector"):
    c = sg.c
    p0 = c.val("pp")
    q0 = c.val("cc")
    d0 = c.val("dd")

    def _p(e):
        for i in range(n):
            w = q0 + i - nps + 1
            if w > 0:
                c.wait(e, "cc", w)
            c.inc("pp", produce(e, i, i % nps))
        c.wait(e, "pp")
        c.inc(ENG_SEM[peng], _nop(e))

    def _c(e):
        for i in range(n):
            c.wait(e, "pp", p0 + i + 1)
            w = d0 + 16 * (i - nst + 1)
            if w > 0:
                c.wait(e, "dd", w)
            c.inc("cc", consume(e, i, i % nps, i % nst))
        c.wait(e, "cc")
        c.inc(ENG_SEM[ceng], _nop(e))

    def _s(e):
        for i in range(n):
            c.wait(e, "cc", q0 + i + 1)
            c.inc("dd", store(e, i, i % nst), 16)
        c.wait(e, "dd")
        c.inc(ENG_SEM["sync"], _nop(e))

    getattr(sg.block, peng)(_p)
    getattr(sg.block, ceng)(_c)
    sg.block.sync(_s)


def emit_bcast_row(c, ones1, src_row, dst, f32rhs=True):
    nc = c.nc
    W = src_row.shape[1] if hasattr(src_row, "shape") else D
    with c.scope(), nc.Block() as block:
        sg = Seg(c, block)
        psb = c.stack.enter_context(nc.psum_tensor(c.uname("psbc"), [128, 512], F32))

        for sl in range(W // 512):
            def mm(t, fin, sl=sl):
                fin(t.matmul(psb[:, :], ones1[:, :], src_row[0:1, 512 * sl:512 * (sl + 1)],
                             start=True, stop=True))
            sg.step("tensor", ["vector"], mm)

            def ev(v, fin, sl=sl):
                fin(v.tensor_copy(dst[:, 512 * sl:512 * (sl + 1)], psb[:, :]))
            sg.step("vector", ["tensor"], ev)


def emit_l0_proj(c, io, scr, kind, src_fm, wbuf, tanhw, dw1, ident, stage, gbbc=None):
    nc = c.nc
    wsrc = {"k": "w_Wk", "r": "w_Wr", "v": "w_Wv", "g": "w_Wg", "sig": "w_gate"}
    with c.scope(), nc.Block() as block:
        sg = Seg(c, block)
        if kind != "w":
            def ldw(g, fin):
                fin(g.dma_start(*ld3(wbuf[:, :], io[wsrc[kind]])), 16)
            sg.step("gpsimd", ["tensor"], ldw)
            sg.barrier_wait("tensor", ["gpsimd"])

        if kind == "w":
            ps = [c.stack.enter_context(nc.psum_tensor(c.uname(f"psw{i}"), [128, 512], F32))
                  for i in range(2)]

            def produce(t, i, s):
                last = None
                for jj in range(NDB):
                    last = t.matmul(ps[s][0:64, :], dw1[:, 64 * jj:64 * (jj + 1)],
                                    src_fm[:, T * jj + 512 * i:T * jj + 512 * (i + 1)],
                                    start=(jj == 0), stop=(jj == NDB - 1))
                return last

            def consume(s, i, sl):
                return s.activation(tanhw[:, 512 * i:512 * (i + 1)], ps[sl][0:64, :], AF.Tanh)

            pipe(sg, T // 512, produce, consume, 2, ceng="scalar")
            return

        ps = [c.stack.enter_context(nc.psum_tensor(c.uname(f"psp_{kind}{i}"), [128, 512], F32))
              for i in range(2)]
        tmpf = c.stack.enter_context(nc.sbuf_tensor(c.uname(f"tmpf_{kind}"), [128, 1024], F32)) \
            if kind == "sig" else None

        if kind in ("k", "r"):
            items = [(j, sl) for j in range(NDB) for sl in range(T // 512)]

            def produce(t, i, s):
                j, sl = items[i]
                last = None
                for jj in range(NDB):
                    last = t.matmul(ps[s][:, :],
                                    wbuf[:, D * jj + 128 * j:D * jj + 128 * (j + 1)],
                                    src_fm[:, T * jj + 512 * sl:T * jj + 512 * (sl + 1)],
                                    start=(jj == 0), stop=(jj == NDB - 1))
                return last

            def consume(v, i, s, st):
                return v.tensor_copy(stage[:, 1024 * st:1024 * st + 512], ps[s][:, :])

            def store(y, i, st):
                j, sl = items[i]
                dst = scr["s_" + kind][4 * sl:4 * (sl + 1), :, 128 * j:128 * (j + 1)] \
                    .rearrange("gg p tt -> p gg tt")
                return y.dma_start(dst, stage[:, 1024 * st:1024 * st + 512]
                                   .rearrange("p (gg tt) -> p gg tt", gg=4))

            pipe3(sg, len(items), produce, consume, store, 2, 2)
        else:
            items = [(g, dsl) for g in range(NTC) for dsl in range(2)]
            dstn = {"v": "s_v", "g": "s_g", "sig": "s_sig"}[kind]

            def produce(t, i, s):
                g, dsl = items[i]
                last = None
                for jj in range(NDB):
                    last = t.matmul(ps[s][:, :],
                                    src_fm[:, T * jj + 128 * g:T * jj + 128 * (g + 1)],
                                    wbuf[:, D * jj + 512 * dsl:D * jj + 512 * (dsl + 1)],
                                    start=(jj == 0), stop=(jj == NDB - 1))
                return last

            if kind == "g":
                def consume(s, i, sl, st):
                    return s.activation(stage[:, 1024 * st:1024 * st + 512], ps[sl][:, :],
                                        AF.Silu)
                ceng = "scalar"
            elif kind == "sig":
                def consume(v, i, sl, st):
                    g, dsl = items[i]
                    v.tensor_tensor(tmpf[:, :], ps[sl][:, :],
                                    gbbc[:, 512 * dsl:512 * (dsl + 1)], ALU.add)
                    return v.tensor_copy(stage[:, 1024 * st:1024 * st + 512], tmpf[:, :])
                ceng = "vector"
            else:
                def consume(v, i, sl, st):
                    return v.tensor_copy(stage[:, 1024 * st:1024 * st + 512], ps[sl][:, :])
                ceng = "vector"

            def store(y, i, st):
                g, dsl = items[i]
                return y.dma_start(scr[dstn][g][:, 512 * dsl:512 * (dsl + 1)],
                                   stage[:, 1024 * st:1024 * st + 512])

            if kind == "sig":
                p0 = c.val("pp"); q0 = c.val("cc"); r0 = c.val("qq"); d0 = c.val("dd")

                def _p(t):
                    for i in range(len(items)):
                        if i >= 2:
                            c.wait(t, "cc", q0 + i - 1)
                        c.inc("pp", produce(t, i, i % 2))
                    c.wait(t, "pp")
                    c.inc("eT", _nop(t))

                def _v(v):
                    for i in range(len(items)):
                        c.wait(v, "pp", p0 + i + 1)
                        if i >= 2:
                            c.wait(v, "qq", r0 + i - 1)
                        g, dsl = items[i]
                        c.inc("cc", v.tensor_tensor(
                            tmpf[:, 512 * (i % 2):512 * (i % 2) + 512],
                            ps[i % 2][:, :],
                            gbbc[:, 512 * dsl:512 * (dsl + 1)], ALU.add))
                    c.wait(v, "cc")
                    c.inc("eV", _nop(v))

                def _s(s):
                    for i in range(len(items)):
                        c.wait(s, "cc", q0 + i + 1)
                        if i >= 2:
                            c.wait(s, "dd", d0 + 16 * (i - 1))
                        c.inc("qq", s.activation(
                            stage[:, 1024 * (i % 2):1024 * (i % 2) + 512],
                            tmpf[:, 512 * (i % 2):512 * (i % 2) + 512], AF.Sigmoid))
                    c.wait(s, "qq")
                    c.inc("eS", _nop(s))

                def _y(y):
                    for i in range(len(items)):
                        c.wait(y, "qq", r0 + i + 1)
                        c.inc("dd", store(y, i, i % 2), 16)
                    c.wait(y, "dd")
                    c.inc("eY", _nop(y), 16)

                sg.block.tensor(_p)
                sg.block.vector(_v)
                sg.block.scalar(_s)
                sg.block.sync(_y)
            else:
                pipe3(sg, len(items), produce, consume, store, 2, 2, ceng=ceng)


def emit_rwkv_sweeps(c, io, scr, tanhw, dw2, tdbc, Lf, Lb, maskf, maskb, ident, y_tm):
    nc = c.nc
    c.push()
    sb = lambda n, shp, dt: c.stack.enter_context(nc.sbuf_tensor(n, shp, dt))
    Sp = sb("Sp", [128, D], BF16)
    totcol = sb("totcol", [128, NCH * NDB], F32)
    rch = sb("rch", [128, D], BF16)
    kch = sb("kch", [128, D], BF16)
    vch = sb("vch", [128, D], BF16)
    wch = sb("wch", [128, D], F32)
    ewt = sb("ewt", [128, D], F32)
    epm = sb("epm", [128, D], BF16)
    emm = sb("emm", [128, D], BF16)
    Epf = sb("Epf", [128, D], BF16)
    Emf = sb("Emf", [128, D], BF16)
    rt = sb("rt", [128, D], BF16)
    kt = sb("kt", [128, D], BF16)
    ktm = sb("ktm", [128, D], BF16)
    Mf = sb("Mf", [128, H * 128], BF16)
    zerot = sb("zerot", [128, 128], F32)
    stmp = sb("stmp", [128, D], BF16)

    for dirn in ("f", "b"):
        with c.scope(), nc.Block() as block:
            sg = Seg(c, block)
            ps_a = c.stack.enter_context(nc.psum_tensor(f"psa{dirn}", [128, D], F32))
            ps_y = [c.stack.enter_context(nc.psum_tensor(f"psy{dirn}{i}", [128, 512], F32))
                    for i in range(2)]
            ps_A = [c.stack.enter_context(nc.psum_tensor(f"psA{dirn}{i}", [128, 128], F32))
                    for i in range(2)]
            ps_t = [c.stack.enter_context(nc.psum_tensor(f"pst{dirn}{i}", [128, 128], BF16))
                    for i in range(2)]
            L = Lf if dirn == "f" else Lb
            msk = maskf if dirn == "f" else maskb
            chunks = list(range(NCH)) if dirn == "f" else list(range(NCH - 1, -1, -1))

            def init(v, fin):
                fin(v.memset(Sp[:, :], 0.0))
                fin(v.memset(zerot[:, :], 0.0))
            sg.step("vector", [], init)

            for g in chunks:
                def ld(s, fin, g=g):
                    fin(s.dma_start(rch[:, :], scr["s_r"][g]), 16)
                    fin(s.dma_start(kch[:, :], scr["s_k"][g]), 16)
                    fin(s.dma_start(vch[:, :], scr["s_v"][g]), 16)
                sg.step("sync", ["tensor", "vector"], ld)

                def wmm(t, fin, g=g):
                    t.matmul(ps_a[:, 0:512], tanhw[:, 128 * g:128 * (g + 1)],
                             dw2[:, 0:512], start=True, stop=True)
                    fin(t.matmul(ps_a[:, 512:1024], tanhw[:, 128 * g:128 * (g + 1)],
                                 dw2[:, 512:1024], start=True, stop=True))
                sg.step("tensor", ["vector", "scalar"], wmm)

                def wadd(v, fin):
                    fin(v.tensor_tensor(wch[:, :], ps_a[:, :], tdbc[:, :], ALU.add))
                sg.step("vector", ["tensor"], wadd)

                def expw(s, fin):
                    fin(s.activation(ewt[:, :], wch[:, :], AF.Exp))
                sg.step("scalar", ["vector"], expw)

                def cums(t, fin):
                    t.matmul(ps_a[:, 0:512], L[:, :], ewt[:, 0:512], start=True, stop=True)
                    fin(t.matmul(ps_a[:, 512:1024], L[:, :], ewt[:, 512:1024],
                                 start=True, stop=True))
                sg.step("tensor", ["scalar", "vector"], cums)

                def exps(s, fin):
                    s.activation(epm[:, :], ps_a[:, :], AF.Exp)
                    fin(s.activation(emm[:, :], ps_a[:, :], AF.Exp, scale=-1.0))
                sg.step("scalar", ["tensor"], exps)

                sg.barrier_wait("tensor", ["scalar", "vector"])

                def tprod(t, i, sl):
                    srct = epm if i < NDB else emm
                    j = i % NDB
                    return t.transpose(ps_t[sl][:, :], srct[:, 128 * j:128 * (j + 1)],
                                       ident[:, :])

                def tcons(v, i, sl):
                    dstt = Epf if i < NDB else Emf
                    j = i % NDB
                    return v.tensor_copy(dstt[:, 128 * j:128 * (j + 1)], ps_t[sl][:, :])

                pipe(sg, 2 * NDB, tprod, tcons, 2)

                def scale(v, fin, g=g, dirn=dirn):
                    if dirn == "f":
                        v.tensor_tensor(rt[:, :], rch[:, :], Epf[:, :], ALU.mult)
                        v.tensor_tensor(kt[:, :], kch[:, :], Emf[:, :], ALU.mult)
                        fin(v.tensor_copy(
                            totcol[:, NDB * g:NDB * (g + 1)].rearrange(
                                "p (j o) -> p j o", o=1),
                            Epf[:, :].rearrange("p (j t) -> p j t", t=128)[:, :, 127:128]))
                    else:
                        v.tensor_tensor(rt[:, :], rch[:, :], Emf[:, :], ALU.mult)
                        fin(v.tensor_tensor(kt[:, :], kch[:, :], Epf[:, :], ALU.mult))
                sg.step("vector", ["sync"], scale)

                sg.barrier_wait("tensor", ["vector"])

                def kprod(t, i, sl):
                    return t.transpose(ps_t[sl][:, :], kt[:, 128 * i:128 * (i + 1)],
                                       ident[:, :])

                def kcons(v, i, sl):
                    return v.tensor_copy(ktm[:, 128 * i:128 * (i + 1)], ps_t[sl][:, :])

                pipe(sg, NDB, kprod, kcons, 2)

                sg.barrier_wait("tensor", ["vector"])

                def aprod(t, i, sl):
                    hh = i
                    j, half = hh // 2, 64 * (hh % 2)
                    return t.matmul(ps_A[sl][:, :],
                                    kt[half:half + 64, 128 * j:128 * (j + 1)],
                                    rt[half:half + 64, 128 * j:128 * (j + 1)],
                                    start=True, stop=True)

                def acons(v, i, sl):
                    return v.select(Mf[:, 128 * i:128 * (i + 1)], msk[:, :],
                                    ps_A[sl][:, :], zerot[:, :])

                pipe(sg, H, aprod, acons, 2)

                def ymm(t, fin, g=g):
                    for half in range(2):
                        for hh in range(8 * half, 8 * half + 8):
                            t.matmul(ps_y[half][:, 64 * hh - 512 * half:
                                               64 * (hh + 1) - 512 * half],
                                     Mf[:, 128 * hh:128 * (hh + 1)],
                                     vch[:, 64 * hh:64 * (hh + 1)],
                                     start=(hh % 8 == 0), stop=False)
                        for pj in range(4 * half, 4 * half + 4):
                            last = t.matmul(ps_y[half][:, 128 * pj - 512 * half:
                                                       128 * (pj + 1) - 512 * half],
                                            rt[:, 128 * pj:128 * (pj + 1)],
                                            Sp[:, 128 * pj:128 * (pj + 1)],
                                            start=False, stop=(pj % 4 == 3))
                    fin(last)
                sg.step("tensor", ["vector"], ymm)

                def simm(t, fin, g=g):
                    for hh in range(H):
                        j, half = hh // 2, 64 * (hh % 2)
                        ins = t.matmul(ps_a[half:half + 64,
                                            128 * j + half:128 * j + half + 64],
                                       ktm[:, 128 * j + half:128 * j + half + 64],
                                       vch[:, 64 * hh:64 * (hh + 1)],
                                       start=True, stop=True)
                    fin(ins)
                sg.step("tensor", [], simm)

                def yev(v, fin, g=g, dirn=dirn):
                    last = None
                    for half in range(2):
                        dst = y_tm[:, D * g + 512 * half:D * g + 512 * (half + 1)]
                        if dirn == "f":
                            last = v.tensor_copy(dst, ps_y[half][:, :])
                        else:
                            last = v.tensor_tensor(dst, ps_y[half][:, :], dst, ALU.add)
                    fin(last)
                sg.step("vector", ["tensor"], yev)

                def supd(v, fin, g=g):
                    for hh in range(H):
                        j, half = hh // 2, 64 * (hh % 2)
                        blk = (slice(half, half + 64),
                               slice(128 * j + half, 128 * j + half + 64))
                        v.tensor_tensor(stmp[blk[0], blk[1]], Sp[blk[0], blk[1]],
                                        ps_a[blk[0], blk[1]], ALU.add)
                    v.drain()
                    last = None
                    for hh in range(H):
                        j, half = hh // 2, 64 * (hh % 2)
                        blk = (slice(half, half + 64),
                               slice(128 * j + half, 128 * j + half + 64))
                        ec = totcol[half:half + 64, NDB * g + j:NDB * g + j + 1]
                        last = v.tensor_scalar(Sp[blk[0], blk[1]], stmp[blk[0], blk[1]],
                                               ec, None, ALU.mult)
                    fin(last)
                sg.step("vector", ["tensor"], supd)
    c.pop()


def emit_l0_post(c, io, scr, y_tm, lnwbc, lnbbc, ident, wbuf, stage):
    nc = c.nc
    c.push()
    sb = lambda n, shp, dt: c.stack.enter_context(nc.sbuf_tensor(n, shp, dt))
    sigch = sb("sigch", [128, D], BF16)
    gch = sb("gch", [128, D], BF16)
    ygt = sb("ygt", [128, D], BF16)
    sqt2 = sb("sqt2", [128, D], BF16)
    st16 = sb("st16", [128, 4 * 16], F32)
    st16b = sb("st16b", [128, 32], BF16)
    zzt = sb("zzt", [128, D], BF16)
    zzf = sb("zzf", [128, D], BF16)

    with c.scope(), nc.Block() as block:
        sg = Seg(c, block)

        def ldw(g, fin):
            fin(g.dma_start(*ld3(wbuf[:, :], io["w_Wo"])), 16)
        sg.step("gpsimd", [], ldw)

        ps_t = [c.stack.enter_context(nc.psum_tensor(f"pstP{i}", [128, 128], BF16))
                for i in range(2)]
        ps_o = [c.stack.enter_context(nc.psum_tensor(f"psoP{i}", [128, 512], F32))
                for i in range(2)]

        for g in range(NTC):
            def ld(s, fin, g=g):
                fin(s.dma_start(sigch[:, :], scr["s_sig"][g]), 16)
                fin(s.dma_start(gch[:, :], scr["s_g"][g]), 16)
            sg.step("sync", ["vector", "tensor"], ld)

            def gnorm(v, fin, g=g):
                fin(v.tensor_tensor(ygt[:, :], y_tm[:, D * g:D * (g + 1)], sigch[:, :],
                                    ALU.mult))
            sg.step("vector", ["sync"], gnorm)

            def sq(s, fin):
                fin(s.activation(sqt2[:, :], ygt[:, :], AF.Square))
            sg.step("scalar", ["vector"], sq)

            def stats(v, fin):
                yv = ygt[:, :].rearrange("p (h k) -> p h k", h=H)
                sv = sqt2[:, :].rearrange("p (h k) -> p h k", h=H)
                v.tensor_reduce(st16[:, 0:16], yv, AX.X, ALU.add)
                v.tensor_reduce(st16[:, 16:32], sv, AX.X, ALU.add)
                v.drain()
                v.tensor_scalar(st16[:, 0:16], st16[:, 0:16], 1.0 / KH, None, ALU.mult)
                v.tensor_scalar(st16[:, 16:32], st16[:, 16:32], 1.0 / KH, None, ALU.mult)
                v.drain()
                v.tensor_tensor(st16[:, 32:48], st16[:, 0:16], st16[:, 0:16], ALU.mult)
                v.drain()
                fin(v.tensor_tensor(st16[:, 32:48], st16[:, 16:32], st16[:, 32:48],
                                    ALU.subtract))
            sg.step("vector", ["scalar"], stats)

            def rstd(s, fin):
                s.activation(st16[:, 48:64], st16[:, 32:48], AF.Ln, bias=GNEPS)
                s.drain()
                fin(s.activation(st16[:, 48:64], st16[:, 48:64], AF.Exp, scale=-0.5))
            sg.step("scalar", ["vector"], rstd)

            def apply(v, fin, g=g):
                v.tensor_copy(st16b[:, 0:16], st16[:, 0:16])
                v.tensor_copy(st16b[:, 16:32], st16[:, 48:64])
                yv = ygt[:, :].rearrange("p (h k) -> p h k", h=H)
                zv = zzt[:, :].rearrange("p (h k) -> p h k", h=H)
                mb = st16b[:, 0:16].unsqueeze(2).broadcast_to([128, H, KH])
                rb = st16b[:, 16:32].unsqueeze(2).broadcast_to([128, H, KH])
                v.drain()
                v.tensor_tensor(zv, yv, mb, ALU.subtract)
                v.tensor_tensor(zv, zv, rb, ALU.mult)
                v.tensor_tensor(zzt[:, :], zzt[:, :], lnwbc[:, :], ALU.mult)
                v.tensor_tensor(zzt[:, :], zzt[:, :], lnbbc[:, :], ALU.add)
                fin(v.tensor_tensor(zzt[:, :], zzt[:, :], gch[:, :], ALU.mult))
            sg.step("vector", ["scalar"], apply)

            sg.barrier_wait("tensor", ["vector"])

            def zprod(t, i, sl):
                return t.transpose(ps_t[sl][:, :], zzt[:, 128 * i:128 * (i + 1)],
                                   ident[:, :])

            def zcons(v, i, sl):
                return v.tensor_copy(zzf[:, 128 * i:128 * (i + 1)], ps_t[sl][:, :])

            pipe(sg, NDB, zprod, zcons, 2)

            def produce(t, i, sl, g=g):
                last = None
                for jj in range(NDB):
                    last = t.matmul(ps_o[sl][:, :], zzf[:, 128 * jj:128 * (jj + 1)],
                                    wbuf[:, D * jj + 512 * i:D * jj + 512 * (i + 1)],
                                    start=(jj == 0), stop=(jj == NDB - 1))
                return last

            def consume(v, i, sl, st):
                return v.tensor_copy(stage[:, 1024 * st:1024 * st + 512], ps_o[sl][:, :])

            def store(y, i, st, g=g):
                return y.dma_start(scr["s_attn"][g][:, 512 * i:512 * (i + 1)],
                                   stage[:, 1024 * st:1024 * st + 512])

            sg.barrier_wait("tensor", ["vector", "gpsimd"])
            pipe3(sg, 2, produce, consume, store, 2, 2)
    c.pop()


SEG_INPUTS = {
    "l0a": ["x0", "w_Wr", "w_Wk", "w_Wv", "w_Wg", "w_Wo", "w_gate", "w_maa_w1",
            "w_maa_w2", "w_decay_w1", "w_decay_w2", "v_td_row", "v_convw",
            "r_maa_x", "r_maa_w", "r_maa_k", "r_maa_v", "r_maa_r", "r_maa_g",
            "c_maa_x", "r_ln_w", "r_ln_b", "r_gate_b",
            "c_ident", "c_Lf", "c_Lb", "c_mask_f", "c_mask_b"],
    "moe": ["x0", "r_rms_w", "w_moe_gateT", "w_moe_W1", "w_moe_W2", "c_moe_b1",
            "w_moe_b2", "r_l1n_w", "r_l1n_b", "c_ident"],
    "mamba": ["w_WinT", "c_mconv_w", "c_mconv_b", "w_WxT", "w_WdtT", "c_bdt",
              "c_Alog", "c_mD", "w_WoutT", "c_ident"],
    "l1f": ["w_fb_T", "w_fc", "w_fcT", "w_proj", "w_projT", "w_lrv", "w_lruT",
            "v_lrgate", "c_ident"],
}


def build_fused():
    nc = bass.Bass("TRN2", target_bir_lowering=False)
    io = {}
    seen = set()
    for seg in SEGS:
        for nm in SEG_INPUTS[seg]:
            if nm not in seen:
                shp, dt = IN_SPECS[nm]
                io[nm] = nc.dram_tensor(nm, shp, dt, kind="ExternalInput").ap()
                seen.add(nm)
    scr = {}
    for nm, (shp, dt) in SCRATCH.items():
        scr[nm] = nc.dram_tensor(nm, shp, dt).ap()
    io["out"] = nc.dram_tensor("out", [T, D], F32, kind="ExternalOutput").ap()
    c = Ctx(nc)
    with c.stack:
        c.init_sems()
        c.setup_consts()
        c.push()
        seg_l0a(c, io, scr)
        c.pop()
        c.push()
        seg_moe(c, io, scr)
        c.pop()
        c.push()
        l1ctx = seg_l1f_pre(c, io, scr)
        c.pop()
        c.push()
        seg_mamba(c, io, scr, l1ctx)
        c.pop()
        c.push()
        seg_l1f_final(c, io, scr)
        c.pop()
        emit_tail(c)
    return nc


def build_segment(seg):
    nc = bass.Bass("TRN2", target_bir_lowering=False)
    io = {}
    for nm in SEG_INPUTS[seg]:
        shp, dt = IN_SPECS[nm]
        io[nm] = nc.dram_tensor(nm, shp, dt, kind="ExternalInput").ap()
    rd, wr = SEG_IO[seg]
    scr = {}
    for nm in rd:
        shp, dt = SCRATCH[nm]
        scr[nm] = nc.dram_tensor(nm, shp, dt, kind="ExternalInput").ap()
    for nm in wr:
        shp, dt = SCRATCH[nm]
        scr[nm] = nc.dram_tensor(nm, shp, dt, kind="ExternalOutput").ap()
    if seg == "l1f":
        io["out"] = nc.dram_tensor("out", [T, D], F32, kind="ExternalOutput").ap()
    c = Ctx(nc)
    with c.stack:
        c.init_sems()
        c.setup_consts()
        if seg == "l0a":
            seg_l0a(c, io, scr)
        elif seg == "moe":
            seg_moe(c, io, scr)
        else:
            raise NotImplementedError("unfused build only supports l0a/moe now")
        emit_tail(c)
    return nc


def emit_tail(c):
    nc = c.nc
    with nc.Block() as block:
        sg = Seg(c, block)
        sg.barrier_wait("sync")
        sg.barrier_wait("gpsimd", ["sync"])

        def fini(s):
            for nm in ("pp", "cc", "qq", "dd"):
                v = c.val(nm)
                if v > 0:
                    s.wait_ge(c.sem(nm)[0], v)
        block.sync(fini)


_PROGRAMS = {}


def kernel(**inputs):
    maps = shard_inputs(inputs)
    if not UNFUSED:
        if "all" not in _PROGRAMS:
            _PROGRAMS["all"] = build_fused()
        nc = _PROGRAMS["all"]
        allin = set()
        for seg in SEGS:
            allin.update(SEG_INPUTS[seg])
        in_maps = [{nm: maps[b][nm] for nm in allin} for b in range(NCORE)]
        res = run_bass_kernel_spmd(nc, in_maps, core_ids=list(range(NCORE)))
        return np.stack([np.asarray(res.results[b]["out"], dtype=np.float32)
                         for b in range(NCORE)], axis=0)
    segs = SEGS
    scr_state = [dict() for _ in range(NCORE)]
    out = None
    for seg in segs:
        if seg not in _PROGRAMS:
            _PROGRAMS[seg] = build_segment(seg)
        nc = _PROGRAMS[seg]
        rd, wr = SEG_IO[seg]
        in_maps = []
        for b in range(NCORE):
            m = {nm: maps[b][nm] for nm in SEG_INPUTS[seg]}
            for nm in rd:
                m[nm] = scr_state[b][nm]
            in_maps.append(m)
        res = run_bass_kernel_spmd(nc, in_maps, core_ids=list(range(NCORE)))
        for b in range(NCORE):
            for nm in wr:
                scr_state[b][nm] = np.asarray(res.results[b][nm])
        if seg == "l1f":
            out = np.stack([np.asarray(res.results[b]["out"], dtype=np.float32)
                            for b in range(NCORE)], axis=0)
    return out


def seg_moe(c, io, scr):
    nc = c.nc
    sb = lambda n, shp, dt: c.stack.enter_context(nc.sbuf_tensor(n, shp, dt))
    ident = sb("identM", [128, 128], BF16)
    ones1b = sb("ones1bM", [1, 128], BF16)
    rmsbc = sb("rmsbc", [128, D], BF16)
    l1wbc = sb("l1wbc", [128, D], BF16)
    l1bbc = sb("l1bbc", [128, D], BF16)
    rows = {}
    for nm in ("r_rms_w", "r_l1n_w", "r_l1n_b"):
        rows[nm] = sb(nm + "_sbM", [1, D], BF16)
    gate8 = sb("gate8", [128, NDB * E], BF16)
    gate8f = sb("gate8f", [128, NDB * E], F32)
    gate8lo = sb("gate8lo", [128, NDB * E], BF16)
    xnrf = sb("xnrf", [128, NDB * T], BF16)
    xnrlog = sb("xnrlog", [128, D], BF16)
    mskrows = sb("mskrows", [E, T], BF16)
    mskrow_t = sb("mskrow_t", [1, T], BF16)
    hsb = sb("hsb", [128, 8 * T], BF16)
    b1c = sb("b1c", [128, NFB], F32)
    b2b = sb("b2b", [E, D], BF16)
    selbc = sb("selbc", [128, T], BF16)

    with nc.Block() as block:
        sg = Seg(c, block)

        def ld(s, fin):
            fin(s.dma_start(ident[:, :], io["c_ident"]), 16)
        sg.step("sync", [], ld)

        def ldg(g, fin):
            for nm in rows:
                fin(g.dma_start(rows[nm][:, :], io[nm]), 16)
            fin(g.dma_start(*ld3(gate8[:, :], io["w_moe_gateT"])), 16)
            fin(g.dma_start(*ld3(gate8f[:, :], io["w_moe_gateT"])), 16)
            fin(g.dma_start(b2b[:, :], io["w_moe_b2"]), 16)
        sg.step("gpsimd", [], ldg)

        def ms(v, fin):
            fin(v.memset(ones1b[:, :], 1.0))
        sg.step("vector", [], ms)

        def glo(v, fin):
            fin(v.tensor_tensor(gate8lo[:, :], gate8f[:, :], gate8[:, :], ALU.subtract))
        sg.step("vector", ["gpsimd", "sync"], glo)

    emit_bcast_row(c, ones1b, rows["r_rms_w"], rmsbc)
    emit_bcast_row(c, ones1b, rows["r_l1n_w"], l1wbc)
    emit_bcast_row(c, ones1b, rows["r_l1n_b"], l1bbc)

    c.push()
    with nc.Block() as block:
        sg = Seg(c, block)
        xnt = [c.stack.enter_context(nc.sbuf_tensor(f"xntM{i}", [128, D], BF16)) for i in range(2)]
        xnr = c.stack.enter_context(nc.sbuf_tensor("xnrM", [128, D], BF16))
        xnrF = c.stack.enter_context(nc.sbuf_tensor("xnrF", [128, D], F32))
        xnrL = c.stack.enter_context(nc.sbuf_tensor("xnrL", [128, D], BF16))
        sqt = c.stack.enter_context(nc.sbuf_tensor("sqtM", [128, D], BF16))
        col = c.stack.enter_context(nc.sbuf_tensor("colM", [128, 8], F32))
        lgt = c.stack.enter_context(nc.sbuf_tensor("lgt", [128, 8], F32))
        mskb = c.stack.enter_context(nc.sbuf_tensor("mskb", [128, 8], BF16))
        msk8 = c.stack.enter_context(nc.sbuf_tensor("msk8", [8, 128], BF16))
        mskdram = nc.dram_tensor("mskdram", [E, T], BF16).ap()
        psT = [c.stack.enter_context(nc.psum_tensor(f"psTM{i}", [128, 128], BF16)) for i in range(2)]
        psL = c.stack.enter_context(nc.psum_tensor("psL", [128, 512], F32))

        for g in range(NTC):
            b = g % 2

            def ldx(s, fin, g=g, b=b):
                fin(s.dma_start(xnt[b][:, :], scr["s_xn0tm"][g]), 16)
            sg.step("sync", ["vector", "scalar", "tensor"], ldx)

            def sq_(s, fin, b=b):
                fin(s.activation(sqt[:, :], xnt[b][:, :], AF.Square))
            sg.step("scalar", ["sync"], sq_)

            def stats(v, fin, b=b):
                v.tensor_reduce(col[:, 1:2], sqt[:, :], AX.X, ALU.add)
                v.drain()
                fin(v.tensor_scalar(col[:, 1:2], col[:, 1:2], 1.0 / D, None, ALU.mult))
            sg.step("vector", ["scalar"], stats)

            def rstd(s, fin):
                s.activation(col[:, 3:4], col[:, 1:2], AF.Ln, bias=RMSEPS)
                s.drain()
                fin(s.activation(col[:, 3:4], col[:, 3:4], AF.Exp, scale=-0.5))
            sg.step("scalar", ["vector"], rstd)

            def apply(v, fin, b=b):
                v.tensor_scalar(xnrF[:, :], xnt[b][:, :], col[:, 3:4], None, ALU.mult)
                v.tensor_tensor(xnrF[:, :], xnrF[:, :], rmsbc[:, :], ALU.mult)
                v.tensor_copy(xnr[:, :], xnrF[:, :])
                fin(v.tensor_tensor(xnrL[:, :], xnrF[:, :], xnr[:, :], ALU.subtract))
            sg.step("vector", ["scalar", "tensor"], apply)

            sg.barrier_wait("tensor", ["vector"])

            def tprod(t, i, sl, g=g):
                srct = xnr if i < NDB else xnrL
                j = i % NDB
                return t.transpose(psT[sl][:, :], srct[:, 128 * j:128 * (j + 1)], ident[:, :])

            def tcons(v, i, sl, g=g):
                j = i % NDB
                if i < NDB:
                    return v.tensor_copy(xnrf[:, T * j + 128 * g:T * j + 128 * (g + 1)],
                                         psT[sl][:, :])
                return v.tensor_copy(xnrlog[:, 128 * j:128 * (j + 1)], psT[sl][:, :])

            pipe(sg, 2 * NDB, tprod, tcons, 2)

            def lmm(t, fin, g=g):
                last = None
                for pi in range(3):
                    ga = gate8 if pi != 2 else gate8lo
                    for jj in range(NDB):
                        xa = (xnrf[:, T * jj + 128 * g:T * jj + 128 * (g + 1)]
                              if pi != 1 else xnrlog[:, 128 * jj:128 * (jj + 1)])
                        last = t.matmul(psL[:, 0:E], xa, ga[:, E * jj:E * (jj + 1)],
                                        start=(pi == 0 and jj == 0),
                                        stop=(pi == 2 and jj == NDB - 1))
                fin(last)
            sg.step("tensor", ["vector"], lmm)

            def gating(v, fin, g=g):
                v.tensor_copy(lgt[:, :], psL[:, 0:E])
                v.drain()
                v.tensor_reduce(col[:, 0:1], lgt[:, :], AX.X, ALU.max)
                v.drain()
                v.tensor_scalar(mskb[:, :], lgt[:, :], col[:, 0:1], None, ALU.is_ge)
                v.drain()
                fin(v.tensor_scalar(lgt[:, :], lgt[:, :], col[:, 0:1], None, ALU.subtract))
            sg.step("vector", ["tensor"], gating)

            def gexp(s, fin):
                fin(s.activation(lgt[:, :], lgt[:, :], AF.Exp))
            sg.step("scalar", ["vector"], gexp)

            def gfin(v, fin, g=g):
                v.tensor_reduce(col[:, 2:3], lgt[:, :], AX.X, ALU.add)
                v.drain()
                v.reciprocal(col[:, 4:5], col[:, 2:3])
                v.drain()
                fin(v.tensor_scalar(mskb[:, :], mskb[:, :], col[:, 4:5], None, ALU.mult))
            sg.step("vector", ["scalar"], gfin)

            sg.barrier_wait("tensor", ["vector"])

            def mprod(t, i, sl, g=g):
                return t.transpose(psT[sl][0:E, :], mskb[:, :], ident[:, :])

            def mcons(v, i, sl, g=g):
                return v.tensor_copy(msk8[:, :], psT[sl][0:E, :])

            pipe(sg, 1, mprod, mcons, 2)

            def mst(s, fin, g=g):
                fin(s.dma_start(mskdram[:, 128 * g:128 * (g + 1)], msk8[:, :]), 16)
            sg.step("sync", ["vector"], mst)
            sg.barrier_wait("vector", ["sync"])
    with nc.Block() as block:
        sg = Seg(c, block)

        def ldm(s, fin):
            fin(s.dma_start(mskrows[:, :], mskdram), 16)
        sg.step("sync", [], ldm)
    c.pop()

    c.push()
    wdv = {}
    with nc.Block() as block:
        sg = Seg(c, block)
        yfp = c.stack.enter_context(nc.sbuf_tensor("yfpM", [128, 8 * T], F32))
        mtmp = c.stack.enter_context(nc.sbuf_tensor("mtmpM", [128, 512], F32))
        w1q = [c.stack.enter_context(nc.sbuf_tensor(f"w1qM{i}", [128, NDB * 1024], BF16))
               for i in range(2)]
        w2b = [c.stack.enter_context(nc.sbuf_tensor(f"w2bM{i}", [128, 8 * 128], BF16))
               for i in range(8)]
        psh = [c.stack.enter_context(nc.psum_tensor(c.uname(f"psh{i}"), [128, 512], F32))
               for i in range(2)]
        psy = [c.stack.enter_context(nc.psum_tensor(c.uname(f"psyM{i}"), [128, 512], F32))
               for i in range(4)]

        def emit_w1(e_, q_):
            def f(g, fin):
                fin(g.dma_start(*ld3(w1q[q_ % 2][:, :], io["w_moe_W1"][e_, q_])), 16)
                wdv[("w1", e_, q_)] = c.val("eG")
            sg.step("gpsimd", ["tensor"], f)

        def emit_w2(e_, q_):
            def f(g, fin):
                for dc in range(8):
                    fin(g.dma_start(*ld3(w2b[dc][:, :],
                        io["w_moe_W2"][e_, dc][1024 * q_:1024 * (q_ + 1), :])), 16)
                    wdv[("w2", e_, q_, dc)] = c.val("eG")
            sg.step("gpsimd", ["tensor"], f)

        emit_w1(0, 0)
        for e in range(E):
            def ldb1(s, fin, e=e):
                fin(s.dma_start(b1c[:, :], io["c_moe_b1"][e]), 16)
                fin(s.dma_start(mskrow_t[:, :], mskdram[e:e + 1, :]), 16)
            sg.step("sync", ["scalar", "tensor"], ldb1)
            yv = c.val("eY")

            def sprod(t, i, sl):
                if i == 0:
                    c.wait(t, "eY", yv)
                return t.matmul(psh[sl][:, :], ones1b[:, :],
                                mskrow_t[0:1, 512 * i:512 * (i + 1)],
                                start=True, stop=True)

            def scons(v, i, sl):
                return v.tensor_copy(selbc[:, 512 * i:512 * (i + 1)], psh[sl][:, :])

            pipe(sg, 4, sprod, scons, 2)

            for q in range(4):
                emit_w2(e, q)
                if q < 3:
                    emit_w1(e, q + 1)
                elif e + 1 < E:
                    emit_w1(e + 1, 0)

                itemsA = [(fc, nsl) for fc in range(8) for nsl in range(4)]

                def aprod(t, i, sl, e=e, q=q):
                    fc, nsl = itemsA[i]
                    if i == 0:
                        c.wait(t, "eG", wdv[("w1", e, q)])
                    last = None
                    for jj in range(NDB):
                        last = t.matmul(
                            psh[sl][:, :],
                            w1q[q % 2][:, 1024 * jj + 128 * fc:1024 * jj + 128 * (fc + 1)],
                            xnrf[:, T * jj + 512 * nsl:T * jj + 512 * (nsl + 1)],
                            start=(jj == 0), stop=(jj == NDB - 1))
                    return last

                def acons(s, i, sl, q=q):
                    fc, nsl = itemsA[i]
                    if i == 0:
                        c.wait(s, "eY", yv)
                    fcg = 8 * q + fc
                    return s.activation(
                        hsb[:, T * fc + 512 * nsl:T * fc + 512 * (nsl + 1)],
                        psh[sl][:, :], AF.Silu, bias=b1c[:, fcg:fcg + 1])

                pipe(sg, len(itemsA), aprod, acons, 2, ceng="scalar")
                ccq = c.val("cc")

                itemsB = [(dc, nsl) for dc in range(8) for nsl in range(4)]

                def bprod(t, i, sl, e=e, q=q, ccq=ccq):
                    dc, nsl = itemsB[i]
                    if i == 0:
                        c.wait(t, "cc", ccq)
                    if nsl == 0:
                        c.wait(t, "eG", wdv[("w2", e, q, dc)])
                    last = None
                    for fc in range(8):
                        last = t.matmul(
                            psy[sl][:, :],
                            w2b[dc][:, 128 * fc:128 * (fc + 1)],
                            hsb[:, T * fc + 512 * nsl:T * fc + 512 * (nsl + 1)],
                            start=(fc == 0), stop=(fc == 7))
                    return last

                def bcons(v, i, sl, e=e, q=q):
                    dc, nsl = itemsB[i]
                    sel = selbc[:, 512 * nsl:512 * (nsl + 1)]
                    dst = yfp[:, T * dc + 512 * nsl:T * dc + 512 * (nsl + 1)]
                    if e == 0 and q == 0:
                        return v.tensor_tensor(dst, psy[sl][:, :], sel, ALU.mult)
                    v.tensor_tensor(mtmp[:, :], psy[sl][:, :], sel, ALU.mult)
                    return v.tensor_tensor(dst, mtmp[:, :], dst, ALU.add)

                pipe(sg, len(itemsB), bprod, bcons, 4)

        def ycast(v, fin):
            fin(v.tensor_copy(hsb[:, :], yfp[:, :]))
        sg.step("vector", ["tensor"], ycast)
    c.pop()

    c.push()
    with nc.Block() as block:
        sg = Seg(c, block)
        psT2 = [c.stack.enter_context(nc.psum_tensor(c.uname(f"psT2{i}"), [128, 128], BF16))
                for i in range(2)]
        psB2 = c.stack.enter_context(nc.psum_tensor(c.uname("psB2"), [128, D], F32))
        fftile = c.stack.enter_context(nc.sbuf_tensor(c.uname("fftile"), [128, D], BF16))

        for g in range(NTC):
            def b2mm(t, fin, g=g):
                t.matmul(psB2[:, 0:512], mskrows[:, 128 * g:128 * (g + 1)],
                         b2b[:, 0:512], start=True, stop=True)
                fin(t.matmul(psB2[:, 512:1024], mskrows[:, 128 * g:128 * (g + 1)],
                             b2b[:, 512:1024], start=True, stop=True))
            sg.step("tensor", ["vector"], b2mm)

            def tprod(t, i, sl, g=g):
                dc = i
                return t.transpose(psT2[sl][:, :],
                                   hsb[:, T * dc + 128 * g:T * dc + 128 * (g + 1)],
                                   ident[:, :])

            def tcons(v, i, sl):
                dc = i
                return v.tensor_copy(fftile[:, 128 * dc:128 * (dc + 1)], psT2[sl][:, :])

            pipe(sg, 8, tprod, tcons, 2)

            def addb2(v, fin, g=g):
                fin(v.tensor_tensor(fftile[:, :], psB2[:, :], fftile[:, :], ALU.add))
            sg.step("vector", ["tensor"], addb2)

            def st(s, fin, g=g):
                fin(s.dma_start(scr["s_ffn"][g], fftile[:, :]), 16)
            sg.step("sync", ["vector"], st)
            sg.barrier_wait("vector", ["sync"])
    c.pop()

    c.push()
    with nc.Block() as block:
        sg = Seg(c, block)
        x0t = c.stack.enter_context(nc.sbuf_tensor("x0tM", [128, D], F32))
        att = c.stack.enter_context(nc.sbuf_tensor("attM", [128, D], BF16))
        fft = c.stack.enter_context(nc.sbuf_tensor("fftM", [128, D], BF16))
        x1t = [c.stack.enter_context(nc.sbuf_tensor(f"x1tM{i}", [128, D], F32)) for i in range(2)]
        xn1t = c.stack.enter_context(nc.sbuf_tensor("xn1tM", [128, D], BF16))
        xmt = c.stack.enter_context(nc.sbuf_tensor("xmtM", [128, D], BF16))
        sqt = c.stack.enter_context(nc.sbuf_tensor("sqtM2", [128, D], BF16))
        col = c.stack.enter_context(nc.sbuf_tensor("colM2", [128, 8], F32))
        psT = [c.stack.enter_context(nc.psum_tensor(f"psTM2{i}", [128, 128], BF16)) for i in range(2)]
        stg = c.stack.enter_context(nc.sbuf_tensor("stgM", [128, D], BF16))
        stg2 = c.stack.enter_context(nc.sbuf_tensor("stgM2", [128, D], BF16))

        for g in range(NTC):
            b = g % 2

            def ld(s, fin, g=g, b=b):
                fin(s.dma_start(x0t[:, :], io["x0"][128 * g:128 * (g + 1), :]), 16)
                fin(s.dma_start(att[:, :], scr["s_attn"][g]), 16)
                fin(s.dma_start(fft[:, :], scr["s_ffn"][g]), 16)
            sg.step("sync", ["vector", "scalar", "tensor"], ld)

            def xsum(v, fin, g=g, b=b):
                v.tensor_tensor(x1t[b][:, :], x0t[:, :], att[:, :], ALU.add)
                fin(v.tensor_tensor(x1t[b][:, :], x1t[b][:, :], fft[:, :], ALU.add))
            sg.step("vector", ["sync"], xsum)

            def stx(s, fin, g=g, b=b):
                fin(s.dma_start(scr["s_x1"][g], x1t[b][:, :]), 16)
            sg.step("sync", ["vector"], stx)

            def sq_(s, fin, b=b):
                fin(s.activation(sqt[:, :], x1t[b][:, :], AF.Square))
            sg.step("scalar", ["vector"], sq_)

            def stats(v, fin, b=b):
                v.tensor_reduce(col[:, 0:1], x1t[b][:, :], AX.X, ALU.add)
                v.tensor_reduce(col[:, 1:2], sqt[:, :], AX.X, ALU.add)
                v.drain()
                v.tensor_scalar(col[:, 0:1], col[:, 0:1], 1.0 / D, None, ALU.mult)
                v.tensor_scalar(col[:, 1:2], col[:, 1:2], 1.0 / D, None, ALU.mult)
                v.drain()
                v.tensor_tensor(col[:, 2:3], col[:, 0:1], col[:, 0:1], ALU.mult)
                v.drain()
                fin(v.tensor_tensor(col[:, 2:3], col[:, 1:2], col[:, 2:3], ALU.subtract))
            sg.step("vector", ["scalar"], stats)

            def rstd(s, fin):
                s.activation(col[:, 3:4], col[:, 2:3], AF.Ln, bias=LNEPS)
                s.drain()
                fin(s.activation(col[:, 3:4], col[:, 3:4], AF.Exp, scale=-0.5))
            sg.step("scalar", ["vector"], rstd)

            def apply(v, fin, g=g, b=b):
                v.scalar_tensor_tensor(col[:, 4:5], col[:, 0:1], -1.0, col[:, 3:4],
                                       ALU.mult, ALU.mult)
                v.drain()
                fin(v.tensor_scalar(xn1t[:, :], x1t[b][:, :], col[:, 3:4], col[:, 4:5],
                                    ALU.mult, ALU.add))
            sg.step("vector", ["scalar", "tensor", "sync"], apply)

            def stxn1(s, fin, g=g):
                fin(s.dma_start(scr["s_xn1"][g], xn1t[:, :]), 16)
            sg.step("sync", ["vector"], stxn1)

            def stats2(v, fin):
                v.tensor_scalar(col[:, 5:6], col[:, 2:3], LNEPS, None, ALU.add)
                v.drain()
                v.reciprocal(col[:, 6:7], col[:, 5:6])
                v.drain()
                fin(v.tensor_tensor(col[:, 2:3], col[:, 2:3], col[:, 6:7], ALU.mult))
            sg.step("vector", ["scalar"], stats2)

            def rstd2(s, fin):
                s.activation(col[:, 3:4], col[:, 2:3], AF.Ln, bias=LNEPS)
                s.drain()
                fin(s.activation(col[:, 3:4], col[:, 3:4], AF.Exp, scale=-0.5))
            sg.step("scalar", ["vector"], rstd2)

            def apply2(v, fin, g=g):
                v.tensor_scalar(xmt[:, :], xn1t[:, :], col[:, 3:4], None, ALU.mult)
                v.tensor_tensor(xmt[:, :], xmt[:, :], l1wbc[:, :], ALU.mult)
                fin(v.tensor_tensor(xmt[:, :], xmt[:, :], l1bbc[:, :], ALU.add))
            sg.step("vector", ["scalar", "tensor", "sync"], apply2)

            sg.barrier_wait("tensor", ["vector", "sync"])

            def tp1(t, i, sl, g=g):
                return t.transpose(psT[sl][:, :], xn1t[:, 128 * i:128 * (i + 1)], ident[:, :])

            def tc1(v, i, sl, g=g):
                return v.tensor_copy(stg2[:, 128 * i:128 * (i + 1)], psT[sl][:, :])

            pipe(sg, NDB, tp1, tc1, 2)

            def st1(s, fin, g=g):
                fin(s.dma_start(
                    scr["s_xn1f"][:, :, 128 * g:128 * (g + 1)].rearrange("j p tt -> p j tt"),
                    stg2[:, :].rearrange("p (j tt) -> p j tt", j=NDB)), 16)
            sg.step("sync", ["vector"], st1)

            def tp2(t, i, sl, g=g):
                return t.transpose(psT[sl][:, :], xmt[:, 128 * i:128 * (i + 1)], ident[:, :])

            def tc2(v, i, sl, g=g):
                return v.tensor_copy(stg[:, 128 * i:128 * (i + 1)], psT[sl][:, :])

            sg.barrier_wait("tensor", ["sync"])
            pipe(sg, NDB, tp2, tc2, 2)

            def st2(s, fin, g=g):
                fin(s.dma_start(
                    scr["s_xmf"][:, :, 128 * g:128 * (g + 1)].rearrange("j p tt -> p j tt"),
                    stg[:, :].rearrange("p (j tt) -> p j tt", j=NDB)), 16)
            sg.step("sync", ["vector"], st2)
            sg.barrier_wait("vector", ["sync"])
    c.pop()


def seg_mamba(c, io, scr, l1ctx=None):
    nc = c.nc
    sb = lambda n, shp, dt: c.stack.enter_context(nc.sbuf_tensor(n, shp, dt))
    ones1b = sb("ones1bX", [1, 128], BF16)
    convw = sb("mconvw", [128, NDI * 4], F32)
    convb = sb("mconvb", [128, NDI], F32)
    bdt = sb("bdtX", [128, NDI], F32)
    mD = sb("mDX", [128, NDI], F32)
    a_sb = sb("a_sbX", [128, NDI * DS], F32)
    bcd = nc.dram_tensor("bcdram", [2 * DS, T], BF16).ap()
    carry = sb("carryX", [128, 2 * DS * NDI // 128 * 8] if False else [128, 256], F32)

    with nc.Block() as block:
        sg = Seg(c, block)

        def ld(s, fin):
            fin(s.dma_start(convw[:, :], io["c_mconv_w"]), 16)
            fin(s.dma_start(convb[:, :], io["c_mconv_b"]), 16)
            fin(s.dma_start(bdt[:, :], io["c_bdt"]), 16)
            fin(s.dma_start(mD[:, :], io["c_mD"]), 16)
            fin(s.dma_start(a_sb[:, :], io["c_Alog"]), 16)
        sg.step("sync", [], ld)

        def ms(v, fin):
            v.memset(carry[:, :], 0.0)
            fin(v.memset(ones1b[:, :], 1.0))
        sg.step("vector", [], ms)

        def mka(s, fin):
            fin(s.activation(a_sb[:, :], a_sb[:, :], AF.Exp))
        sg.step("scalar", ["sync"], mka)

        def nega(v, fin):
            v.drain()
            fin(v.tensor_scalar(a_sb[:, :], a_sb[:, :], -1.0, None, ALU.mult))
        sg.step("vector", ["scalar"], nega)

    c.push()
    with nc.Block() as block:
        sg = Seg(c, block)
        xmf = c.stack.enter_context(nc.sbuf_tensor("xmfX", [128, NDB * T], BF16))
        wu = c.stack.enter_context(nc.sbuf_tensor("wuX", [128, NDB * 128], BF16))
        wz = c.stack.enter_context(nc.sbuf_tensor("wzX", [128, NDB * 128], BF16))
        wxb = c.stack.enter_context(nc.sbuf_tensor("wxbX", [128, NDI * 128], BF16))
        uraw = c.stack.enter_context(nc.sbuf_tensor("urawX", [128, T], BF16))
        ucnv = c.stack.enter_context(nc.sbuf_tensor("ucnvX", [128, T], BF16))
        zt = c.stack.enter_context(nc.sbuf_tensor("ztX", [128, T], BF16))
        bc32 = c.stack.enter_context(nc.sbuf_tensor("bc32X", [32, T], BF16))
        wdt = c.stack.enter_context(nc.sbuf_tensor("wdtX", [64, DI], BF16))
        dtr = c.stack.enter_context(nc.sbuf_tensor("dtrX", [64, T], BF16))
        psx = [c.stack.enter_context(nc.psum_tensor(c.uname("psxX"), [128, 512], F32))
               for i in range(2)]
        psdt = [c.stack.enter_context(nc.psum_tensor(c.uname("psdtX"), [128, 512], F32))
                for i in range(4)]

        def ldxm(s, fin):
            for j in range(NDB):
                fin(s.dma_start(xmf[:, T * j:T * (j + 1)], scr["s_xmf"][j]), 16)
        sg.step("sync", [], ldxm)

        def ldwx(g, fin):
            fin(g.dma_start(*ld3(wxb[:, :], io["w_WxT"])), 16)
            fin(g.dma_start(wdt[:, :], io["w_WdtT"]), 16)
        sg.step("gpsimd", [], ldwx)

        for db in range(NDI):
            def ldw(g, fin, db=db):
                fin(g.dma_start(*ld3(wu[:, :], io["w_WinT"][:, 128 * db:128 * (db + 1)])), 16)
                fin(g.dma_start(*ld3(wz[:, :], io["w_WinT"][:, DI + 128 * db:DI + 128 * (db + 1)])), 16)
            sg.step("gpsimd", ["tensor"], ldw)
            sg.barrier_wait("tensor", ["gpsimd", "vector", "scalar", "sync"])

            items = [(0, sl) for sl in range(4)] + [(1, sl) for sl in range(4)]

            def produce(t, i, s, db=db):
                uz, sl = items[i]
                w = wu if uz == 0 else wz
                last = None
                for jj in range(NDB):
                    last = t.matmul(
                        psx[s][:, :], w[:, 128 * jj:128 * (jj + 1)],
                        xmf[:, T * jj + 512 * sl:T * jj + 512 * (sl + 1)],
                        start=(jj == 0), stop=(jj == NDB - 1))
                return last

            def consume(v, i, s, db=db):
                uz, sl = items[i]
                dst = uraw if uz == 0 else zt
                return v.tensor_copy(dst[:, 512 * sl:512 * (sl + 1)], psx[s][:, :])

            pipe(sg, len(items), produce, consume, 2)

            def conv(v, fin, db=db):
                w0 = convw[:, 4 * db + 0:4 * db + 1]
                w1 = convw[:, 4 * db + 1:4 * db + 2]
                w2 = convw[:, 4 * db + 2:4 * db + 3]
                w3 = convw[:, 4 * db + 3:4 * db + 4]
                v.tensor_scalar(ucnv[:, :], uraw[:, :], w3, None, ALU.mult)
                v.scalar_tensor_tensor(ucnv[:, 1:T], uraw[:, 0:T - 1], w2, ucnv[:, 1:T],
                                       ALU.mult, ALU.add)
                v.scalar_tensor_tensor(ucnv[:, 2:T], uraw[:, 0:T - 2], w1, ucnv[:, 2:T],
                                       ALU.mult, ALU.add)
                fin(v.scalar_tensor_tensor(ucnv[:, 3:T], uraw[:, 0:T - 3], w0, ucnv[:, 3:T],
                                           ALU.mult, ALU.add))
            sg.step("vector", ["tensor"], conv)

            def usilu(s, fin, db=db):
                fin(s.activation(ucnv[:, :], ucnv[:, :], AF.Silu, bias=convb[:, db:db + 1]))
            sg.step("scalar", ["vector"], usilu)

            def stu(s, fin, db=db):
                fin(s.dma_start(scr["s_u"][db], ucnv[:, :]), 16)
                fin(s.dma_start(scr["s_z"][db], zt[:, :]), 16)
            sg.step("sync", ["scalar", "vector"], stu)

            def dtmm(t, fin, db=db):
                last = None
                for sl in range(4):
                    last = t.matmul(psdt[sl][:, :], wxb[:, 128 * db:128 * (db + 1)],
                                    ucnv[:, 512 * sl:512 * (sl + 1)],
                                    start=(db == 0), stop=(db == NDI - 1))
                fin(last)
            sg.step("tensor", ["scalar", "sync"], dtmm)

        def dtev(v, fin):
            last = None
            for sl in range(4):
                v.tensor_copy(dtr[:, 512 * sl:512 * (sl + 1)], psdt[sl][0:64, :])
                last = v.tensor_copy(bc32[:, 512 * sl:512 * (sl + 1)], psdt[sl][64:96, :])
            fin(last)
        sg.step("vector", ["tensor"], dtev)

        def stbc(s, fin):
            fin(s.dma_start(bcd[0:DS, :], bc32[0:16, :]), 16)
            fin(s.dma_start(bcd[DS:2 * DS, :], bc32[16:32, :]), 16)
        sg.step("sync", ["vector"], stbc)
        sg.barrier_wait("vector", ["sync"])
    c.pop()

    c.push()
    with nc.Block() as block:
        sg = Seg(c, block)
        psd = [c.stack.enter_context(nc.psum_tensor(c.uname("psdX"), [128, 512], F32))
               for i in range(2)]
        ut = c.stack.enter_context(nc.sbuf_tensor("utX2", [128, T], BF16))
        sp1 = c.stack.enter_context(nc.sbuf_tensor("sp1X", [128, T], F32))
        sp2 = c.stack.enter_context(nc.sbuf_tensor("sp2X", [128, T], F32))
        dlt = c.stack.enter_context(nc.sbuf_tensor("dltX", [128, T], BF16))

        for db in range(NDI):
            def ldu(s, fin, db=db):
                fin(s.dma_start(ut[:, :], scr["s_u"][db]), 16)
            sg.step("sync", ["vector", "scalar"], ldu)

            for pr in range(2):
                def dmm(t, fin, db=db, pr=pr):
                    t.matmul(psd[0][:, :], wdt[:, 128 * db:128 * (db + 1)],
                             dtr[:, 1024 * pr:1024 * pr + 512], start=True, stop=True)
                    fin(t.matmul(psd[1][:, :], wdt[:, 128 * db:128 * (db + 1)],
                                 dtr[:, 1024 * pr + 512:1024 * (pr + 1)],
                                 start=True, stop=True))
                sg.step("tensor", ["vector", "scalar"], dmm)

                def soft(s, fin, db=db, pr=pr):
                    o = 1024 * pr
                    s.activation(sp1[:, o:o + 512], psd[0][:, :], AF.Abs,
                                 bias=bdt[:, db:db + 1])
                    s.activation(sp1[:, o + 512:o + 1024], psd[1][:, :], AF.Abs,
                                 bias=bdt[:, db:db + 1])
                    s.activation(sp1[:, o:o + 1024], sp1[:, o:o + 1024], AF.Exp, scale=-1.0)
                    fin(s.activation(sp1[:, o:o + 1024], sp1[:, o:o + 1024], AF.Ln, bias=1.0))
                sg.step("scalar", ["tensor"], soft)

                def relu(v, fin, db=db, pr=pr):
                    o = 1024 * pr
                    v.tensor_scalar(sp2[:, o:o + 512], psd[0][:, :], bdt[:, db:db + 1], 0.0,
                                    ALU.add, ALU.max)
                    fin(v.tensor_scalar(sp2[:, o + 512:o + 1024], psd[1][:, :],
                                        bdt[:, db:db + 1], 0.0, ALU.add, ALU.max))
                sg.step("vector", ["tensor"], relu)

            def dsum(v, fin, db=db):
                v.tensor_tensor(dlt[:, :], sp1[:, :], sp2[:, :], ALU.add)
                fin(v.tensor_tensor(ut[:, :], dlt[:, :], ut[:, :], ALU.mult))
            sg.step("vector", ["scalar", "sync"], dsum)

            def std(s, fin, db=db):
                fin(s.dma_start(scr["s_delta"][db], dlt[:, :]), 16)
                fin(s.dma_start(scr["s_du"][db], ut[:, :]), 16)
            sg.step("sync", ["vector"], std)
    c.pop()

    TH = T // 2
    c.push()
    dh = c.stack.enter_context(nc.sbuf_tensor("dhX", [128, NDI * TH], BF16))
    duh = c.stack.enter_context(nc.sbuf_tensor("duhX", [128, NDI * TH], BF16))
    yh = c.stack.enter_context(nc.sbuf_tensor("yhX", [128, NDI * TH], BF16))
    Bbc = c.stack.enter_context(nc.sbuf_tensor("BbcX", [128, TH], BF16))
    Cbc = c.stack.enter_context(nc.sbuf_tensor("CbcX", [128, TH], BF16))
    brow1 = c.stack.enter_context(nc.sbuf_tensor("brow1X", [1, TH], BF16))
    crow1 = c.stack.enter_context(nc.sbuf_tensor("crow1X", [1, TH], BF16))
    dA4 = [c.stack.enter_context(nc.sbuf_tensor(f"dA4X{i}", [128, 4 * TH], BF16))
           for i in range(2)]
    X4 = c.stack.enter_context(nc.sbuf_tensor("X4X", [128, 4 * TH], BF16))
    h44 = [c.stack.enter_context(nc.sbuf_tensor(f"h4X{i}", [128, 4 * TH], BF16))
           for i in range(2)]
    psb = [c.stack.enter_context(nc.psum_tensor(c.uname("psbX"), [128, 512], F32))
           for i in range(2)]

    ml = l1ctx is not None
    if ml:
        wcd, wpd = l1ctx["wcd"], l1ctx["wpd"]
        xn1s = c.stack.enter_context(nc.sbuf_tensor("xn1sX", [128, NDB * 512], BF16))
        wfc = [c.stack.enter_context(nc.sbuf_tensor(f"wfcX{i}", [128, 1024], BF16))
               for i in range(2)]
        r2 = c.stack.enter_context(nc.sbuf_tensor("r2X", [128, NFB * 512], BF16))
        wpj = [c.stack.enter_context(nc.sbuf_tensor(f"wpjX{i}", [128, 1024], BF16))
               for i in range(2)]
        hbf = c.stack.enter_context(nc.sbuf_tensor("hbfX", [128, 1024], BF16))
        pstg = c.stack.enter_context(nc.sbuf_tensor("pstgX", [128, 2048], BF16))
        psh = [c.stack.enter_context(nc.psum_tensor(c.uname("pshX"), [128, 512], F32))
               for i in range(2)]
        psp = [c.stack.enter_context(nc.psum_tensor(c.uname("pspX"), [128, 512], F32))
               for i in range(4)]
        L = dict(pend=c.val("pp"), cc_end=c.val("cc"), qq_end=c.val("qq"),
                 proj_pp=0, evac_cc=0, nhalf=0, store_dd=[], qb={})

    def emit_h(block, tsl):
        pb = c.val("pp")
        yv = {}

        def gload(g, pb=pb, tsl=tsl):
            if L["pend"] > 0:
                c.wait(g, "pp", L["pend"])
            c.inc("eG", g.dma_start(
                xn1s[:, :].rearrange("p (j t) -> p j t", j=NDB),
                scr["s_xn1f"][:, :, 512 * tsl:512 * (tsl + 1)].rearrange(
                    "j p t -> p j t")), 16)
            for fc in range(NFB):
                if fc >= 2:
                    c.wait(g, "pp", pb + fc - 1)
                c.inc("eG", g.dma_start(
                    wfc[fc % 2][:, :].rearrange("p (dc t) -> p dc t", dc=NDB),
                    wcd[fc].rearrange("dc p t -> p dc t")), 16)
                yv[fc] = c.val("eG")
        block.gpsimd(gload)

        q0 = c.val("cc")
        qb = c.val("qq")
        L["qb"][tsl] = qb

        def produce(t, pb=pb, q0=q0):
            for fc in range(NFB):
                c.wait(t, "eG", yv[fc])
                w = q0 + fc - 1 if fc >= 2 else L["cc_end"]
                if w > 0:
                    c.wait(t, "cc", w)
                last = None
                for jj in range(NDB):
                    last = t.matmul(psh[fc % 2][:, :],
                                    wfc[fc % 2][:, 128 * jj:128 * (jj + 1)],
                                    xn1s[:, 512 * jj:512 * (jj + 1)],
                                    start=(jj == 0), stop=(jj == NDB - 1))
                c.inc("pp", last)
        block.tensor(produce)

        ppj = L["proj_pp"]

        def hcons(s, pb=pb, ppj=ppj):
            for fc in range(NFB):
                c.wait(s, "pp", pb + fc + 1)
                if fc == 0 and ppj > 0:
                    c.wait(s, "pp", ppj)
                c.inc("cc", s.activation(hbf[:, 512 * (fc % 2):512 * (fc % 2 + 1)],
                                         psh[fc % 2][:, :], AF.Relu))
                c.inc("qq", s.activation(r2[:, 512 * fc:512 * (fc + 1)],
                                         hbf[:, 512 * (fc % 2):512 * (fc % 2 + 1)],
                                         AF.Square))
        block.scalar(hcons)
        L["pend"] = c.val("pp")
        L["cc_end"] = c.val("cc")
        L["qq_end"] = c.val("qq")

    def emit_pmm(block, tsl, half):
        qb = L["qb"][tsl]
        pb = c.val("pp")
        ypv = {}

        def gload(g, pb=pb):
            for fc in range(NFB):
                w = pb + fc - 1 if fc >= 2 else L["pend"]
                if w > 0:
                    c.wait(g, "pp", w)
                c.inc("eG", g.dma_start(wpj[fc % 2][:, :], wpd[fc]), 16)
                ypv[fc] = c.val("eG")
        block.gpsimd(gload)

        evprev = L["evac_cc"]

        def pmm(t, qb=qb, half=half, evprev=evprev):
            for fc in range(NFB):
                c.wait(t, "eG", ypv[fc])
                c.wait(t, "qq", qb + fc + 1)
                if fc == 0 and evprev > 0:
                    c.wait(t, "cc", evprev)
                last = None
                for ti in range(2):
                    tci = 2 * half + ti
                    for dsl in range(2):
                        last = t.matmul(
                            psp[2 * ti + dsl][:, :],
                            r2[:, 512 * fc + 128 * tci:512 * fc + 128 * (tci + 1)],
                            wpj[fc % 2][:, 512 * dsl:512 * (dsl + 1)],
                            start=(fc == 0), stop=(fc == NFB - 1))
                c.inc("pp", last)
        block.tensor(pmm)
        L["pmm_pp"] = c.val("pp")
        L["pend"] = c.val("pp")
        L["proj_pp"] = c.val("pp")

    def emit_evac(block, tsl, half):
        pend2 = L["pmm_pp"]
        ddg = L["store_dd"][-1] if L["store_dd"] else 0

        def evac(v, pend2=pend2, ddg=ddg):
            c.wait(v, "pp", pend2)
            if ddg > 0:
                c.wait(v, "dd", ddg)
            last = None
            for ti in range(2):
                for dsl in range(2):
                    last = v.tensor_copy(
                        pstg[:, 1024 * ti + 512 * dsl:1024 * ti + 512 * (dsl + 1)],
                        psp[2 * ti + dsl][:, :])
            c.inc("cc", last)
        block.vector(evac)

        ecc = c.val("cc")

        def store(s, ecc=ecc, half=half, tsl=tsl):
            c.wait(s, "cc", ecc)
            for ti in range(2):
                g_ = 4 * tsl + 2 * half + ti
                c.inc("dd", s.dma_start(scr["s_ffn2"][g_],
                                        pstg[:, 1024 * ti:1024 * (ti + 1)]), 16)
        block.sync(store)
        L["store_dd"].append(c.val("dd"))
        L["evac_cc"] = ecc
        L["cc_end"] = c.val("cc")

    for th in range(2):
        with nc.Block() as block:
            sg = Seg(c, block)

            def ldh(s, fin, th=th):
                for db in range(NDI):
                    fin(s.dma_start(dh[:, TH * db:TH * (db + 1)],
                                    scr["s_delta"][db][:, TH * th:TH * (th + 1)]), 16)
                    fin(s.dma_start(duh[:, TH * db:TH * (db + 1)],
                                    scr["s_du"][db][:, TH * th:TH * (th + 1)]), 16)
            sg.step("sync", ["vector", "scalar", "gpsimd"], ldh)
            schist = []
            yv_ldh = c.val("eY")
            pendc = [None]

            for s_ in range(DS):
                def ldrow(s, fin, s_=s_, th=th):
                    fin(s.dma_start(brow1[:, :], bcd[s_:s_ + 1, TH * th:TH * (th + 1)]), 16)
                    fin(s.dma_start(crow1[:, :],
                                    bcd[DS + s_:DS + s_ + 1, TH * th:TH * (th + 1)]), 16)
                sg.step("sync", ["vector", "tensor"], ldrow)

                def bcmm(t, fin, s_=s_):
                    t.matmul(psb[0][:, :], ones1b[:, :], brow1[0:1, 0:512],
                             start=True, stop=True)
                    fin(t.matmul(psb[1][:, :], ones1b[:, :], brow1[0:1, 512:1024],
                                 start=True, stop=True))
                sg.step("tensor", ["sync", "vector"], bcmm)

                def bcev(v, fin):
                    v.tensor_copy(Bbc[:, 0:512], psb[0][:, :])
                    fin(v.tensor_copy(Bbc[:, 512:1024], psb[1][:, :]))
                sg.step("vector", ["tensor", "gpsimd"], bcev)

                def ccmm(t, fin, s_=s_):
                    t.matmul(psb[0][:, :], ones1b[:, :], crow1[0:1, 0:512],
                             start=True, stop=True)
                    fin(t.matmul(psb[1][:, :], ones1b[:, :], crow1[0:1, 512:1024],
                                 start=True, stop=True))
                sg.step("tensor", ["vector"], ccmm)

                def ccev(v, fin):
                    v.tensor_copy(Cbc[:, 0:512], psb[0][:, :])
                    fin(v.tensor_copy(Cbc[:, 512:1024], psb[1][:, :]))
                sg.step("vector", ["tensor", "gpsimd"], ccev)

                for grp in range(4):
                    gi = 4 * s_ + grp
                    dbs = [4 * grp + k for k in range(4)]
                    dA = dA4[grp % 2]

                    hv = schist[gi - 2] if gi >= 2 else 0

                    def dAe(s, fin, s_=s_, dbs=dbs, dA=dA, hv=hv):
                        if hv > 0:
                            c.wait(s, "eV", hv)
                        else:
                            c.wait(s, "eY", yv_ldh)
                        last = None
                        for k, db in enumerate(dbs):
                            last = s.activation(dA[:, TH * k:TH * (k + 1)],
                                                dh[:, TH * db:TH * (db + 1)], AF.Exp,
                                                scale=a_sb[:, DS * db + s_:DS * db + s_ + 1])
                        fin(last)
                    sg.step("scalar", [], dAe)

                    def xb(v, fin, s_=s_, dbs=dbs, th=th, dA=dA, grp=grp,
                           pc=pendc[0]):
                        h4 = h44[grp % 2]
                        if pc is not None:
                            pdbs, ps_, ppar = pc
                            for k in range(4):
                                idx = DS * pdbs[k] + ps_
                                v.tensor_copy(carry[:, idx:idx + 1],
                                              h44[ppar][:, TH * (k + 1) - 1:TH * (k + 1)])
                        for k in range(4):
                            v.tensor_tensor(X4[:, TH * k:TH * (k + 1)],
                                            duh[:, TH * dbs[k]:TH * (dbs[k] + 1)],
                                            Bbc[:, :], ALU.mult)
                        for k in range(4):
                            idx = DS * dbs[k] + s_
                            v.tensor_tensor_scan(h4[:, TH * k:TH * (k + 1)],
                                                 dA[:, TH * k:TH * (k + 1)],
                                                 X4[:, TH * k:TH * (k + 1)],
                                                 carry[:, idx:idx + 1] if th == 1 else 0.0,
                                                 ALU.mult, ALU.add)
                        for k in range(4):
                            v.tensor_tensor(X4[:, TH * k:TH * (k + 1)],
                                            h4[:, TH * k:TH * (k + 1)], Cbc[:, :],
                                            ALU.mult)
                        last = None
                        for k in range(4):
                            dst = yh[:, TH * dbs[k]:TH * (dbs[k] + 1)]
                            if s_ == 0:
                                last = v.tensor_copy(dst, X4[:, TH * k:TH * (k + 1)])
                            else:
                                last = v.tensor_tensor(dst, X4[:, TH * k:TH * (k + 1)],
                                                       dst, ALU.add)
                        fin(last)
                    sg.step("vector", ["scalar"], xb)
                    schist.append(c.val("eV"))
                    pendc[0] = (dbs, s_, grp % 2) if th == 0 else None

                if ml:
                    if s_ == 2:
                        emit_h(block, 2 * th)
                    elif s_ == 5:
                        emit_pmm(block, 2 * th, 0)
                    elif s_ == 7:
                        emit_evac(block, 2 * th, 0)
                    elif s_ == 8:
                        emit_pmm(block, 2 * th, 1)
                    elif s_ == 9:
                        emit_h(block, 2 * th + 1)
                    elif s_ == 10:
                        emit_evac(block, 2 * th, 1)
                    elif s_ == 12:
                        emit_pmm(block, 2 * th + 1, 0)
                    elif s_ == 14:
                        emit_evac(block, 2 * th + 1, 0)
                        emit_pmm(block, 2 * th + 1, 1)
                    elif s_ == 15:
                        emit_evac(block, 2 * th + 1, 1)

            if pendc[0] is not None:
                def cflush(v, fin, pc=pendc[0]):
                    pdbs, ps_, ppar = pc
                    last = None
                    for k in range(4):
                        idx = DS * pdbs[k] + ps_
                        last = v.tensor_copy(carry[:, idx:idx + 1],
                                             h44[ppar][:, TH * (k + 1) - 1:TH * (k + 1)])
                    fin(last)
                sg.step("vector", [], cflush)
                pendc[0] = None

            def sty(s, fin, th=th):
                for db in range(NDI):
                    fin(s.dma_start(scr["s_yac"][db][:, TH * th:TH * (th + 1)],
                                    yh[:, TH * db:TH * (db + 1)]), 16)
            sg.step("sync", ["vector", "gpsimd"], sty)
            sg.barrier_wait("vector", ["sync"])
    c.pop()

    c.push()
    with nc.Block() as block:
        sg = Seg(c, block)
        t4 = c.stack.enter_context(nc.sbuf_tensor("t4X", [128, NDI * T], BF16))
        ut = c.stack.enter_context(nc.sbuf_tensor("utX3", [128, T], BF16))
        zt = c.stack.enter_context(nc.sbuf_tensor("ztX3", [128, T], BF16))
        yt = c.stack.enter_context(nc.sbuf_tensor("ytX3", [128, T], BF16))
        woutb = c.stack.enter_context(nc.sbuf_tensor("woutbX", [128, NDI * D], BF16))
        stg = c.stack.enter_context(nc.sbuf_tensor("stgX", [128, 2 * D], BF16))
        pso = [c.stack.enter_context(nc.psum_tensor(c.uname("psoX"), [128, 512], F32))
               for i in range(2)]

        def ldwo(g, fin):
            fin(g.dma_start(*ld3(woutb[:, :], io["w_WoutT"])), 16)
        sg.step("gpsimd", [], ldwo)

        for db in range(NDI):
            def ld(s, fin, db=db):
                fin(s.dma_start(ut[:, :], scr["s_u"][db]), 16)
                fin(s.dma_start(zt[:, :], scr["s_z"][db]), 16)
                fin(s.dma_start(yt[:, :], scr["s_yac"][db]), 16)
            sg.step("sync", ["vector", "scalar"], ld)

            def zs(s, fin, db=db):
                fin(s.activation(zt[:, :], zt[:, :], AF.Silu))
            sg.step("scalar", ["sync"], zs)

            def comb(v, fin, db=db):
                v.tensor_scalar(ut[:, :], ut[:, :], mD[:, db:db + 1], None, ALU.mult)
                v.tensor_tensor(ut[:, :], yt[:, :], ut[:, :], ALU.add)
                fin(v.tensor_tensor(t4[:, T * db:T * (db + 1)], ut[:, :], zt[:, :],
                                    ALU.mult))
            sg.step("vector", ["scalar", "sync"], comb)

        sg.barrier_wait("tensor", ["vector"])
        items = [(g, dsl) for g in range(NTC) for dsl in range(2)]

        def produce(t, i, sl):
            g, dsl = items[i]
            last = None
            for db in range(NDI):
                last = t.matmul(pso[sl][:, :],
                                t4[:, T * db + 128 * g:T * db + 128 * (g + 1)],
                                woutb[:, D * db + 512 * dsl:D * db + 512 * (dsl + 1)],
                                start=(db == 0), stop=(db == NDI - 1))
            return last

        def consume(v, i, sl, st):
            return v.tensor_copy(stg[:, 1024 * st:1024 * st + 512], pso[sl][:, :])

        def store(y, i, st):
            g, dsl = items[i]
            return y.dma_start(scr["s_mamba"][g][:, 512 * dsl:512 * (dsl + 1)],
                               stg[:, 1024 * st:1024 * st + 512])

        pipe3(sg, len(items), produce, consume, store, 2, 2)
    c.pop()


def seg_l1f_pre(c, io, scr):
    nc = c.nc
    sb = lambda n, shp, dt: c.stack.enter_context(nc.sbuf_tensor(n, shp, dt))
    ones1f = sb("ones1fL", [1, 128], F32)
    gcol = sb("gcolL", [128, 1], F32)
    identL = sb("identL", [128, 128], BF16)
    wfcTd = nc.dram_tensor("wfcT_d", [NDB, 128, FF], BF16).ap()
    wcd = nc.dram_tensor("wcomb_d", [NFB, 8, 128, 128], BF16).ap()
    wpd = nc.dram_tensor("wproj_d", [NFB, 128, D], BF16).ap()

    with nc.Block() as block:
        sg = Seg(c, block)

        def ld(s, fin):
            fin(s.dma_start(identL[:, :], io["c_ident"]), 16)
        sg.step("sync", [], ld)

        def ms(v, fin):
            fin(v.memset(ones1f[:, :], 1.0))
        sg.step("vector", [], ms)

    c.push()
    with nc.Block() as block:
        sg = Seg(c, block)
        psg = c.stack.enter_context(nc.psum_tensor(c.uname("psgL"), [128, 512], F32))
        grow = c.stack.enter_context(nc.sbuf_tensor("growL", [1, 1], F32))

        def ldg(s, fin):
            fin(s.dma_start(grow[:, :], io["v_lrgate"]), 16)
        sg.step("sync", [], ldg)

        def gmm(t, fin):
            fin(t.matmul(psg[:, 0:1], ones1f[:, :], grow[0:1, 0:1], start=True, stop=True))
        sg.step("tensor", ["sync", "vector"], gmm)

        def gev(v, fin):
            fin(v.tensor_copy(gcol[:, :], psg[:, 0:1]))
        sg.step("vector", ["tensor"], gev)
    c.pop()

    AFCopy = AF.Copy if hasattr(AF, "Copy") else AF.Identity

    def ternarize(io_key, nrow_tiles, ncol, store_fn):
        c.push()
        with nc.Block() as block:
            sg = Seg(c, block)
            rowt = [c.stack.enter_context(nc.sbuf_tensor(c.uname("rowtL"), [128, ncol], F32))
                    for i in range(2)]
            weffr = [c.stack.enter_context(nc.sbuf_tensor(c.uname("weffrL"), [128, ncol], BF16))
                     for i in range(2)]
            ge = c.stack.enter_context(nc.sbuf_tensor(c.uname("geL"), [128, ncol], BF16))
            le = c.stack.enter_context(nc.sbuf_tensor(c.uname("leL"), [128, ncol], BF16))
            scol = c.stack.enter_context(nc.sbuf_tensor(c.uname("scolL"), [128, 2], F32))
            stg = c.stack.enter_context(nc.sbuf_tensor(c.uname("stgL"), [128, ncol], BF16))
            pst = [c.stack.enter_context(nc.psum_tensor(c.uname("pstL"), [128, 128], BF16))
                   for i in range(2)]
            eyst = [0]

            for i in range(nrow_tiles):
                b = i % 2

                def ldr(s, fin, i=i, b=b):
                    fin(s.dma_start(rowt[b][:, :], io[io_key][128 * i:128 * (i + 1), :]), 16)
                sg.step("sync", ["vector"], ldr)

                def tern(v, fin, i=i, b=b):
                    v.tensor_reduce(scol[:, 0:1], rowt[b][:, :], AX.X, ALU.add,
                                    apply_absolute_value=True)
                    v.drain()
                    v.tensor_scalar(scol[:, 0:1], scol[:, 0:1], 1.0 / ncol, 1e-5,
                                    ALU.mult, ALU.max)
                    v.drain()
                    v.reciprocal(scol[:, 1:2], scol[:, 0:1])
                    v.drain()
                    v.tensor_scalar(ge[:, :], rowt[b][:, :], scol[:, 1:2], 0.5,
                                    ALU.mult, ALU.is_gt)
                    v.tensor_scalar(le[:, :], rowt[b][:, :], scol[:, 1:2], -0.5,
                                    ALU.mult, ALU.is_lt)
                    v.tensor_tensor(ge[:, :], ge[:, :], le[:, :], ALU.subtract)
                    v.drain()
                    fin(v.tensor_scalar(weffr[b][:, :], ge[:, :], scol[:, 0:1], None,
                                        ALU.mult))
                sg.step("vector", ["sync", "tensor"], tern)
                pw = [("eV", c.val("eV"))]
                if eyst[0] > 0:
                    pw.append(("eY", eyst[0]))

                def tp(t, k, sl, i=i, b=b):
                    return t.transpose(pst[sl][:, :], weffr[b][:, 128 * k:128 * (k + 1)],
                                       identL[:, :])

                def tc(s, k, sl, i=i):
                    return s.activation(stg[:, 128 * k:128 * (k + 1)], pst[sl][:, :],
                                        AFCopy)

                pipe(sg, ncol // 128, tp, tc, 2, ceng="scalar", pwait=pw)

                def stw(s, fin, i=i):
                    fin(store_fn(s, i, stg), 16)
                sg.step("sync", ["scalar"], stw)
                eyst[0] = c.val("eY")
        c.pop()

    def store_fc(s, i, stg):
        return s.dma_start(
            wfcTd[:, :, 128 * i:128 * (i + 1)].rearrange("j p f -> p j f"),
            stg[:, 0:D].rearrange("p (j f) -> p j f", j=NDB))

    def store_proj(s, i, stg):
        return s.dma_start(
            wpd[:, :, 128 * i:128 * (i + 1)].rearrange("fc p d -> p fc d"),
            stg[:, 0:FF].rearrange("p (fc d) -> p fc d", fc=NFB))

    ternarize("w_fc", NFB, D, store_fc)
    ternarize("w_proj", NDB, FF, store_proj)

    c.push()
    with nc.Block() as block:
        sg = Seg(c, block)
        fbT = c.stack.enter_context(nc.sbuf_tensor("fbTL", [16, 16], BF16))
        vb = c.stack.enter_context(nc.sbuf_tensor("vbL", [16, D], BF16))
        uTb = c.stack.enter_context(nc.sbuf_tensor("uTbL", [16, FF], BF16))
        m1t = c.stack.enter_context(nc.sbuf_tensor("m1tL", [16, D], BF16))
        wslc = [c.stack.enter_context(nc.sbuf_tensor(f"wslcL{i}", [128, 512], BF16))
                for i in range(2)]
        stg = c.stack.enter_context(nc.sbuf_tensor("stg2L", [128, 2 * 512], BF16))
        psl = [c.stack.enter_context(nc.psum_tensor(c.uname("pslL"), [128, 512], F32))
               for i in range(2)]

        def ldw(g, fin):
            fin(g.dma_start(fbT[:, :], io["w_fb_T"]), 16)
            fin(g.dma_start(vb[:, :], io["w_lrv"]), 16)
            fin(g.dma_start(uTb[:, :], io["w_lruT"]), 16)
        sg.step("gpsimd", [], ldw)

        def m1mm(t, fin):
            t.matmul(psl[0][0:16, :], fbT[:, :], vb[:, 0:512], start=True, stop=True)
            fin(t.matmul(psl[1][0:16, :], fbT[:, :], vb[:, 512:1024], start=True, stop=True))
        sg.step("tensor", ["gpsimd", "vector"], m1mm)

        def m1ev(v, fin):
            v.tensor_copy(m1t[:, 0:512], psl[0][0:16, :])
            fin(v.tensor_copy(m1t[:, 512:1024], psl[1][0:16, :]))
        sg.step("vector", ["tensor"], m1ev)

        items = [(dc, fsl) for dc in range(8) for fsl in range(8)]
        N64 = len(items)
        p0 = c.val("pp")
        q0 = c.val("cc")
        ev0 = c.val("eV")
        ey0 = c.val("eY")
        yld = {}
        yst = {}

        def _ld(s, i):
            dc, fsl = items[i]
            c.inc("eY", s.dma_start(wslc[i % 2][:, :],
                                    wfcTd[dc, :, 512 * fsl:512 * (fsl + 1)]), 16)
            yld[i] = c.val("eY")

        def syncsec(s):
            _ld(s, 0)
            _ld(s, 1)
            for i in range(N64):
                dc, fsl = items[i]
                c.wait(s, "cc", q0 + i + 1)
                c.inc("eY", s.dma_start(
                    wcd[4 * fsl:4 * (fsl + 1), dc].rearrange("ff p t -> p ff t"),
                    stg[:, 512 * (i % 2):512 * (i % 2 + 1)].rearrange(
                        "p (ff t) -> p ff t", ff=4)), 16)
                yst[i] = c.val("eY")
                if i + 2 < N64:
                    _ld(s, i + 2)
        block.sync(syncsec)

        def prodsec(t):
            t.wait_ge(c.sem("eV")[0], ev0)
            for i in range(N64):
                dc, fsl = items[i]
                w = q0 + i - 1
                if w > q0:
                    c.wait(t, "cc", w)
                c.inc("pp", t.matmul(psl[i % 2][:, :], m1t[:, 128 * dc:128 * (dc + 1)],
                                     uTb[:, 512 * fsl:512 * (fsl + 1)],
                                     start=True, stop=True))
        block.tensor(prodsec)

        def conssec(v):
            for i in range(N64):
                c.wait(v, "pp", p0 + i + 1)
                c.wait(v, "eY", yld[i])
                if i >= 2:
                    c.wait(v, "eY", yst[i - 2])
                c.inc("cc", v.scalar_tensor_tensor(
                    stg[:, 512 * (i % 2):512 * (i % 2 + 1)],
                    psl[i % 2][:, :], gcol[:, 0:1], wslc[i % 2][:, :],
                    ALU.mult, ALU.add))
        block.vector(conssec)

        def fini(s):
            c.wait(s, "eY")
        block.sync(fini)
    c.pop()

    return dict(wcd=wcd, wpd=wpd)


def seg_l1f_final(c, io, scr):
    nc = c.nc
    c.push()
    with nc.Block() as block:
        sg = Seg(c, block)
        x1t = [c.stack.enter_context(nc.sbuf_tensor(f"x1tF{i}", [128, D], F32))
               for i in range(2)]
        mbt = [c.stack.enter_context(nc.sbuf_tensor(f"mbtF{i}", [128, D], BF16))
               for i in range(2)]
        xnt = [c.stack.enter_context(nc.sbuf_tensor(f"xntF{i}", [128, D], BF16))
               for i in range(2)]
        ff2 = [c.stack.enter_context(nc.sbuf_tensor(f"ff2F{i}", [128, D], BF16))
               for i in range(2)]
        ot = [c.stack.enter_context(nc.sbuf_tensor(f"otF{i}", [128, D], F32))
              for i in range(2)]

        for g in range(NTC):
            b = g % 2

            def ldf(s, fin, g=g, b=b):
                fin(s.dma_start(x1t[b][:, :], scr["s_x1"][g]), 16)
                fin(s.dma_start(ff2[b][:, :], scr["s_ffn2"][g]), 16)
                fin(s.dma_start(mbt[b][:, :], scr["s_mamba"][g]), 16)
                fin(s.dma_start(xnt[b][:, :], scr["s_xn1"][g]), 16)
            sg.step("sync", ["vector"], ldf)

            def fsum(v, fin, g=g, b=b):
                v.tensor_tensor(ot[b][:, :], x1t[b][:, :], ff2[b][:, :], ALU.add)
                v.tensor_tensor(ot[b][:, :], ot[b][:, :], mbt[b][:, :], ALU.add)
                fin(v.tensor_tensor(ot[b][:, :], ot[b][:, :], xnt[b][:, :], ALU.add))
            sg.step("vector", ["sync"], fsum)

            def sto(s, fin, g=g, b=b):
                fin(s.dma_start(io["out"][128 * g:128 * (g + 1), :], ot[b][:, :]), 16)
            sg.step("sync", ["vector"], sto)
        sg.barrier_wait("vector", ["sync"])
    c.pop()


# version 16
# speedup vs baseline: 1.0151x; 1.0151x over previous
import numpy as np
import ml_dtypes
from contextlib import ExitStack

import concourse.bass as bass
import concourse.mybir as mybir
from concourse.bass_utils import run_bass_kernel_spmd

F32 = mybir.dt.float32
BF16 = mybir.dt.bfloat16
U8 = mybir.dt.uint8
AF = mybir.ActivationFunctionType
ALU = mybir.AluOpType
AX = mybir.AxisListType
BF = ml_dtypes.bfloat16

T, D, H, KH = 2048, 1024, 16, 64
NCORE = 8
CH = 128
NCH = T // CH
NDB = D // 128
NTC = T // 128
DI = 2048
NDI = DI // 128
DS = 16
DTR = 64
FF = 4096
NFB = FF // 128
E = 8
TSL = 512
MTS = 1024
LNEPS = 1e-5
GNEPS = 1e-5 * 64.0
RMSEPS = 1e-6

UNFUSED = False
STOP_AFTER = None

SEGS = ("l0a", "moe", "mamba", "l1f")


def host_constants():
    c = {}
    c["c_ident"] = np.eye(128, dtype=np.float32).astype(BF)
    tau = np.arange(128)[:, None]
    tt = np.arange(128)[None, :]
    c["c_Lf"] = -(tau <= tt).astype(np.float32)
    c["c_Lb"] = (tau > tt).astype(np.float32)
    c["c_mask_f"] = (tau <= tt).astype(np.uint8)
    c["c_mask_b"] = (tau > tt).astype(np.uint8)
    return c


def _colpack(v, nb):
    return np.ascontiguousarray(np.asarray(v, np.float32).reshape(nb, 128).T)


def shard_inputs(ins):
    f32 = lambda a: np.ascontiguousarray(np.asarray(a, dtype=np.float32))
    cm = {}
    for nm, key in (("w_Wr", "L0_attn_Wr"), ("w_Wk", "L0_attn_Wk"),
                    ("w_Wv", "L0_attn_Wv"), ("w_Wg", "L0_attn_Wg"),
                    ("w_Wo", "L0_attn_Wo"), ("w_gate", "L0_attn_gate_w"),
                    ("w_maa_w1", "L0_attn_maa_w1"), ("w_decay_w1", "L0_attn_decay_w1"),
                    ("w_decay_w2", "L0_attn_decay_w2")):
        cm[nm] = f32(ins[key])
    cm["w_maa_w2"] = f32(ins["L0_attn_maa_w2"])
    cm["v_td_row"] = f32(ins["L0_attn_time_decay"]).reshape(1, D)
    cm["v_convw"] = _colpack(f32(ins["L0_attn_conv_w"]).reshape(D, 3).T.reshape(3 * D), 3 * NDB) \
        if False else np.ascontiguousarray(
            f32(ins["L0_attn_conv_w"]).reshape(D, 3).reshape(NDB, 128, 3).transpose(1, 0, 2).reshape(128, NDB * 3))
    for nm in ("maa_x", "maa_w", "maa_k", "maa_v", "maa_r", "maa_g"):
        cm["r_" + nm] = f32(ins["L0_attn_" + nm]).reshape(1, D)
    cm["c_ln_w"] = _colpack(ins["L0_attn_ln_w"], NDB)
    cm["c_maa_x"] = _colpack(ins["L0_attn_maa_x"], NDB)
    cm["r_ln_w"] = f32(ins["L0_attn_ln_w"]).reshape(1, D)
    cm["r_ln_b"] = f32(ins["L0_attn_ln_b"]).reshape(1, D)
    cm["r_gate_b"] = f32(ins["L0_attn_gate_b"]).reshape(1, D)
    cm["r_rms_w"] = f32(ins["L0_ffn_rms_w"]).reshape(1, D)
    cm["w_moe_gateT"] = f32(ins["L0_ffn_gate_w"]).T.copy()
    cm["w_moe_W1"] = np.ascontiguousarray(
        f32(ins["L0_ffn_W1"]).reshape(E, D, 4, 1024).transpose(0, 2, 1, 3))
    cm["w_moe_W2"] = np.ascontiguousarray(
        f32(ins["L0_ffn_W2"]).reshape(E, FF, 8, 128).transpose(0, 2, 1, 3))
    cm["c_moe_b1"] = np.ascontiguousarray(
        f32(ins["L0_ffn_b1"]).reshape(E, NFB, 128).transpose(0, 2, 1))
    cm["w_moe_b2"] = f32(ins["L0_ffn_b2"])
    cm["r_l1n_w"] = f32(ins["L1_attn_norm_w"]).reshape(1, D)
    cm["r_l1n_b"] = f32(ins["L1_attn_norm_b"]).reshape(1, D)
    cm["w_WinT"] = f32(ins["L1_attn_W_in"]).T.copy()
    cm["c_mconv_w"] = np.ascontiguousarray(
        f32(ins["L1_attn_conv_w"]).reshape(NDI, 128, 4).transpose(1, 0, 2).reshape(128, NDI * 4))
    cm["c_mconv_b"] = _colpack(ins["L1_attn_conv_b"], NDI)
    cm["w_WxT"] = np.ascontiguousarray(np.pad(f32(ins["L1_attn_W_x"]).T, ((0, 0), (0, 32))))
    cm["w_WdtT"] = f32(ins["L1_attn_W_dt"]).T.copy()
    cm["c_bdt"] = _colpack(ins["L1_attn_b_dt"], NDI)
    cm["c_Alog"] = np.ascontiguousarray(
        f32(ins["L1_attn_A_log"]).reshape(NDI, 128, DS).transpose(1, 0, 2).reshape(128, NDI * DS))
    cm["c_mD"] = _colpack(ins["L1_attn_D"], NDI)
    cm["w_WoutT"] = f32(ins["L1_attn_W_out"]).T.copy()
    cm["w_fb_T"] = f32(ins["L1_ffn_fast_b"]).T.copy()
    cm["w_fc"] = f32(ins["L1_ffn_w_fc"])
    cm["w_fcT"] = f32(ins["L1_ffn_w_fc"]).T.copy()
    cm["w_proj"] = f32(ins["L1_ffn_w_proj"])
    cm["w_projT"] = f32(ins["L1_ffn_w_proj"]).T.copy()
    cm["w_lrv"] = f32(ins["L1_ffn_v"])
    cm["w_lruT"] = f32(ins["L1_ffn_u"]).T.copy()
    cm["v_lrgate"] = f32(ins["L1_ffn_gate"]).reshape(1, 1)
    cm.update(host_constants())
    x = f32(ins["L0_attn_x"])
    return [dict(cm, x0=x[b]) for b in range(NCORE)]


IN_SPECS = {
    "x0": ([T, D], F32),
    "w_Wr": ([D, D], F32), "w_Wk": ([D, D], F32), "w_Wv": ([D, D], F32),
    "w_Wg": ([D, D], F32), "w_Wo": ([D, D], F32), "w_gate": ([D, D], F32),
    "w_maa_w1": ([D, 160], F32), "w_maa_w2": ([5, 32, D], F32),
    "w_decay_w1": ([D, 64], F32), "w_decay_w2": ([64, D], F32),
    "v_td_row": ([1, D], F32), "v_convw": ([128, NDB * 3], F32),
    "r_maa_x": ([1, D], F32), "r_maa_w": ([1, D], F32), "r_maa_k": ([1, D], F32),
    "r_maa_v": ([1, D], F32), "r_maa_r": ([1, D], F32), "r_maa_g": ([1, D], F32),
    "c_ln_w": ([128, NDB], F32), "c_maa_x": ([128, NDB], F32), "r_ln_w": ([1, D], F32), "r_ln_b": ([1, D], F32),
    "r_gate_b": ([1, D], F32),
    "r_rms_w": ([1, D], F32), "w_moe_gateT": ([D, E], F32),
    "w_moe_W1": ([E, 4, D, 1024], F32), "w_moe_W2": ([E, 8, FF, 128], F32),
    "c_moe_b1": ([E, 128, NFB], F32), "w_moe_b2": ([E, D], F32),
    "r_l1n_w": ([1, D], F32), "r_l1n_b": ([1, D], F32),
    "w_WinT": ([D, 2 * DI], F32), "c_mconv_w": ([128, NDI * 4], F32),
    "c_mconv_b": ([128, NDI], F32),
    "w_WxT": ([DI, 128], F32), "w_WdtT": ([DTR, DI], F32),
    "c_bdt": ([128, NDI], F32), "c_Alog": ([128, NDI * DS], F32),
    "c_mD": ([128, NDI], F32), "w_WoutT": ([DI, D], F32),
    "w_fb_T": ([16, 16], F32), "w_fc": ([FF, D], F32), "w_fcT": ([D, FF], F32),
    "w_proj": ([D, FF], F32), "w_projT": ([FF, D], F32),
    "w_lrv": ([16, D], F32), "w_lruT": ([16, FF], F32), "v_lrgate": ([1, 1], F32),
    "c_ident": ([128, 128], BF16), "c_Lf": ([128, 128], F32), "c_Lb": ([128, 128], F32),
    "c_mask_f": ([128, 128], U8), "c_mask_b": ([128, 128], U8),
}

SCRATCH = {
    "s_xn0tm": ([NTC, 128, D], BF16),
    "s_r": ([NCH, 128, D], BF16),
    "s_k": ([NCH, 128, D], BF16),
    "s_v": ([NTC, 128, D], BF16),
    "s_g": ([NTC, 128, D], BF16),
    "s_sig": ([NTC, 128, D], BF16),
    "s_attn": ([NTC, 128, D], BF16),
    "s_ffn": ([NTC, 128, D], BF16),
    "s_x1": ([NTC, 128, D], F32),
    "s_xn1": ([NTC, 128, D], BF16),
    "s_xn1f": ([NDB, 128, T], BF16),
    "s_xmf": ([NDB, 128, T], BF16),
    "s_u": ([NDI, 128, T], BF16),
    "s_z": ([NDI, 128, T], BF16),
    "s_delta": ([NDI, 128, T], BF16),
    "s_du": ([NDI, 128, T], BF16),
    "s_yac": ([NDI, 128, T], BF16),
    "s_mamba": ([NTC, 128, D], BF16),
    "s_ffn2": ([NTC, 128, D], BF16),
}

SEG_IO = {
    "l0a": ([], ["s_xn0tm", "s_attn", "s_sig", "s_g", "s_r", "s_k", "s_v"]),
    "moe": (["s_xn0tm", "s_attn"], ["s_ffn", "s_x1", "s_xn1", "s_xn1f", "s_xmf"]),
    "mamba": (["s_xmf"], ["s_u", "s_z", "s_delta", "s_du", "s_yac", "s_mamba"]),
    "l1f": (["s_xn1f", "s_xn1", "s_x1", "s_mamba"], []),
}


class Ctx:
    def __init__(self, nc):
        self.nc = nc
        self.stack = ExitStack()
        self.ostack = self.stack
        self._sems = {}

    def init_sems(self):
        for n in ("pp", "cc", "qq", "dd", "eT", "eV", "eS", "eG", "eY"):
            self.sem(n)

    def sem(self, name):
        if name not in self._sems:
            h = self.ostack.enter_context(self.nc.semaphore(name))
            self._sems[name] = [h, 0]
        return self._sems[name]

    def uname(self, base):
        if not hasattr(self, "_uc"):
            self._uc = 0
        self._uc += 1
        return f"{base}_{self._uc}"

    def setup_consts(self, vals=(-1.0, -0.5, LNEPS, GNEPS, RMSEPS)):
        nc = self.nc
        tile = self.ostack.enter_context(
            nc.sbuf_tensor("cstpool", [128, len(vals)], F32))
        with nc.Block() as block:
            def ms(v):
                last = None
                for i, val in enumerate(vals):
                    last = v.memset(tile[:, i:i + 1], float(val))
                self.inc("eV", last)
            block.vector(ms)
        for i, val in enumerate(vals):
            nc.const_aps.register(F32, float(val), tile[:, i:i + 1])

    def push(self):
        if not hasattr(self, "_saved"):
            self._saved = []
        self._saved.append(self.stack)
        self.stack = ExitStack()

    def pop(self):
        self.stack.close()
        self.stack = self._saved.pop()

    def scope(self):
        from contextlib import contextmanager

        @contextmanager
        def _s():
            old = self.stack
            self.stack = ExitStack()
            try:
                with self.stack:
                    yield
            finally:
                self.stack = old
        return _s()

    def inc(self, name, instr, k=1):
        s = self.sem(name)
        instr.then_inc(s[0], k)
        s[1] += k
        return instr

    def val(self, name):
        return self.sem(name)[1]

    def wait(self, eng, name, val=None):
        s = self.sem(name)
        eng.wait_ge(s[0], s[1] if val is None else val)


ENG_SEM = {"tensor": "eT", "vector": "eV", "scalar": "eS", "gpsimd": "eG",
           "sync": "eY"}


def _nop(e):
    return e.engine_nop() if hasattr(e, "engine_nop") else e.nop()


def ld3(dst2d, src2d, p=128):
    j = src2d.shape[0] // p
    return (dst2d.rearrange("p (j c) -> p j c", j=j),
            src2d.rearrange("(j p) c -> p j c", p=p))


class Seg:

    def __init__(self, c, block):
        self.c = c
        self.block = block

    def step(self, eng, deps, fn):
        c = self.c
        my = ENG_SEM[eng]
        snap = [(ENG_SEM[d], c.val(ENG_SEM[d])) for d in deps if c.val(ENG_SEM[d]) > 0]

        def body(e):
            for s, v in snap:
                e.wait_ge(c.sem(s)[0], v)
            fn(e, lambda instr, k=1: c.inc(my, instr, k))

        getattr(self.block, eng)(body)

    def barrier_wait(self, eng, engs=None):
        c = self.c

        def body(e):
            for d in (engs or ENG_SEM):
                if d == eng:
                    continue
                v = c.val(ENG_SEM[d])
                if v > 0:
                    e.wait_ge(c.sem(ENG_SEM[d])[0], v)

        getattr(self.block, eng)(body)


def pipe(sg, n, produce, consume, nslots, peng="tensor", ceng="vector", pwait=None):
    c = sg.c
    p0 = c.val("pp")
    q0 = c.val("cc")

    def _p(e):
        if pwait:
            for nm_, vv_ in pwait:
                e.wait_ge(c.sem(nm_)[0], vv_)
        for i in range(n):
            w = q0 + i - nslots + 1
            if w > 0:
                c.wait(e, "cc", w)
            c.inc("pp", produce(e, i, i % nslots))
        c.wait(e, "pp")
        c.inc(ENG_SEM[peng], _nop(e))

    def _c(e):
        for i in range(n):
            c.wait(e, "pp", p0 + i + 1)
            c.inc("cc", consume(e, i, i % nslots))
        c.wait(e, "cc")
        c.inc(ENG_SEM[ceng], _nop(e))

    getattr(sg.block, peng)(_p)
    getattr(sg.block, ceng)(_c)


def ln_rows_tm(sg, x_tile, out_tile, tmp, eps, extra=None):
    raise NotImplementedError


def seg_l0a(c, io, scr):
    nc = c.nc
    ident = c.stack.enter_context(nc.sbuf_tensor("ident", [128, 128], BF16))
    maskf = c.stack.enter_context(nc.sbuf_tensor("maskf", [128, 128], U8))
    maskb = c.stack.enter_context(nc.sbuf_tensor("maskb", [128, 128], U8))
    Lf = c.stack.enter_context(nc.sbuf_tensor("Lf", [128, 128], F32))
    Lb = c.stack.enter_context(nc.sbuf_tensor("Lb", [128, 128], F32))
    convw = c.stack.enter_context(nc.sbuf_tensor("convw", [128, NDB * 3], F32))
    cmaax = c.stack.enter_context(nc.sbuf_tensor("cmaax", [128, NDB], F32))
    ones1 = c.stack.enter_context(nc.sbuf_tensor("ones1", [1, 128], F32))
    ones1b = c.stack.enter_context(nc.sbuf_tensor("ones1b", [1, 128], BF16))
    rows = {}
    for nm in ("r_maa_x", "r_maa_w", "r_maa_k", "r_maa_v", "r_maa_r", "r_maa_g",
               "r_gate_b", "r_ln_w", "r_ln_b"):
        rows[nm] = c.stack.enter_context(nc.sbuf_tensor(nm + "_sb", [1, D], BF16))
    rows["v_td_row"] = c.stack.enter_context(nc.sbuf_tensor("v_td_row_sb", [1, D], F32))
    dw1 = c.stack.enter_context(nc.sbuf_tensor("dw1", [128, NDB * 64], BF16))
    dw2 = c.stack.enter_context(nc.sbuf_tensor("dw2", [64, D], BF16))
    mw1 = c.stack.enter_context(nc.sbuf_tensor("mw1", [128, NDB * 160], BF16))
    w2aug = c.stack.enter_context(nc.sbuf_tensor("w2aug", [33, D], BF16))

    dxf = c.stack.enter_context(nc.sbuf_tensor("dxf", [128, NDB * T], BF16))
    y_tm = dxf
    wbuf = c.stack.enter_context(nc.sbuf_tensor("wbuf", [128, NDB * D], BF16))
    tanhw = c.stack.enter_context(nc.sbuf_tensor("tanhw", [64, T], BF16))
    tdbc = c.stack.enter_context(nc.sbuf_tensor("tdbc", [128, D], F32))
    gbbc = c.stack.enter_context(nc.sbuf_tensor("gbbc", [128, D], F32))
    lnwbc = c.stack.enter_context(nc.sbuf_tensor("lnwbc", [128, D], BF16))
    lnbbc = c.stack.enter_context(nc.sbuf_tensor("lnbbc", [128, D], BF16))
    stage = c.stack.enter_context(nc.sbuf_tensor("stage", [128, 2 * D], BF16))

    with nc.Block() as block:
        sg = Seg(c, block)

        def ld(s, fin):
            fin(s.dma_start(ident[:, :], io["c_ident"]), 16)
            fin(s.dma_start(maskf[:, :], io["c_mask_f"]), 16)
            fin(s.dma_start(maskb[:, :], io["c_mask_b"]), 16)
            fin(s.dma_start(Lf[:, :], io["c_Lf"]), 16)
            fin(s.dma_start(Lb[:, :], io["c_Lb"]), 16)
            fin(s.dma_start(convw[:, :], io["v_convw"]), 16)
            fin(s.dma_start(cmaax[:, :], io["c_maa_x"]), 16)
            fin(s.dma_start(rows["v_td_row"][:, :], io["v_td_row"]), 16)
        sg.step("sync", [], ld)

        def ldg(g, fin):
            for nm in rows:
                if nm != "v_td_row":
                    fin(g.dma_start(rows[nm][:, :], io[nm]), 16)
            fin(g.dma_start(*ld3(dw1[:, :], io["w_decay_w1"])), 16)
            fin(g.dma_start(dw2[:, :], io["w_decay_w2"]), 16)
            fin(g.dma_start(*ld3(mw1[:, :], io["w_maa_w1"])), 16)
        sg.step("gpsimd", [], ldg)

        def ms(v, fin):
            fin(v.memset(ones1[:, :], 1.0))
            fin(v.memset(ones1b[:, :], 1.0))
        sg.step("vector", [], ms)

    c.push()
    xn0f = c.stack.enter_context(nc.sbuf_tensor("xn0f", [128, NDB * T], BF16))
    xxxf = c.stack.enter_context(nc.sbuf_tensor("xxxf", [128, NDB * T], BF16))
    xff = xn0f
    t5 = [c.stack.enter_context(nc.sbuf_tensor(f"t5_{f}", [33, T], BF16)) for f in range(5)]
    import os
    if os.environ.get("NOMS5") != "1":
        with nc.Block() as block:
            sg = Seg(c, block)

            def ms5(v, fin):
                last = None
                for f in range(5):
                    last = v.memset(t5[f][32:33, :], 1.0)
                fin(last)
            sg.step("vector", [], ms5)

    with c.scope(), nc.Block() as block:
        sg = Seg(c, block)
        xt = [c.stack.enter_context(nc.sbuf_tensor(f"xt{i}", [128, D], F32)) for i in range(2)]
        xnt = [c.stack.enter_context(nc.sbuf_tensor(f"xnt{i}", [128, D], BF16)) for i in range(2)]
        sqt = c.stack.enter_context(nc.sbuf_tensor("sqt", [128, D], BF16))
        col = c.stack.enter_context(nc.sbuf_tensor("col", [128, 8], F32))
        psT = [c.ps(f"psTB{i}", [128, 128], BF16) if False else
               c.stack.enter_context(nc.psum_tensor(f"psTB{i}", [128, 128], BF16)) for i in range(2)]

        for g in range(NTC):
            b = g % 2

            def ldx(s, fin, g=g, b=b):
                fin(s.dma_start(xt[b][:, :], io["x0"][128 * g:128 * (g + 1), :]), 16)
            sg.step("sync", ["vector", "scalar"], ldx)

            def sq_(s, fin, b=b):
                fin(s.activation(sqt[:, :], xt[b][:, :], AF.Square))
            sg.step("scalar", ["sync"], sq_)

            def stats(v, fin, b=b):
                v.tensor_reduce(col[:, 0:1], xt[b][:, :], AX.X, ALU.add)
                v.tensor_reduce(col[:, 1:2], sqt[:, :], AX.X, ALU.add)
                v.drain()
                v.tensor_scalar(col[:, 0:1], col[:, 0:1], 1.0 / D, None, ALU.mult)
                v.tensor_scalar(col[:, 1:2], col[:, 1:2], 1.0 / D, None, ALU.mult)
                v.drain()
                v.tensor_tensor(col[:, 2:3], col[:, 0:1], col[:, 0:1], ALU.mult)
                v.drain()
                fin(v.tensor_tensor(col[:, 2:3], col[:, 1:2], col[:, 2:3], ALU.subtract))
            sg.step("vector", ["scalar"], stats)

            def rstd(s, fin):
                s.activation(col[:, 3:4], col[:, 2:3], AF.Ln, bias=LNEPS)
                s.drain()
                fin(s.activation(col[:, 3:4], col[:, 3:4], AF.Exp, scale=-0.5))
            sg.step("scalar", ["vector"], rstd)

            def apply(v, fin, b=b):
                v.scalar_tensor_tensor(col[:, 4:5], col[:, 0:1], -1.0, col[:, 3:4],
                                       ALU.mult, ALU.mult)
                v.drain()
                fin(v.tensor_scalar(xnt[b][:, :], xt[b][:, :], col[:, 3:4], col[:, 4:5],
                                    ALU.mult, ALU.add))
            sg.step("vector", ["scalar", "tensor"], apply)

            def st(s, fin, g=g, b=b):
                fin(s.dma_start(scr["s_xn0tm"][g], xnt[b][:, :]), 16)
            sg.step("sync", ["vector"], st)

            def produce(t, i, sl, g=g, b=b):
                return t.transpose(psT[sl][:, :], xnt[b][:, 128 * i:128 * (i + 1)], ident[:, :])

            def consume(v, i, sl, g=g):
                return v.tensor_copy(xn0f[:, T * i + 128 * g:T * i + 128 * (g + 1)], psT[sl][:, :])

            sg.barrier_wait("tensor", ["vector", "sync"])
            pipe(sg, NDB, produce, consume, 2)

    if STOP_AFTER == "B":
        c.pop()
        return
    with nc.Block() as block:
        sg = Seg(c, block)

        def conv(v, fin):
            J = [(xn0f[:, T * j:T * (j + 1)], dxf[:, T * j:T * (j + 1)],
                  convw[:, 3 * j:3 * j + 1], convw[:, 3 * j + 1:3 * j + 2],
                  convw[:, 3 * j + 2:3 * j + 3], j) for j in range(NDB)]
            for x, dx, w0, w1, w2, j in J:
                v.tensor_scalar(dx, x, w1, None, ALU.mult)
            for x, dx, w0, w1, w2, j in J:
                v.tensor_tensor(dx, dx, x, ALU.subtract)
            for x, dx, w0, w1, w2, j in J:
                v.scalar_tensor_tensor(dx[:, 1:T], x[:, 0:T - 1], w0, dx[:, 1:T],
                                       ALU.mult, ALU.add)
            for x, dx, w0, w1, w2, j in J:
                v.scalar_tensor_tensor(dx[:, 0:T - 1], x[:, 1:T], w2, dx[:, 0:T - 1],
                                       ALU.mult, ALU.add)
            last = None
            for x, dx, w0, w1, w2, j in J:
                last = v.scalar_tensor_tensor(xxxf[:, T * j:T * (j + 1)], dx,
                                              cmaax[:, j:j + 1], x, ALU.mult, ALU.add)
            fin(last)
        sg.step("vector", [], conv)

    if STOP_AFTER == "C":
        c.pop()
        return
    with c.scope(), nc.Block() as block:
        sg = Seg(c, block)
        ps5 = [c.stack.enter_context(nc.psum_tensor(f"ps5_{i}", [128, 512], F32)) for i in range(2)]
        items = [(f, sl) for f in range(5) for sl in range(T // 512)]

        def produce(t, i, s):
            f, sl = items[i]
            last = None
            for jj in range(NDB):
                last = t.matmul(ps5[s][0:32, :],
                                mw1[:, 160 * jj + 32 * f:160 * jj + 32 * (f + 1)],
                                xxxf[:, T * jj + 512 * sl:T * jj + 512 * (sl + 1)],
                                start=(jj == 0), stop=(jj == NDB - 1))
            return last

        def consume(s, i, sl_):
            f, sl = items[i]
            return s.activation(t5[f][0:32, 512 * sl:512 * (sl + 1)], ps5[sl_][0:32, :], AF.Tanh)

        sg.barrier_wait("tensor", ["vector"])
        pipe(sg, len(items), produce, consume, 2, ceng="scalar")

    if STOP_AFTER == "D":
        c.pop()
        return
    fnames = [("r_maa_w", "w"), ("r_maa_k", "k"), ("r_maa_v", "v"),
              ("r_maa_r", "r"), ("r_maa_g", "g")]
    drowb = c.stack.enter_context(nc.sbuf_tensor("drowb", [1, D], BF16))
    for fi, (rname, kind) in enumerate(fnames):
        with c.scope(), nc.Block() as block:
            sg = Seg(c, block)

            def mkaug(v, fin, rname=rname, fi=fi):
                fin(v.tensor_tensor(drowb[:, :], rows[rname][:, :],
                                    rows["r_maa_x"][:, :], ALU.subtract))
            sg.step("vector", [], mkaug)

            def ldw2(g, fin, fi=fi):
                fin(g.dma_start(w2aug[0:32, :], io["w_maa_w2"][fi]), 16)
            sg.step("gpsimd", ["tensor"], ldw2)

            def cpaug(v, fin):
                v.drain()
                fin(v.tensor_copy(w2aug[32:33, :], drowb[:, :]))
            sg.step("vector", ["gpsimd"], cpaug)

            psm = [c.stack.enter_context(nc.psum_tensor(c.uname(f"psm{fi}_{i}"), [128, 512], F32))
                   for i in range(2)]
            items = [(j, sl) for j in range(NDB) for sl in range(T // 512)]

            def produce(t, i, s, fi=fi):
                j, sl = items[i]
                return t.matmul(psm[s][:, :], w2aug[:, 128 * j:128 * (j + 1)],
                                t5[fi][:, 512 * sl:512 * (sl + 1)], start=True, stop=True)

            def consume(v, i, s):
                j, sl = items[i]
                cols = slice(T * j + 512 * sl, T * j + 512 * (sl + 1))
                v.tensor_tensor(xff[:, cols], psm[s][:, :], dxf[:, cols], ALU.mult)
                return v.tensor_tensor(xff[:, cols], xff[:, cols], xxxf[:, cols], ALU.add)

            sg.barrier_wait("tensor", ["vector", "gpsimd"])
            pipe(sg, len(items), produce, consume, 2)

        emit_l0_proj(c, io, scr, kind, xff, wbuf, tanhw, dw1, ident, stage)

    c.pop()
    if STOP_AFTER == "E":
        return

    emit_bcast_row(c, ones1b, rows["r_gate_b"], gbbc)
    emit_l0_proj(c, io, scr, "sig", dxf, wbuf, tanhw, dw1, ident, stage, gbbc=gbbc)
    emit_bcast_row(c, ones1, rows["v_td_row"], tdbc)
    emit_bcast_row(c, ones1b, rows["r_ln_w"], lnwbc)
    emit_bcast_row(c, ones1b, rows["r_ln_b"], lnbbc)

    if STOP_AFTER == "F":
        return
    emit_rwkv_sweeps(c, io, scr, tanhw, dw2, tdbc, Lf, Lb, maskf, maskb, ident, y_tm)

    if STOP_AFTER == "G":
        return
    emit_l0_post(c, io, scr, y_tm, lnwbc, lnbbc, ident, wbuf, stage)


def pipe3(sg, n, produce, consume, store, nps, nst, peng="tensor", ceng="vector"):
    c = sg.c
    p0 = c.val("pp")
    q0 = c.val("cc")
    d0 = c.val("dd")

    def _p(e):
        for i in range(n):
            w = q0 + i - nps + 1
            if w > 0:
                c.wait(e, "cc", w)
            c.inc("pp", produce(e, i, i % nps))
        c.wait(e, "pp")
        c.inc(ENG_SEM[peng], _nop(e))

    def _c(e):
        for i in range(n):
            c.wait(e, "pp", p0 + i + 1)
            w = d0 + 16 * (i - nst + 1)
            if w > 0:
                c.wait(e, "dd", w)
            c.inc("cc", consume(e, i, i % nps, i % nst))
        c.wait(e, "cc")
        c.inc(ENG_SEM[ceng], _nop(e))

    def _s(e):
        for i in range(n):
            c.wait(e, "cc", q0 + i + 1)
            c.inc("dd", store(e, i, i % nst), 16)
        c.wait(e, "dd")
        c.inc(ENG_SEM["sync"], _nop(e))

    getattr(sg.block, peng)(_p)
    getattr(sg.block, ceng)(_c)
    sg.block.sync(_s)


def emit_bcast_row(c, ones1, src_row, dst, f32rhs=True):
    nc = c.nc
    W = src_row.shape[1] if hasattr(src_row, "shape") else D
    with c.scope(), nc.Block() as block:
        sg = Seg(c, block)
        psb = c.stack.enter_context(nc.psum_tensor(c.uname("psbc"), [128, 512], F32))

        for sl in range(W // 512):
            def mm(t, fin, sl=sl):
                fin(t.matmul(psb[:, :], ones1[:, :], src_row[0:1, 512 * sl:512 * (sl + 1)],
                             start=True, stop=True))
            sg.step("tensor", ["vector"], mm)

            def ev(v, fin, sl=sl):
                fin(v.tensor_copy(dst[:, 512 * sl:512 * (sl + 1)], psb[:, :]))
            sg.step("vector", ["tensor"], ev)


def emit_l0_proj(c, io, scr, kind, src_fm, wbuf, tanhw, dw1, ident, stage, gbbc=None):
    nc = c.nc
    wsrc = {"k": "w_Wk", "r": "w_Wr", "v": "w_Wv", "g": "w_Wg", "sig": "w_gate"}
    with c.scope(), nc.Block() as block:
        sg = Seg(c, block)
        if kind != "w":
            def ldw(g, fin):
                fin(g.dma_start(*ld3(wbuf[:, :], io[wsrc[kind]])), 16)
            sg.step("gpsimd", ["tensor"], ldw)
            sg.barrier_wait("tensor", ["gpsimd"])

        if kind == "w":
            ps = [c.stack.enter_context(nc.psum_tensor(c.uname(f"psw{i}"), [128, 512], F32))
                  for i in range(2)]

            def produce(t, i, s):
                last = None
                for jj in range(NDB):
                    last = t.matmul(ps[s][0:64, :], dw1[:, 64 * jj:64 * (jj + 1)],
                                    src_fm[:, T * jj + 512 * i:T * jj + 512 * (i + 1)],
                                    start=(jj == 0), stop=(jj == NDB - 1))
                return last

            def consume(s, i, sl):
                return s.activation(tanhw[:, 512 * i:512 * (i + 1)], ps[sl][0:64, :], AF.Tanh)

            pipe(sg, T // 512, produce, consume, 2, ceng="scalar")
            return

        ps = [c.stack.enter_context(nc.psum_tensor(c.uname(f"psp_{kind}{i}"), [128, 512], F32))
              for i in range(2)]
        tmpf = c.stack.enter_context(nc.sbuf_tensor(c.uname(f"tmpf_{kind}"), [128, 1024], F32)) \
            if kind == "sig" else None

        if kind in ("k", "r"):
            items = [(j, sl) for j in range(NDB) for sl in range(T // 512)]

            def produce(t, i, s):
                j, sl = items[i]
                last = None
                for jj in range(NDB):
                    last = t.matmul(ps[s][:, :],
                                    wbuf[:, D * jj + 128 * j:D * jj + 128 * (j + 1)],
                                    src_fm[:, T * jj + 512 * sl:T * jj + 512 * (sl + 1)],
                                    start=(jj == 0), stop=(jj == NDB - 1))
                return last

            def consume(v, i, s, st):
                return v.tensor_copy(stage[:, 1024 * st:1024 * st + 512], ps[s][:, :])

            def store(y, i, st):
                j, sl = items[i]
                dst = scr["s_" + kind][4 * sl:4 * (sl + 1), :, 128 * j:128 * (j + 1)] \
                    .rearrange("gg p tt -> p gg tt")
                return y.dma_start(dst, stage[:, 1024 * st:1024 * st + 512]
                                   .rearrange("p (gg tt) -> p gg tt", gg=4))

            pipe3(sg, len(items), produce, consume, store, 2, 2)
        else:
            items = [(g, dsl) for g in range(NTC) for dsl in range(2)]
            dstn = {"v": "s_v", "g": "s_g", "sig": "s_sig"}[kind]

            def produce(t, i, s):
                g, dsl = items[i]
                last = None
                for jj in range(NDB):
                    last = t.matmul(ps[s][:, :],
                                    src_fm[:, T * jj + 128 * g:T * jj + 128 * (g + 1)],
                                    wbuf[:, D * jj + 512 * dsl:D * jj + 512 * (dsl + 1)],
                                    start=(jj == 0), stop=(jj == NDB - 1))
                return last

            if kind == "g":
                def consume(s, i, sl, st):
                    return s.activation(stage[:, 1024 * st:1024 * st + 512], ps[sl][:, :],
                                        AF.Silu)
                ceng = "scalar"
            elif kind == "sig":
                def consume(v, i, sl, st):
                    g, dsl = items[i]
                    v.tensor_tensor(tmpf[:, :], ps[sl][:, :],
                                    gbbc[:, 512 * dsl:512 * (dsl + 1)], ALU.add)
                    return v.tensor_copy(stage[:, 1024 * st:1024 * st + 512], tmpf[:, :])
                ceng = "vector"
            else:
                def consume(v, i, sl, st):
                    return v.tensor_copy(stage[:, 1024 * st:1024 * st + 512], ps[sl][:, :])
                ceng = "vector"

            def store(y, i, st):
                g, dsl = items[i]
                return y.dma_start(scr[dstn][g][:, 512 * dsl:512 * (dsl + 1)],
                                   stage[:, 1024 * st:1024 * st + 512])

            if kind == "sig":
                p0 = c.val("pp"); q0 = c.val("cc"); r0 = c.val("qq"); d0 = c.val("dd")

                def _p(t):
                    for i in range(len(items)):
                        if i >= 2:
                            c.wait(t, "cc", q0 + i - 1)
                        c.inc("pp", produce(t, i, i % 2))
                    c.wait(t, "pp")
                    c.inc("eT", _nop(t))

                def _v(v):
                    for i in range(len(items)):
                        c.wait(v, "pp", p0 + i + 1)
                        if i >= 2:
                            c.wait(v, "qq", r0 + i - 1)
                        g, dsl = items[i]
                        c.inc("cc", v.tensor_tensor(
                            tmpf[:, 512 * (i % 2):512 * (i % 2) + 512],
                            ps[i % 2][:, :],
                            gbbc[:, 512 * dsl:512 * (dsl + 1)], ALU.add))
                    c.wait(v, "cc")
                    c.inc("eV", _nop(v))

                def _s(s):
                    for i in range(len(items)):
                        c.wait(s, "cc", q0 + i + 1)
                        if i >= 2:
                            c.wait(s, "dd", d0 + 16 * (i - 1))
                        c.inc("qq", s.activation(
                            stage[:, 1024 * (i % 2):1024 * (i % 2) + 512],
                            tmpf[:, 512 * (i % 2):512 * (i % 2) + 512], AF.Sigmoid))
                    c.wait(s, "qq")
                    c.inc("eS", _nop(s))

                def _y(y):
                    for i in range(len(items)):
                        c.wait(y, "qq", r0 + i + 1)
                        c.inc("dd", store(y, i, i % 2), 16)
                    c.wait(y, "dd")
                    c.inc("eY", _nop(y), 16)

                sg.block.tensor(_p)
                sg.block.vector(_v)
                sg.block.scalar(_s)
                sg.block.sync(_y)
            else:
                pipe3(sg, len(items), produce, consume, store, 2, 2, ceng=ceng)


def emit_rwkv_sweeps(c, io, scr, tanhw, dw2, tdbc, Lf, Lb, maskf, maskb, ident, y_tm):
    nc = c.nc
    c.push()
    sb = lambda n, shp, dt: c.stack.enter_context(nc.sbuf_tensor(n, shp, dt))
    Sp = sb("Sp", [128, D], BF16)
    totcol = sb("totcol", [128, NCH * NDB], F32)
    rch = sb("rch", [128, D], BF16)
    kch = sb("kch", [128, D], BF16)
    vch = sb("vch", [128, D], BF16)
    wch = sb("wch", [128, D], F32)
    ewt = sb("ewt", [128, D], F32)
    epm = sb("epm", [128, D], BF16)
    emm = sb("emm", [128, D], BF16)
    Epf = sb("Epf", [128, D], BF16)
    Emf = sb("Emf", [128, D], BF16)
    rt = sb("rt", [128, D], BF16)
    kt = sb("kt", [128, D], BF16)
    ktm = sb("ktm", [128, D], BF16)
    Mf = sb("Mf", [128, H * 128], BF16)
    zerot = sb("zerot", [128, 128], F32)
    stmp = sb("stmp", [128, D], BF16)

    for dirn in ("f", "b"):
        with c.scope(), nc.Block() as block:
            sg = Seg(c, block)
            ps_a = c.stack.enter_context(nc.psum_tensor(f"psa{dirn}", [128, D], F32))
            ps_y = [c.stack.enter_context(nc.psum_tensor(f"psy{dirn}{i}", [128, 512], F32))
                    for i in range(2)]
            ps_A = [c.stack.enter_context(nc.psum_tensor(f"psA{dirn}{i}", [128, 128], F32))
                    for i in range(2)]
            ps_t = [c.stack.enter_context(nc.psum_tensor(f"pst{dirn}{i}", [128, 128], BF16))
                    for i in range(2)]
            L = Lf if dirn == "f" else Lb
            msk = maskf if dirn == "f" else maskb
            chunks = list(range(NCH)) if dirn == "f" else list(range(NCH - 1, -1, -1))

            def init(v, fin):
                fin(v.memset(Sp[:, :], 0.0))
                fin(v.memset(zerot[:, :], 0.0))
            sg.step("vector", [], init)

            for g in chunks:
                def ld(s, fin, g=g):
                    fin(s.dma_start(rch[:, :], scr["s_r"][g]), 16)
                    fin(s.dma_start(kch[:, :], scr["s_k"][g]), 16)
                    fin(s.dma_start(vch[:, :], scr["s_v"][g]), 16)
                sg.step("sync", ["tensor", "vector"], ld)

                def wmm(t, fin, g=g):
                    t.matmul(ps_a[:, 0:512], tanhw[:, 128 * g:128 * (g + 1)],
                             dw2[:, 0:512], start=True, stop=True)
                    fin(t.matmul(ps_a[:, 512:1024], tanhw[:, 128 * g:128 * (g + 1)],
                                 dw2[:, 512:1024], start=True, stop=True))
                sg.step("tensor", ["vector", "scalar"], wmm)

                def wadd(v, fin):
                    fin(v.tensor_tensor(wch[:, :], ps_a[:, :], tdbc[:, :], ALU.add))
                sg.step("vector", ["tensor"], wadd)

                def expw(s, fin):
                    fin(s.activation(ewt[:, :], wch[:, :], AF.Exp))
                sg.step("scalar", ["vector"], expw)

                def cums(t, fin):
                    t.matmul(ps_a[:, 0:512], L[:, :], ewt[:, 0:512], start=True, stop=True)
                    fin(t.matmul(ps_a[:, 512:1024], L[:, :], ewt[:, 512:1024],
                                 start=True, stop=True))
                sg.step("tensor", ["scalar", "vector"], cums)

                def exps(s, fin):
                    s.activation(epm[:, :], ps_a[:, :], AF.Exp)
                    fin(s.activation(emm[:, :], ps_a[:, :], AF.Exp, scale=-1.0))
                sg.step("scalar", ["tensor"], exps)

                sg.barrier_wait("tensor", ["scalar", "vector"])

                def tprod(t, i, sl):
                    srct = epm if i < NDB else emm
                    j = i % NDB
                    return t.transpose(ps_t[sl][:, :], srct[:, 128 * j:128 * (j + 1)],
                                       ident[:, :])

                def tcons(v, i, sl):
                    dstt = Epf if i < NDB else Emf
                    j = i % NDB
                    return v.tensor_copy(dstt[:, 128 * j:128 * (j + 1)], ps_t[sl][:, :])

                pipe(sg, 2 * NDB, tprod, tcons, 2)

                def scale(v, fin, g=g, dirn=dirn):
                    if dirn == "f":
                        v.tensor_tensor(rt[:, :], rch[:, :], Epf[:, :], ALU.mult)
                        v.tensor_tensor(kt[:, :], kch[:, :], Emf[:, :], ALU.mult)
                        fin(v.tensor_copy(
                            totcol[:, NDB * g:NDB * (g + 1)].rearrange(
                                "p (j o) -> p j o", o=1),
                            Epf[:, :].rearrange("p (j t) -> p j t", t=128)[:, :, 127:128]))
                    else:
                        v.tensor_tensor(rt[:, :], rch[:, :], Emf[:, :], ALU.mult)
                        fin(v.tensor_tensor(kt[:, :], kch[:, :], Epf[:, :], ALU.mult))
                sg.step("vector", ["sync"], scale)

                sg.barrier_wait("tensor", ["vector"])

                def kprod(t, i, sl):
                    return t.transpose(ps_t[sl][:, :], kt[:, 128 * i:128 * (i + 1)],
                                       ident[:, :])

                def kcons(v, i, sl):
                    return v.tensor_copy(ktm[:, 128 * i:128 * (i + 1)], ps_t[sl][:, :])

                pipe(sg, NDB, kprod, kcons, 2)

                sg.barrier_wait("tensor", ["vector"])

                def aprod(t, i, sl):
                    hh = i
                    j, half = hh // 2, 64 * (hh % 2)
                    return t.matmul(ps_A[sl][:, :],
                                    kt[half:half + 64, 128 * j:128 * (j + 1)],
                                    rt[half:half + 64, 128 * j:128 * (j + 1)],
                                    start=True, stop=True)

                def acons(v, i, sl):
                    return v.select(Mf[:, 128 * i:128 * (i + 1)], msk[:, :],
                                    ps_A[sl][:, :], zerot[:, :])

                pipe(sg, H, aprod, acons, 2)

                def ymm(t, fin, g=g):
                    for half in range(2):
                        for hh in range(8 * half, 8 * half + 8):
                            t.matmul(ps_y[half][:, 64 * hh - 512 * half:
                                               64 * (hh + 1) - 512 * half],
                                     Mf[:, 128 * hh:128 * (hh + 1)],
                                     vch[:, 64 * hh:64 * (hh + 1)],
                                     start=(hh % 8 == 0), stop=False)
                        for pj in range(4 * half, 4 * half + 4):
                            last = t.matmul(ps_y[half][:, 128 * pj - 512 * half:
                                                       128 * (pj + 1) - 512 * half],
                                            rt[:, 128 * pj:128 * (pj + 1)],
                                            Sp[:, 128 * pj:128 * (pj + 1)],
                                            start=False, stop=(pj % 4 == 3))
                    fin(last)
                sg.step("tensor", ["vector"], ymm)

                def simm(t, fin, g=g):
                    for hh in range(H):
                        j, half = hh // 2, 64 * (hh % 2)
                        ins = t.matmul(ps_a[half:half + 64,
                                            128 * j + half:128 * j + half + 64],
                                       ktm[:, 128 * j + half:128 * j + half + 64],
                                       vch[:, 64 * hh:64 * (hh + 1)],
                                       start=True, stop=True)
                    fin(ins)
                sg.step("tensor", [], simm)

                def yev(v, fin, g=g, dirn=dirn):
                    last = None
                    for half in range(2):
                        dst = y_tm[:, D * g + 512 * half:D * g + 512 * (half + 1)]
                        if dirn == "f":
                            last = v.tensor_copy(dst, ps_y[half][:, :])
                        else:
                            last = v.tensor_tensor(dst, ps_y[half][:, :], dst, ALU.add)
                    fin(last)
                sg.step("vector", ["tensor"], yev)

                def supd(v, fin, g=g):
                    for hh in range(H):
                        j, half = hh // 2, 64 * (hh % 2)
                        blk = (slice(half, half + 64),
                               slice(128 * j + half, 128 * j + half + 64))
                        v.tensor_tensor(stmp[blk[0], blk[1]], Sp[blk[0], blk[1]],
                                        ps_a[blk[0], blk[1]], ALU.add)
                    v.drain()
                    last = None
                    for hh in range(H):
                        j, half = hh // 2, 64 * (hh % 2)
                        blk = (slice(half, half + 64),
                               slice(128 * j + half, 128 * j + half + 64))
                        ec = totcol[half:half + 64, NDB * g + j:NDB * g + j + 1]
                        last = v.tensor_scalar(Sp[blk[0], blk[1]], stmp[blk[0], blk[1]],
                                               ec, None, ALU.mult)
                    fin(last)
                sg.step("vector", ["tensor"], supd)
    c.pop()


def emit_l0_post(c, io, scr, y_tm, lnwbc, lnbbc, ident, wbuf, stage):
    nc = c.nc
    c.push()
    sb = lambda n, shp, dt: c.stack.enter_context(nc.sbuf_tensor(n, shp, dt))
    sigch = sb("sigch", [128, D], BF16)
    gch = sb("gch", [128, D], BF16)
    ygt = sb("ygt", [128, D], BF16)
    sqt2 = sb("sqt2", [128, D], BF16)
    st16 = sb("st16", [128, 4 * 16], F32)
    st16b = sb("st16b", [128, 32], BF16)
    zzt = sb("zzt", [128, D], BF16)
    zzf = sb("zzf", [128, D], BF16)

    with c.scope(), nc.Block() as block:
        sg = Seg(c, block)

        def ldw(g, fin):
            fin(g.dma_start(*ld3(wbuf[:, :], io["w_Wo"])), 16)
        sg.step("gpsimd", [], ldw)

        ps_t = [c.stack.enter_context(nc.psum_tensor(f"pstP{i}", [128, 128], BF16))
                for i in range(2)]
        ps_o = [c.stack.enter_context(nc.psum_tensor(f"psoP{i}", [128, 512], F32))
                for i in range(2)]

        for g in range(NTC):
            def ld(s, fin, g=g):
                fin(s.dma_start(sigch[:, :], scr["s_sig"][g]), 16)
                fin(s.dma_start(gch[:, :], scr["s_g"][g]), 16)
            sg.step("sync", ["vector", "tensor"], ld)

            def gnorm(v, fin, g=g):
                fin(v.tensor_tensor(ygt[:, :], y_tm[:, D * g:D * (g + 1)], sigch[:, :],
                                    ALU.mult))
            sg.step("vector", ["sync"], gnorm)

            def sq(s, fin):
                fin(s.activation(sqt2[:, :], ygt[:, :], AF.Square))
            sg.step("scalar", ["vector"], sq)

            def stats(v, fin):
                yv = ygt[:, :].rearrange("p (h k) -> p h k", h=H)
                sv = sqt2[:, :].rearrange("p (h k) -> p h k", h=H)
                v.tensor_reduce(st16[:, 0:16], yv, AX.X, ALU.add)
                v.tensor_reduce(st16[:, 16:32], sv, AX.X, ALU.add)
                v.drain()
                v.tensor_scalar(st16[:, 0:16], st16[:, 0:16], 1.0 / KH, None, ALU.mult)
                v.tensor_scalar(st16[:, 16:32], st16[:, 16:32], 1.0 / KH, None, ALU.mult)
                v.drain()
                v.tensor_tensor(st16[:, 32:48], st16[:, 0:16], st16[:, 0:16], ALU.mult)
                v.drain()
                fin(v.tensor_tensor(st16[:, 32:48], st16[:, 16:32], st16[:, 32:48],
                                    ALU.subtract))
            sg.step("vector", ["scalar"], stats)

            def rstd(s, fin):
                s.activation(st16[:, 48:64], st16[:, 32:48], AF.Ln, bias=GNEPS)
                s.drain()
                fin(s.activation(st16[:, 48:64], st16[:, 48:64], AF.Exp, scale=-0.5))
            sg.step("scalar", ["vector"], rstd)

            def apply(v, fin, g=g):
                v.tensor_copy(st16b[:, 0:16], st16[:, 0:16])
                v.tensor_copy(st16b[:, 16:32], st16[:, 48:64])
                yv = ygt[:, :].rearrange("p (h k) -> p h k", h=H)
                zv = zzt[:, :].rearrange("p (h k) -> p h k", h=H)
                mb = st16b[:, 0:16].unsqueeze(2).broadcast_to([128, H, KH])
                rb = st16b[:, 16:32].unsqueeze(2).broadcast_to([128, H, KH])
                v.drain()
                v.tensor_tensor(zv, yv, mb, ALU.subtract)
                v.tensor_tensor(zv, zv, rb, ALU.mult)
                v.tensor_tensor(zzt[:, :], zzt[:, :], lnwbc[:, :], ALU.mult)
                v.tensor_tensor(zzt[:, :], zzt[:, :], lnbbc[:, :], ALU.add)
                fin(v.tensor_tensor(zzt[:, :], zzt[:, :], gch[:, :], ALU.mult))
            sg.step("vector", ["scalar"], apply)

            sg.barrier_wait("tensor", ["vector"])

            def zprod(t, i, sl):
                return t.transpose(ps_t[sl][:, :], zzt[:, 128 * i:128 * (i + 1)],
                                   ident[:, :])

            def zcons(v, i, sl):
                return v.tensor_copy(zzf[:, 128 * i:128 * (i + 1)], ps_t[sl][:, :])

            pipe(sg, NDB, zprod, zcons, 2)

            def produce(t, i, sl, g=g):
                last = None
                for jj in range(NDB):
                    last = t.matmul(ps_o[sl][:, :], zzf[:, 128 * jj:128 * (jj + 1)],
                                    wbuf[:, D * jj + 512 * i:D * jj + 512 * (i + 1)],
                                    start=(jj == 0), stop=(jj == NDB - 1))
                return last

            def consume(v, i, sl, st):
                return v.tensor_copy(stage[:, 1024 * st:1024 * st + 512], ps_o[sl][:, :])

            def store(y, i, st, g=g):
                return y.dma_start(scr["s_attn"][g][:, 512 * i:512 * (i + 1)],
                                   stage[:, 1024 * st:1024 * st + 512])

            sg.barrier_wait("tensor", ["vector", "gpsimd"])
            pipe3(sg, 2, produce, consume, store, 2, 2)
    c.pop()


SEG_INPUTS = {
    "l0a": ["x0", "w_Wr", "w_Wk", "w_Wv", "w_Wg", "w_Wo", "w_gate", "w_maa_w1",
            "w_maa_w2", "w_decay_w1", "w_decay_w2", "v_td_row", "v_convw",
            "r_maa_x", "r_maa_w", "r_maa_k", "r_maa_v", "r_maa_r", "r_maa_g",
            "c_maa_x", "r_ln_w", "r_ln_b", "r_gate_b",
            "c_ident", "c_Lf", "c_Lb", "c_mask_f", "c_mask_b"],
    "moe": ["x0", "r_rms_w", "w_moe_gateT", "w_moe_W1", "w_moe_W2", "c_moe_b1",
            "w_moe_b2", "r_l1n_w", "r_l1n_b", "c_ident"],
    "mamba": ["w_WinT", "c_mconv_w", "c_mconv_b", "w_WxT", "w_WdtT", "c_bdt",
              "c_Alog", "c_mD", "w_WoutT", "c_ident"],
    "l1f": ["w_fb_T", "w_fc", "w_fcT", "w_proj", "w_projT", "w_lrv", "w_lruT",
            "v_lrgate", "c_ident"],
}


def build_fused():
    nc = bass.Bass("TRN2", target_bir_lowering=False)
    io = {}
    seen = set()
    for seg in SEGS:
        for nm in SEG_INPUTS[seg]:
            if nm not in seen:
                shp, dt = IN_SPECS[nm]
                io[nm] = nc.dram_tensor(nm, shp, dt, kind="ExternalInput").ap()
                seen.add(nm)
    scr = {}
    for nm, (shp, dt) in SCRATCH.items():
        scr[nm] = nc.dram_tensor(nm, shp, dt).ap()
    io["out"] = nc.dram_tensor("out", [T, D], F32, kind="ExternalOutput").ap()
    c = Ctx(nc)
    with c.stack:
        c.init_sems()
        c.setup_consts()
        c.push()
        seg_l0a(c, io, scr)
        c.pop()
        c.push()
        seg_moe(c, io, scr)
        c.pop()
        c.push()
        l1ctx = seg_l1f_pre(c, io, scr)
        c.pop()
        c.push()
        seg_mamba(c, io, scr, l1ctx)
        c.pop()
        c.push()
        seg_l1f_final(c, io, scr)
        c.pop()
        emit_tail(c)
    return nc


def build_segment(seg):
    nc = bass.Bass("TRN2", target_bir_lowering=False)
    io = {}
    for nm in SEG_INPUTS[seg]:
        shp, dt = IN_SPECS[nm]
        io[nm] = nc.dram_tensor(nm, shp, dt, kind="ExternalInput").ap()
    rd, wr = SEG_IO[seg]
    scr = {}
    for nm in rd:
        shp, dt = SCRATCH[nm]
        scr[nm] = nc.dram_tensor(nm, shp, dt, kind="ExternalInput").ap()
    for nm in wr:
        shp, dt = SCRATCH[nm]
        scr[nm] = nc.dram_tensor(nm, shp, dt, kind="ExternalOutput").ap()
    if seg == "l1f":
        io["out"] = nc.dram_tensor("out", [T, D], F32, kind="ExternalOutput").ap()
    c = Ctx(nc)
    with c.stack:
        c.init_sems()
        c.setup_consts()
        if seg == "l0a":
            seg_l0a(c, io, scr)
        elif seg == "moe":
            seg_moe(c, io, scr)
        else:
            raise NotImplementedError("unfused build only supports l0a/moe now")
        emit_tail(c)
    return nc


def emit_tail(c):
    nc = c.nc
    with nc.Block() as block:
        sg = Seg(c, block)
        sg.barrier_wait("sync")
        sg.barrier_wait("gpsimd", ["sync"])

        def fini(s):
            for nm in ("pp", "cc", "qq", "dd"):
                v = c.val(nm)
                if v > 0:
                    s.wait_ge(c.sem(nm)[0], v)
        block.sync(fini)


_PROGRAMS = {}


def kernel(**inputs):
    maps = shard_inputs(inputs)
    if not UNFUSED:
        if "all" not in _PROGRAMS:
            _PROGRAMS["all"] = build_fused()
        nc = _PROGRAMS["all"]
        allin = set()
        for seg in SEGS:
            allin.update(SEG_INPUTS[seg])
        in_maps = [{nm: maps[b][nm] for nm in allin} for b in range(NCORE)]
        res = run_bass_kernel_spmd(nc, in_maps, core_ids=list(range(NCORE)))
        return np.stack([np.asarray(res.results[b]["out"], dtype=np.float32)
                         for b in range(NCORE)], axis=0)
    segs = SEGS
    scr_state = [dict() for _ in range(NCORE)]
    out = None
    for seg in segs:
        if seg not in _PROGRAMS:
            _PROGRAMS[seg] = build_segment(seg)
        nc = _PROGRAMS[seg]
        rd, wr = SEG_IO[seg]
        in_maps = []
        for b in range(NCORE):
            m = {nm: maps[b][nm] for nm in SEG_INPUTS[seg]}
            for nm in rd:
                m[nm] = scr_state[b][nm]
            in_maps.append(m)
        res = run_bass_kernel_spmd(nc, in_maps, core_ids=list(range(NCORE)))
        for b in range(NCORE):
            for nm in wr:
                scr_state[b][nm] = np.asarray(res.results[b][nm])
        if seg == "l1f":
            out = np.stack([np.asarray(res.results[b]["out"], dtype=np.float32)
                            for b in range(NCORE)], axis=0)
    return out


def seg_moe(c, io, scr):
    nc = c.nc
    sb = lambda n, shp, dt: c.stack.enter_context(nc.sbuf_tensor(n, shp, dt))
    ident = sb("identM", [128, 128], BF16)
    ones1b = sb("ones1bM", [1, 128], BF16)
    rmsbc = sb("rmsbc", [128, D], BF16)
    l1wbc = sb("l1wbc", [128, D], BF16)
    l1bbc = sb("l1bbc", [128, D], BF16)
    rows = {}
    for nm in ("r_rms_w", "r_l1n_w", "r_l1n_b"):
        rows[nm] = sb(nm + "_sbM", [1, D], BF16)
    gate8 = sb("gate8", [128, NDB * E], BF16)
    gate8f = sb("gate8f", [128, NDB * E], F32)
    gate8lo = sb("gate8lo", [128, NDB * E], BF16)
    xnrf = sb("xnrf", [128, NDB * T], BF16)
    xnrlog = sb("xnrlog", [128, D], BF16)
    mskrows = sb("mskrows", [E, T], BF16)
    mskrow_t = sb("mskrow_t", [1, T], BF16)
    hsb = sb("hsb", [128, 8 * T], BF16)
    b1c = sb("b1c", [128, NFB], F32)
    b2b = sb("b2b", [E, D], BF16)
    selbc = sb("selbc", [128, T], BF16)

    with nc.Block() as block:
        sg = Seg(c, block)

        def ld(s, fin):
            fin(s.dma_start(ident[:, :], io["c_ident"]), 16)
        sg.step("sync", [], ld)

        def ldg(g, fin):
            for nm in rows:
                fin(g.dma_start(rows[nm][:, :], io[nm]), 16)
            fin(g.dma_start(*ld3(gate8[:, :], io["w_moe_gateT"])), 16)
            fin(g.dma_start(*ld3(gate8f[:, :], io["w_moe_gateT"])), 16)
            fin(g.dma_start(b2b[:, :], io["w_moe_b2"]), 16)
        sg.step("gpsimd", [], ldg)

        def ms(v, fin):
            fin(v.memset(ones1b[:, :], 1.0))
        sg.step("vector", [], ms)

        def glo(v, fin):
            fin(v.tensor_tensor(gate8lo[:, :], gate8f[:, :], gate8[:, :], ALU.subtract))
        sg.step("vector", ["gpsimd", "sync"], glo)

    emit_bcast_row(c, ones1b, rows["r_rms_w"], rmsbc)
    emit_bcast_row(c, ones1b, rows["r_l1n_w"], l1wbc)
    emit_bcast_row(c, ones1b, rows["r_l1n_b"], l1bbc)

    c.push()
    with nc.Block() as block:
        sg = Seg(c, block)
        xnt = [c.stack.enter_context(nc.sbuf_tensor(f"xntM{i}", [128, D], BF16)) for i in range(2)]
        xnr = c.stack.enter_context(nc.sbuf_tensor("xnrM", [128, D], BF16))
        xnrF = c.stack.enter_context(nc.sbuf_tensor("xnrF", [128, D], F32))
        xnrL = c.stack.enter_context(nc.sbuf_tensor("xnrL", [128, D], BF16))
        sqt = c.stack.enter_context(nc.sbuf_tensor("sqtM", [128, D], BF16))
        col = c.stack.enter_context(nc.sbuf_tensor("colM", [128, 8], F32))
        lgt = c.stack.enter_context(nc.sbuf_tensor("lgt", [128, 8], F32))
        mskb = c.stack.enter_context(nc.sbuf_tensor("mskb", [128, 8], BF16))
        msk8 = c.stack.enter_context(nc.sbuf_tensor("msk8", [8, 128], BF16))
        mskdram = nc.dram_tensor("mskdram", [E, T], BF16).ap()
        psT = [c.stack.enter_context(nc.psum_tensor(f"psTM{i}", [128, 128], BF16)) for i in range(2)]
        psL = c.stack.enter_context(nc.psum_tensor("psL", [128, 512], F32))

        for g in range(NTC):
            b = g % 2

            def ldx(s, fin, g=g, b=b):
                fin(s.dma_start(xnt[b][:, :], scr["s_xn0tm"][g]), 16)
            sg.step("sync", ["vector", "scalar", "tensor"], ldx)

            def sq_(s, fin, b=b):
                fin(s.activation(sqt[:, :], xnt[b][:, :], AF.Square))
            sg.step("scalar", ["sync"], sq_)

            def stats(v, fin, b=b):
                v.tensor_reduce(col[:, 1:2], sqt[:, :], AX.X, ALU.add)
                v.drain()
                fin(v.tensor_scalar(col[:, 1:2], col[:, 1:2], 1.0 / D, None, ALU.mult))
            sg.step("vector", ["scalar"], stats)

            def rstd(s, fin):
                s.activation(col[:, 3:4], col[:, 1:2], AF.Ln, bias=RMSEPS)
                s.drain()
                fin(s.activation(col[:, 3:4], col[:, 3:4], AF.Exp, scale=-0.5))
            sg.step("scalar", ["vector"], rstd)

            def apply(v, fin, b=b):
                v.tensor_scalar(xnrF[:, :], xnt[b][:, :], col[:, 3:4], None, ALU.mult)
                v.tensor_tensor(xnrF[:, :], xnrF[:, :], rmsbc[:, :], ALU.mult)
                v.tensor_copy(xnr[:, :], xnrF[:, :])
                fin(v.tensor_tensor(xnrL[:, :], xnrF[:, :], xnr[:, :], ALU.subtract))
            sg.step("vector", ["scalar", "tensor"], apply)

            sg.barrier_wait("tensor", ["vector"])

            def tprod(t, i, sl, g=g):
                srct = xnr if i < NDB else xnrL
                j = i % NDB
                return t.transpose(psT[sl][:, :], srct[:, 128 * j:128 * (j + 1)], ident[:, :])

            def tcons(v, i, sl, g=g):
                j = i % NDB
                if i < NDB:
                    return v.tensor_copy(xnrf[:, T * j + 128 * g:T * j + 128 * (g + 1)],
                                         psT[sl][:, :])
                return v.tensor_copy(xnrlog[:, 128 * j:128 * (j + 1)], psT[sl][:, :])

            pipe(sg, 2 * NDB, tprod, tcons, 2)

            def lmm(t, fin, g=g):
                last = None
                for pi in range(3):
                    ga = gate8 if pi != 2 else gate8lo
                    for jj in range(NDB):
                        xa = (xnrf[:, T * jj + 128 * g:T * jj + 128 * (g + 1)]
                              if pi != 1 else xnrlog[:, 128 * jj:128 * (jj + 1)])
                        last = t.matmul(psL[:, 0:E], xa, ga[:, E * jj:E * (jj + 1)],
                                        start=(pi == 0 and jj == 0),
                                        stop=(pi == 2 and jj == NDB - 1))
                fin(last)
            sg.step("tensor", ["vector"], lmm)

            def gating(v, fin, g=g):
                v.tensor_copy(lgt[:, :], psL[:, 0:E])
                v.drain()
                v.tensor_reduce(col[:, 0:1], lgt[:, :], AX.X, ALU.max)
                v.drain()
                v.tensor_scalar(mskb[:, :], lgt[:, :], col[:, 0:1], None, ALU.is_ge)
                v.drain()
                fin(v.tensor_scalar(lgt[:, :], lgt[:, :], col[:, 0:1], None, ALU.subtract))
            sg.step("vector", ["tensor"], gating)

            def gexp(s, fin):
                fin(s.activation(lgt[:, :], lgt[:, :], AF.Exp))
            sg.step("scalar", ["vector"], gexp)

            def gfin(v, fin, g=g):
                v.tensor_reduce(col[:, 2:3], lgt[:, :], AX.X, ALU.add)
                v.drain()
                v.reciprocal(col[:, 4:5], col[:, 2:3])
                v.drain()
                fin(v.tensor_scalar(mskb[:, :], mskb[:, :], col[:, 4:5], None, ALU.mult))
            sg.step("vector", ["scalar"], gfin)

            sg.barrier_wait("tensor", ["vector"])

            def mprod(t, i, sl, g=g):
                return t.transpose(psT[sl][0:E, :], mskb[:, :], ident[:, :])

            def mcons(v, i, sl, g=g):
                return v.tensor_copy(msk8[:, :], psT[sl][0:E, :])

            pipe(sg, 1, mprod, mcons, 2)

            def mst(s, fin, g=g):
                fin(s.dma_start(mskdram[:, 128 * g:128 * (g + 1)], msk8[:, :]), 16)
            sg.step("sync", ["vector"], mst)
            sg.barrier_wait("vector", ["sync"])
    with nc.Block() as block:
        sg = Seg(c, block)

        def ldm(s, fin):
            fin(s.dma_start(mskrows[:, :], mskdram), 16)
        sg.step("sync", [], ldm)
    c.pop()

    c.push()
    wdv = {}
    with nc.Block() as block:
        sg = Seg(c, block)
        yfp = c.stack.enter_context(nc.sbuf_tensor("yfpM", [128, 8 * T], F32))
        mtmp = c.stack.enter_context(nc.sbuf_tensor("mtmpM", [128, 512], F32))
        w1q = [c.stack.enter_context(nc.sbuf_tensor(f"w1qM{i}", [128, NDB * 1024], BF16))
               for i in range(2)]
        w2b = [c.stack.enter_context(nc.sbuf_tensor(f"w2bM{i}", [128, 8 * 128], BF16))
               for i in range(8)]
        psh = [c.stack.enter_context(nc.psum_tensor(c.uname(f"psh{i}"), [128, 512], F32))
               for i in range(2)]
        psy = [c.stack.enter_context(nc.psum_tensor(c.uname(f"psyM{i}"), [128, 512], F32))
               for i in range(4)]

        def emit_w1(e_, q_):
            def f(g, fin):
                fin(g.dma_start(*ld3(w1q[q_ % 2][:, :], io["w_moe_W1"][e_, q_])), 16)
                wdv[("w1", e_, q_)] = c.val("eG")
            sg.step("gpsimd", ["tensor"], f)

        def emit_w2(e_, q_):
            def f(g, fin):
                for dc in range(8):
                    fin(g.dma_start(*ld3(w2b[dc][:, :],
                        io["w_moe_W2"][e_, dc][1024 * q_:1024 * (q_ + 1), :])), 16)
                    wdv[("w2", e_, q_, dc)] = c.val("eG")
            sg.step("gpsimd", ["tensor"], f)

        emit_w1(0, 0)
        for e in range(E):
            def ldb1(s, fin, e=e):
                fin(s.dma_start(b1c[:, :], io["c_moe_b1"][e]), 16)
                fin(s.dma_start(mskrow_t[:, :], mskdram[e:e + 1, :]), 16)
            sg.step("sync", ["scalar", "tensor"], ldb1)
            yv = c.val("eY")

            def sprod(t, i, sl):
                if i == 0:
                    c.wait(t, "eY", yv)
                return t.matmul(psh[sl][:, :], ones1b[:, :],
                                mskrow_t[0:1, 512 * i:512 * (i + 1)],
                                start=True, stop=True)

            def scons(v, i, sl):
                return v.tensor_copy(selbc[:, 512 * i:512 * (i + 1)], psh[sl][:, :])

            pipe(sg, 4, sprod, scons, 2)

            for q in range(4):
                emit_w2(e, q)
                if q < 3:
                    emit_w1(e, q + 1)
                elif e + 1 < E:
                    emit_w1(e + 1, 0)

                itemsA = [(fc, nsl) for fc in range(8) for nsl in range(4)]

                def aprod(t, i, sl, e=e, q=q):
                    fc, nsl = itemsA[i]
                    if i == 0:
                        c.wait(t, "eG", wdv[("w1", e, q)])
                    last = None
                    for jj in range(NDB):
                        last = t.matmul(
                            psh[sl][:, :],
                            w1q[q % 2][:, 1024 * jj + 128 * fc:1024 * jj + 128 * (fc + 1)],
                            xnrf[:, T * jj + 512 * nsl:T * jj + 512 * (nsl + 1)],
                            start=(jj == 0), stop=(jj == NDB - 1))
                    return last

                def acons(s, i, sl, q=q):
                    fc, nsl = itemsA[i]
                    if i == 0:
                        c.wait(s, "eY", yv)
                    fcg = 8 * q + fc
                    return s.activation(
                        hsb[:, T * fc + 512 * nsl:T * fc + 512 * (nsl + 1)],
                        psh[sl][:, :], AF.Silu, bias=b1c[:, fcg:fcg + 1])

                pipe(sg, len(itemsA), aprod, acons, 2, ceng="scalar")
                ccq = c.val("cc")

                itemsB = [(dc, nsl) for dc in range(8) for nsl in range(4)]

                def bprod(t, i, sl, e=e, q=q, ccq=ccq):
                    dc, nsl = itemsB[i]
                    if i == 0:
                        c.wait(t, "cc", ccq)
                    if nsl == 0:
                        c.wait(t, "eG", wdv[("w2", e, q, dc)])
                    last = None
                    for fc in range(8):
                        last = t.matmul(
                            psy[sl][:, :],
                            w2b[dc][:, 128 * fc:128 * (fc + 1)],
                            hsb[:, T * fc + 512 * nsl:T * fc + 512 * (nsl + 1)],
                            start=(fc == 0), stop=(fc == 7))
                    return last

                def bcons(v, i, sl, e=e, q=q):
                    dc, nsl = itemsB[i]
                    sel = selbc[:, 512 * nsl:512 * (nsl + 1)]
                    dst = yfp[:, T * dc + 512 * nsl:T * dc + 512 * (nsl + 1)]
                    if e == 0 and q == 0:
                        return v.tensor_tensor(dst, psy[sl][:, :], sel, ALU.mult)
                    v.tensor_tensor(mtmp[:, :], psy[sl][:, :], sel, ALU.mult)
                    return v.tensor_tensor(dst, mtmp[:, :], dst, ALU.add)

                pipe(sg, len(itemsB), bprod, bcons, 4)

        def ycast(v, fin):
            fin(v.tensor_copy(hsb[:, :], yfp[:, :]))
        sg.step("vector", ["tensor"], ycast)
    c.pop()

    c.push()
    with nc.Block() as block:
        sg = Seg(c, block)
        psT2 = [c.stack.enter_context(nc.psum_tensor(c.uname(f"psT2{i}"), [128, 128], BF16))
                for i in range(2)]
        psB2 = c.stack.enter_context(nc.psum_tensor(c.uname("psB2"), [128, D], F32))
        fftile = c.stack.enter_context(nc.sbuf_tensor(c.uname("fftile"), [128, D], BF16))

        for g in range(NTC):
            def b2mm(t, fin, g=g):
                t.matmul(psB2[:, 0:512], mskrows[:, 128 * g:128 * (g + 1)],
                         b2b[:, 0:512], start=True, stop=True)
                fin(t.matmul(psB2[:, 512:1024], mskrows[:, 128 * g:128 * (g + 1)],
                             b2b[:, 512:1024], start=True, stop=True))
            sg.step("tensor", ["vector"], b2mm)

            def tprod(t, i, sl, g=g):
                dc = i
                return t.transpose(psT2[sl][:, :],
                                   hsb[:, T * dc + 128 * g:T * dc + 128 * (g + 1)],
                                   ident[:, :])

            def tcons(v, i, sl):
                dc = i
                return v.tensor_copy(fftile[:, 128 * dc:128 * (dc + 1)], psT2[sl][:, :])

            pipe(sg, 8, tprod, tcons, 2)

            def addb2(v, fin, g=g):
                fin(v.tensor_tensor(fftile[:, :], psB2[:, :], fftile[:, :], ALU.add))
            sg.step("vector", ["tensor"], addb2)

            def st(s, fin, g=g):
                fin(s.dma_start(scr["s_ffn"][g], fftile[:, :]), 16)
            sg.step("sync", ["vector"], st)
            sg.barrier_wait("vector", ["sync"])
    c.pop()

    c.push()
    with nc.Block() as block:
        sg = Seg(c, block)
        x0t = c.stack.enter_context(nc.sbuf_tensor("x0tM", [128, D], F32))
        att = c.stack.enter_context(nc.sbuf_tensor("attM", [128, D], BF16))
        fft = c.stack.enter_context(nc.sbuf_tensor("fftM", [128, D], BF16))
        x1t = [c.stack.enter_context(nc.sbuf_tensor(f"x1tM{i}", [128, D], F32)) for i in range(2)]
        xn1t = c.stack.enter_context(nc.sbuf_tensor("xn1tM", [128, D], BF16))
        xmt = c.stack.enter_context(nc.sbuf_tensor("xmtM", [128, D], BF16))
        sqt = c.stack.enter_context(nc.sbuf_tensor("sqtM2", [128, D], BF16))
        col = c.stack.enter_context(nc.sbuf_tensor("colM2", [128, 8], F32))
        psT = [c.stack.enter_context(nc.psum_tensor(f"psTM2{i}", [128, 128], BF16)) for i in range(2)]
        stg = c.stack.enter_context(nc.sbuf_tensor("stgM", [128, D], BF16))
        stg2 = c.stack.enter_context(nc.sbuf_tensor("stgM2", [128, D], BF16))

        for g in range(NTC):
            b = g % 2

            def ld(s, fin, g=g, b=b):
                fin(s.dma_start(x0t[:, :], io["x0"][128 * g:128 * (g + 1), :]), 16)
                fin(s.dma_start(att[:, :], scr["s_attn"][g]), 16)
                fin(s.dma_start(fft[:, :], scr["s_ffn"][g]), 16)
            sg.step("sync", ["vector", "scalar", "tensor"], ld)

            def xsum(v, fin, g=g, b=b):
                v.tensor_tensor(x1t[b][:, :], x0t[:, :], att[:, :], ALU.add)
                fin(v.tensor_tensor(x1t[b][:, :], x1t[b][:, :], fft[:, :], ALU.add))
            sg.step("vector", ["sync"], xsum)

            def stx(s, fin, g=g, b=b):
                fin(s.dma_start(scr["s_x1"][g], x1t[b][:, :]), 16)
            sg.step("sync", ["vector"], stx)

            def sq_(s, fin, b=b):
                fin(s.activation(sqt[:, :], x1t[b][:, :], AF.Square))
            sg.step("scalar", ["vector"], sq_)

            def stats(v, fin, b=b):
                v.tensor_reduce(col[:, 0:1], x1t[b][:, :], AX.X, ALU.add)
                v.tensor_reduce(col[:, 1:2], sqt[:, :], AX.X, ALU.add)
                v.drain()
                v.tensor_scalar(col[:, 0:1], col[:, 0:1], 1.0 / D, None, ALU.mult)
                v.tensor_scalar(col[:, 1:2], col[:, 1:2], 1.0 / D, None, ALU.mult)
                v.drain()
                v.tensor_tensor(col[:, 2:3], col[:, 0:1], col[:, 0:1], ALU.mult)
                v.drain()
                fin(v.tensor_tensor(col[:, 2:3], col[:, 1:2], col[:, 2:3], ALU.subtract))
            sg.step("vector", ["scalar"], stats)

            def rstd(s, fin):
                s.activation(col[:, 3:4], col[:, 2:3], AF.Ln, bias=LNEPS)
                s.drain()
                fin(s.activation(col[:, 3:4], col[:, 3:4], AF.Exp, scale=-0.5))
            sg.step("scalar", ["vector"], rstd)

            def apply(v, fin, g=g, b=b):
                v.scalar_tensor_tensor(col[:, 4:5], col[:, 0:1], -1.0, col[:, 3:4],
                                       ALU.mult, ALU.mult)
                v.drain()
                fin(v.tensor_scalar(xn1t[:, :], x1t[b][:, :], col[:, 3:4], col[:, 4:5],
                                    ALU.mult, ALU.add))
            sg.step("vector", ["scalar", "tensor", "sync"], apply)

            def stxn1(s, fin, g=g):
                fin(s.dma_start(scr["s_xn1"][g], xn1t[:, :]), 16)
            sg.step("sync", ["vector"], stxn1)

            def stats2(v, fin):
                v.tensor_scalar(col[:, 5:6], col[:, 2:3], LNEPS, None, ALU.add)
                v.drain()
                v.reciprocal(col[:, 6:7], col[:, 5:6])
                v.drain()
                fin(v.tensor_tensor(col[:, 2:3], col[:, 2:3], col[:, 6:7], ALU.mult))
            sg.step("vector", ["scalar"], stats2)

            def rstd2(s, fin):
                s.activation(col[:, 3:4], col[:, 2:3], AF.Ln, bias=LNEPS)
                s.drain()
                fin(s.activation(col[:, 3:4], col[:, 3:4], AF.Exp, scale=-0.5))
            sg.step("scalar", ["vector"], rstd2)

            def apply2(v, fin, g=g):
                v.tensor_scalar(xmt[:, :], xn1t[:, :], col[:, 3:4], None, ALU.mult)
                v.tensor_tensor(xmt[:, :], xmt[:, :], l1wbc[:, :], ALU.mult)
                fin(v.tensor_tensor(xmt[:, :], xmt[:, :], l1bbc[:, :], ALU.add))
            sg.step("vector", ["scalar", "tensor", "sync"], apply2)

            sg.barrier_wait("tensor", ["vector", "sync"])

            def tp1(t, i, sl, g=g):
                return t.transpose(psT[sl][:, :], xn1t[:, 128 * i:128 * (i + 1)], ident[:, :])

            def tc1(v, i, sl, g=g):
                return v.tensor_copy(stg2[:, 128 * i:128 * (i + 1)], psT[sl][:, :])

            pipe(sg, NDB, tp1, tc1, 2)

            def st1(s, fin, g=g):
                fin(s.dma_start(
                    scr["s_xn1f"][:, :, 128 * g:128 * (g + 1)].rearrange("j p tt -> p j tt"),
                    stg2[:, :].rearrange("p (j tt) -> p j tt", j=NDB)), 16)
            sg.step("sync", ["vector"], st1)

            def tp2(t, i, sl, g=g):
                return t.transpose(psT[sl][:, :], xmt[:, 128 * i:128 * (i + 1)], ident[:, :])

            def tc2(v, i, sl, g=g):
                return v.tensor_copy(stg[:, 128 * i:128 * (i + 1)], psT[sl][:, :])

            sg.barrier_wait("tensor", ["sync"])
            pipe(sg, NDB, tp2, tc2, 2)

            def st2(s, fin, g=g):
                fin(s.dma_start(
                    scr["s_xmf"][:, :, 128 * g:128 * (g + 1)].rearrange("j p tt -> p j tt"),
                    stg[:, :].rearrange("p (j tt) -> p j tt", j=NDB)), 16)
            sg.step("sync", ["vector"], st2)
            sg.barrier_wait("vector", ["sync"])
    c.pop()


def seg_mamba(c, io, scr, l1ctx=None):
    nc = c.nc
    sb = lambda n, shp, dt: c.stack.enter_context(nc.sbuf_tensor(n, shp, dt))
    ones1b = sb("ones1bX", [1, 128], BF16)
    convw = sb("mconvw", [128, NDI * 4], F32)
    convb = sb("mconvb", [128, NDI], F32)
    bdt = sb("bdtX", [128, NDI], F32)
    mD = sb("mDX", [128, NDI], F32)
    a_sb = sb("a_sbX", [128, NDI * DS], F32)
    bcd = nc.dram_tensor("bcdram", [2 * DS, T], BF16).ap()
    carry = sb("carryX", [128, 2 * DS * NDI // 128 * 8] if False else [128, 256], F32)

    with nc.Block() as block:
        sg = Seg(c, block)

        def ld(s, fin):
            fin(s.dma_start(convw[:, :], io["c_mconv_w"]), 16)
            fin(s.dma_start(convb[:, :], io["c_mconv_b"]), 16)
            fin(s.dma_start(bdt[:, :], io["c_bdt"]), 16)
            fin(s.dma_start(mD[:, :], io["c_mD"]), 16)
            fin(s.dma_start(a_sb[:, :], io["c_Alog"]), 16)
        sg.step("sync", [], ld)

        def ms(v, fin):
            v.memset(carry[:, :], 0.0)
            fin(v.memset(ones1b[:, :], 1.0))
        sg.step("vector", [], ms)

        def mka(s, fin):
            fin(s.activation(a_sb[:, :], a_sb[:, :], AF.Exp))
        sg.step("scalar", ["sync"], mka)

        def nega(v, fin):
            v.drain()
            fin(v.tensor_scalar(a_sb[:, :], a_sb[:, :], -1.0, None, ALU.mult))
        sg.step("vector", ["scalar"], nega)

    c.push()
    with nc.Block() as block:
        sg = Seg(c, block)
        xmf = c.stack.enter_context(nc.sbuf_tensor("xmfX", [128, NDB * T], BF16))
        wu = c.stack.enter_context(nc.sbuf_tensor("wuX", [128, NDB * 128], BF16))
        wz = c.stack.enter_context(nc.sbuf_tensor("wzX", [128, NDB * 128], BF16))
        wxb = c.stack.enter_context(nc.sbuf_tensor("wxbX", [128, NDI * 128], BF16))
        uraw = c.stack.enter_context(nc.sbuf_tensor("urawX", [128, T], BF16))
        ucnv = c.stack.enter_context(nc.sbuf_tensor("ucnvX", [128, T], BF16))
        zt = c.stack.enter_context(nc.sbuf_tensor("ztX", [128, T], BF16))
        bc32 = c.stack.enter_context(nc.sbuf_tensor("bc32X", [32, T], BF16))
        wdt = c.stack.enter_context(nc.sbuf_tensor("wdtX", [64, DI], BF16))
        dtr = c.stack.enter_context(nc.sbuf_tensor("dtrX", [64, T], BF16))
        psx = [c.stack.enter_context(nc.psum_tensor(c.uname("psxX"), [128, 512], F32))
               for i in range(2)]
        psdt = [c.stack.enter_context(nc.psum_tensor(c.uname("psdtX"), [128, 512], F32))
                for i in range(4)]

        def ldxm(s, fin):
            for j in range(NDB):
                fin(s.dma_start(xmf[:, T * j:T * (j + 1)], scr["s_xmf"][j]), 16)
        sg.step("sync", [], ldxm)

        def ldwx(g, fin):
            fin(g.dma_start(*ld3(wxb[:, :], io["w_WxT"])), 16)
            fin(g.dma_start(wdt[:, :], io["w_WdtT"]), 16)
        sg.step("gpsimd", [], ldwx)

        for db in range(NDI):
            def ldw(g, fin, db=db):
                fin(g.dma_start(*ld3(wu[:, :], io["w_WinT"][:, 128 * db:128 * (db + 1)])), 16)
                fin(g.dma_start(*ld3(wz[:, :], io["w_WinT"][:, DI + 128 * db:DI + 128 * (db + 1)])), 16)
            sg.step("gpsimd", ["tensor"], ldw)
            if db == 0:
                sg.barrier_wait("tensor", ["gpsimd", "vector", "scalar", "sync"])
            else:
                sg.barrier_wait("tensor", ["gpsimd"])

            items = [(0, sl) for sl in range(4)] + [(1, sl) for sl in range(4)]

            def produce(t, i, s, db=db):
                uz, sl = items[i]
                w = wu if uz == 0 else wz
                last = None
                for jj in range(NDB):
                    last = t.matmul(
                        psx[s][:, :], w[:, 128 * jj:128 * (jj + 1)],
                        xmf[:, T * jj + 512 * sl:T * jj + 512 * (sl + 1)],
                        start=(jj == 0), stop=(jj == NDB - 1))
                return last

            def consume(v, i, s, db=db):
                uz, sl = items[i]
                dst = uraw if uz == 0 else zt
                return v.tensor_copy(dst[:, 512 * sl:512 * (sl + 1)], psx[s][:, :])

            pipe(sg, len(items), produce, consume, 2)

            def conv(v, fin, db=db):
                w0 = convw[:, 4 * db + 0:4 * db + 1]
                w1 = convw[:, 4 * db + 1:4 * db + 2]
                w2 = convw[:, 4 * db + 2:4 * db + 3]
                w3 = convw[:, 4 * db + 3:4 * db + 4]
                v.tensor_scalar(ucnv[:, :], uraw[:, :], w3, None, ALU.mult)
                v.scalar_tensor_tensor(ucnv[:, 1:T], uraw[:, 0:T - 1], w2, ucnv[:, 1:T],
                                       ALU.mult, ALU.add)
                v.scalar_tensor_tensor(ucnv[:, 2:T], uraw[:, 0:T - 2], w1, ucnv[:, 2:T],
                                       ALU.mult, ALU.add)
                fin(v.scalar_tensor_tensor(ucnv[:, 3:T], uraw[:, 0:T - 3], w0, ucnv[:, 3:T],
                                           ALU.mult, ALU.add))
            sg.step("vector", ["tensor"], conv)

            def usilu(s, fin, db=db):
                fin(s.activation(ucnv[:, :], ucnv[:, :], AF.Silu, bias=convb[:, db:db + 1]))
            sg.step("scalar", ["vector"], usilu)

            def stu(s, fin, db=db):
                fin(s.dma_start(scr["s_u"][db], ucnv[:, :]), 16)
                fin(s.dma_start(scr["s_z"][db], zt[:, :]), 16)
            sg.step("sync", ["scalar", "vector"], stu)

            def dtmm(t, fin, db=db):
                last = None
                for sl in range(4):
                    last = t.matmul(psdt[sl][:, :], wxb[:, 128 * db:128 * (db + 1)],
                                    ucnv[:, 512 * sl:512 * (sl + 1)],
                                    start=(db == 0), stop=(db == NDI - 1))
                fin(last)
            sg.step("tensor", ["scalar", "sync"], dtmm)

        def dtev(v, fin):
            last = None
            for sl in range(4):
                v.tensor_copy(dtr[:, 512 * sl:512 * (sl + 1)], psdt[sl][0:64, :])
                last = v.tensor_copy(bc32[:, 512 * sl:512 * (sl + 1)], psdt[sl][64:96, :])
            fin(last)
        sg.step("vector", ["tensor"], dtev)

        def stbc(s, fin):
            fin(s.dma_start(bcd[0:DS, :], bc32[0:16, :]), 16)
            fin(s.dma_start(bcd[DS:2 * DS, :], bc32[16:32, :]), 16)
        sg.step("sync", ["vector"], stbc)
        sg.barrier_wait("vector", ["sync"])
    c.pop()

    c.push()
    with nc.Block() as block:
        sg = Seg(c, block)
        psd = [c.stack.enter_context(nc.psum_tensor(c.uname("psdX"), [128, 512], F32))
               for i in range(2)]
        ut = c.stack.enter_context(nc.sbuf_tensor("utX2", [128, T], BF16))
        sp1 = c.stack.enter_context(nc.sbuf_tensor("sp1X", [128, T], F32))
        sp2 = c.stack.enter_context(nc.sbuf_tensor("sp2X", [128, T], F32))
        dlt = c.stack.enter_context(nc.sbuf_tensor("dltX", [128, T], BF16))

        for db in range(NDI):
            def ldu(s, fin, db=db):
                fin(s.dma_start(ut[:, :], scr["s_u"][db]), 16)
            sg.step("sync", ["vector", "scalar"], ldu)

            for pr in range(2):
                def dmm(t, fin, db=db, pr=pr):
                    t.matmul(psd[0][:, :], wdt[:, 128 * db:128 * (db + 1)],
                             dtr[:, 1024 * pr:1024 * pr + 512], start=True, stop=True)
                    fin(t.matmul(psd[1][:, :], wdt[:, 128 * db:128 * (db + 1)],
                                 dtr[:, 1024 * pr + 512:1024 * (pr + 1)],
                                 start=True, stop=True))
                sg.step("tensor", ["vector", "scalar"], dmm)

                def soft(s, fin, db=db, pr=pr):
                    o = 1024 * pr
                    s.activation(sp1[:, o:o + 512], psd[0][:, :], AF.Abs,
                                 bias=bdt[:, db:db + 1])
                    s.activation(sp1[:, o + 512:o + 1024], psd[1][:, :], AF.Abs,
                                 bias=bdt[:, db:db + 1])
                    s.activation(sp1[:, o:o + 1024], sp1[:, o:o + 1024], AF.Exp, scale=-1.0)
                    fin(s.activation(sp1[:, o:o + 1024], sp1[:, o:o + 1024], AF.Ln, bias=1.0))
                sg.step("scalar", ["tensor"], soft)

                def relu(v, fin, db=db, pr=pr):
                    o = 1024 * pr
                    v.tensor_scalar(sp2[:, o:o + 512], psd[0][:, :], bdt[:, db:db + 1], 0.0,
                                    ALU.add, ALU.max)
                    fin(v.tensor_scalar(sp2[:, o + 512:o + 1024], psd[1][:, :],
                                        bdt[:, db:db + 1], 0.0, ALU.add, ALU.max))
                sg.step("vector", ["tensor"], relu)

            def dsum(v, fin, db=db):
                v.tensor_tensor(dlt[:, :], sp1[:, :], sp2[:, :], ALU.add)
                fin(v.tensor_tensor(ut[:, :], dlt[:, :], ut[:, :], ALU.mult))
            sg.step("vector", ["scalar", "sync"], dsum)

            def std(s, fin, db=db):
                fin(s.dma_start(scr["s_delta"][db], dlt[:, :]), 16)
                fin(s.dma_start(scr["s_du"][db], ut[:, :]), 16)
            sg.step("sync", ["vector"], std)
    c.pop()

    TH = T // 2
    c.push()
    dh = c.stack.enter_context(nc.sbuf_tensor("dhX", [128, NDI * TH], BF16))
    duh = c.stack.enter_context(nc.sbuf_tensor("duhX", [128, NDI * TH], BF16))
    yh = c.stack.enter_context(nc.sbuf_tensor("yhX", [128, NDI * TH], BF16))
    Bbc = c.stack.enter_context(nc.sbuf_tensor("BbcX", [128, TH], BF16))
    Cbc = c.stack.enter_context(nc.sbuf_tensor("CbcX", [128, TH], BF16))
    brow1 = c.stack.enter_context(nc.sbuf_tensor("brow1X", [1, TH], BF16))
    crow1 = c.stack.enter_context(nc.sbuf_tensor("crow1X", [1, TH], BF16))
    dA4 = [c.stack.enter_context(nc.sbuf_tensor(f"dA4X{i}", [128, 4 * TH], BF16))
           for i in range(2)]
    X4 = c.stack.enter_context(nc.sbuf_tensor("X4X", [128, 4 * TH], BF16))
    h44 = [c.stack.enter_context(nc.sbuf_tensor(f"h4X{i}", [128, 4 * TH], BF16))
           for i in range(2)]
    psb = [c.stack.enter_context(nc.psum_tensor(c.uname("psbX"), [128, 512], F32))
           for i in range(2)]

    ml = l1ctx is not None
    if ml:
        wcd, wpd = l1ctx["wcd"], l1ctx["wpd"]
        xn1s = c.stack.enter_context(nc.sbuf_tensor("xn1sX", [128, NDB * 512], BF16))
        wfc = [c.stack.enter_context(nc.sbuf_tensor(f"wfcX{i}", [128, 1024], BF16))
               for i in range(2)]
        r2 = c.stack.enter_context(nc.sbuf_tensor("r2X", [128, NFB * 512], BF16))
        wpj = [c.stack.enter_context(nc.sbuf_tensor(f"wpjX{i}", [128, 1024], BF16))
               for i in range(2)]
        hbf = c.stack.enter_context(nc.sbuf_tensor("hbfX", [128, 1024], BF16))
        pstg = c.stack.enter_context(nc.sbuf_tensor("pstgX", [128, 2048], BF16))
        psh = [c.stack.enter_context(nc.psum_tensor(c.uname("pshX"), [128, 512], F32))
               for i in range(2)]
        psp = [c.stack.enter_context(nc.psum_tensor(c.uname("pspX"), [128, 512], F32))
               for i in range(4)]
        L = dict(pend=c.val("pp"), cc_end=c.val("cc"), qq_end=c.val("qq"),
                 proj_pp=0, evac_cc=0, nhalf=0, store_dd=[], qb={})

    def emit_h(block, tsl):
        pb = c.val("pp")
        yv = {}

        def gload(g, pb=pb, tsl=tsl):
            if L["pend"] > 0:
                c.wait(g, "pp", L["pend"])
            c.inc("eG", g.dma_start(
                xn1s[:, :].rearrange("p (j t) -> p j t", j=NDB),
                scr["s_xn1f"][:, :, 512 * tsl:512 * (tsl + 1)].rearrange(
                    "j p t -> p j t")), 16)
            for fc in range(NFB):
                if fc >= 2:
                    c.wait(g, "pp", pb + fc - 1)
                c.inc("eG", g.dma_start(
                    wfc[fc % 2][:, :].rearrange("p (dc t) -> p dc t", dc=NDB),
                    wcd[fc].rearrange("dc p t -> p dc t")), 16)
                yv[fc] = c.val("eG")
        block.gpsimd(gload)

        q0 = c.val("cc")
        qb = c.val("qq")
        L["qb"][tsl] = qb

        def produce(t, pb=pb, q0=q0):
            for fc in range(NFB):
                c.wait(t, "eG", yv[fc])
                w = q0 + fc - 1 if fc >= 2 else L["cc_end"]
                if w > 0:
                    c.wait(t, "cc", w)
                last = None
                for jj in range(NDB):
                    last = t.matmul(psh[fc % 2][:, :],
                                    wfc[fc % 2][:, 128 * jj:128 * (jj + 1)],
                                    xn1s[:, 512 * jj:512 * (jj + 1)],
                                    start=(jj == 0), stop=(jj == NDB - 1))
                c.inc("pp", last)
        block.tensor(produce)

        ppj = L["proj_pp"]

        def hcons(s, pb=pb, ppj=ppj):
            for fc in range(NFB):
                c.wait(s, "pp", pb + fc + 1)
                if fc == 0 and ppj > 0:
                    c.wait(s, "pp", ppj)
                c.inc("cc", s.activation(hbf[:, 512 * (fc % 2):512 * (fc % 2 + 1)],
                                         psh[fc % 2][:, :], AF.Relu))
                c.inc("qq", s.activation(r2[:, 512 * fc:512 * (fc + 1)],
                                         hbf[:, 512 * (fc % 2):512 * (fc % 2 + 1)],
                                         AF.Square))
        block.scalar(hcons)
        L["pend"] = c.val("pp")
        L["cc_end"] = c.val("cc")
        L["qq_end"] = c.val("qq")

    def emit_pmm(block, tsl, half):
        qb = L["qb"][tsl]
        pb = c.val("pp")
        ypv = {}

        def gload(g, pb=pb):
            for fc in range(NFB):
                w = pb + fc - 1 if fc >= 2 else L["pend"]
                if w > 0:
                    c.wait(g, "pp", w)
                c.inc("eG", g.dma_start(wpj[fc % 2][:, :], wpd[fc]), 16)
                ypv[fc] = c.val("eG")
        block.gpsimd(gload)

        evprev = L["evac_cc"]

        def pmm(t, qb=qb, half=half, evprev=evprev):
            for fc in range(NFB):
                c.wait(t, "eG", ypv[fc])
                c.wait(t, "qq", qb + fc + 1)
                if fc == 0 and evprev > 0:
                    c.wait(t, "cc", evprev)
                last = None
                for ti in range(2):
                    tci = 2 * half + ti
                    for dsl in range(2):
                        last = t.matmul(
                            psp[2 * ti + dsl][:, :],
                            r2[:, 512 * fc + 128 * tci:512 * fc + 128 * (tci + 1)],
                            wpj[fc % 2][:, 512 * dsl:512 * (dsl + 1)],
                            start=(fc == 0), stop=(fc == NFB - 1))
                c.inc("pp", last)
        block.tensor(pmm)
        L["pmm_pp"] = c.val("pp")
        L["pend"] = c.val("pp")
        L["proj_pp"] = c.val("pp")

    def emit_evac(block, tsl, half):
        pend2 = L["pmm_pp"]
        ddg = L["store_dd"][-1] if L["store_dd"] else 0

        def evac(v, pend2=pend2, ddg=ddg):
            c.wait(v, "pp", pend2)
            if ddg > 0:
                c.wait(v, "dd", ddg)
            last = None
            for ti in range(2):
                for dsl in range(2):
                    last = v.tensor_copy(
                        pstg[:, 1024 * ti + 512 * dsl:1024 * ti + 512 * (dsl + 1)],
                        psp[2 * ti + dsl][:, :])
            c.inc("cc", last)
        block.vector(evac)

        ecc = c.val("cc")

        def store(s, ecc=ecc, half=half, tsl=tsl):
            c.wait(s, "cc", ecc)
            for ti in range(2):
                g_ = 4 * tsl + 2 * half + ti
                c.inc("dd", s.dma_start(scr["s_ffn2"][g_],
                                        pstg[:, 1024 * ti:1024 * (ti + 1)]), 16)
        block.sync(store)
        L["store_dd"].append(c.val("dd"))
        L["evac_cc"] = ecc
        L["cc_end"] = c.val("cc")

    for th in range(2):
        with nc.Block() as block:
            sg = Seg(c, block)

            def ldh(s, fin, th=th):
                for db in range(NDI):
                    fin(s.dma_start(dh[:, TH * db:TH * (db + 1)],
                                    scr["s_delta"][db][:, TH * th:TH * (th + 1)]), 16)
                    fin(s.dma_start(duh[:, TH * db:TH * (db + 1)],
                                    scr["s_du"][db][:, TH * th:TH * (th + 1)]), 16)
            sg.step("sync", ["vector", "scalar", "gpsimd"], ldh)
            schist = []
            yv_ldh = c.val("eY")
            pendc = [None]

            for s_ in range(DS):
                def ldrow(s, fin, s_=s_, th=th):
                    fin(s.dma_start(brow1[:, :], bcd[s_:s_ + 1, TH * th:TH * (th + 1)]), 16)
                    fin(s.dma_start(crow1[:, :],
                                    bcd[DS + s_:DS + s_ + 1, TH * th:TH * (th + 1)]), 16)
                sg.step("sync", ["vector", "tensor"], ldrow)

                def bcmm(t, fin, s_=s_):
                    t.matmul(psb[0][:, :], ones1b[:, :], brow1[0:1, 0:512],
                             start=True, stop=True)
                    fin(t.matmul(psb[1][:, :], ones1b[:, :], brow1[0:1, 512:1024],
                                 start=True, stop=True))
                sg.step("tensor", ["sync", "vector"], bcmm)

                def bcev(v, fin):
                    v.tensor_copy(Bbc[:, 0:512], psb[0][:, :])
                    fin(v.tensor_copy(Bbc[:, 512:1024], psb[1][:, :]))
                sg.step("vector", ["tensor", "gpsimd"], bcev)

                def ccmm(t, fin, s_=s_):
                    t.matmul(psb[0][:, :], ones1b[:, :], crow1[0:1, 0:512],
                             start=True, stop=True)
                    fin(t.matmul(psb[1][:, :], ones1b[:, :], crow1[0:1, 512:1024],
                                 start=True, stop=True))
                sg.step("tensor", ["vector"], ccmm)

                def ccev(v, fin):
                    v.tensor_copy(Cbc[:, 0:512], psb[0][:, :])
                    fin(v.tensor_copy(Cbc[:, 512:1024], psb[1][:, :]))
                sg.step("vector", ["tensor", "gpsimd"], ccev)

                for grp in range(4):
                    gi = 4 * s_ + grp
                    dbs = [4 * grp + k for k in range(4)]
                    dA = dA4[grp % 2]

                    hv = schist[gi - 2] if gi >= 2 else 0

                    def dAe(s, fin, s_=s_, dbs=dbs, dA=dA, hv=hv):
                        if hv > 0:
                            c.wait(s, "eV", hv)
                        else:
                            c.wait(s, "eY", yv_ldh)
                        last = None
                        for k, db in enumerate(dbs):
                            last = s.activation(dA[:, TH * k:TH * (k + 1)],
                                                dh[:, TH * db:TH * (db + 1)], AF.Exp,
                                                scale=a_sb[:, DS * db + s_:DS * db + s_ + 1])
                        fin(last)
                    sg.step("scalar", [], dAe)

                    def xb(v, fin, s_=s_, dbs=dbs, th=th, dA=dA, grp=grp,
                           pc=pendc[0]):
                        h4 = h44[grp % 2]
                        if pc is not None:
                            pdbs, ps_, ppar = pc
                            for k in range(4):
                                idx = DS * pdbs[k] + ps_
                                v.tensor_copy(carry[:, idx:idx + 1],
                                              h44[ppar][:, TH * (k + 1) - 1:TH * (k + 1)])
                        for k in range(4):
                            v.tensor_tensor(X4[:, TH * k:TH * (k + 1)],
                                            duh[:, TH * dbs[k]:TH * (dbs[k] + 1)],
                                            Bbc[:, :], ALU.mult)
                        for k in range(4):
                            idx = DS * dbs[k] + s_
                            v.tensor_tensor_scan(h4[:, TH * k:TH * (k + 1)],
                                                 dA[:, TH * k:TH * (k + 1)],
                                                 X4[:, TH * k:TH * (k + 1)],
                                                 carry[:, idx:idx + 1] if th == 1 else 0.0,
                                                 ALU.mult, ALU.add)
                        for k in range(4):
                            v.tensor_tensor(X4[:, TH * k:TH * (k + 1)],
                                            h4[:, TH * k:TH * (k + 1)], Cbc[:, :],
                                            ALU.mult)
                        last = None
                        for k in range(4):
                            dst = yh[:, TH * dbs[k]:TH * (dbs[k] + 1)]
                            if s_ == 0:
                                last = v.tensor_copy(dst, X4[:, TH * k:TH * (k + 1)])
                            else:
                                last = v.tensor_tensor(dst, X4[:, TH * k:TH * (k + 1)],
                                                       dst, ALU.add)
                        fin(last)
                    sg.step("vector", ["scalar"], xb)
                    schist.append(c.val("eV"))
                    pendc[0] = (dbs, s_, grp % 2) if th == 0 else None

                if ml:
                    if s_ == 2:
                        emit_h(block, 2 * th)
                    elif s_ == 5:
                        emit_pmm(block, 2 * th, 0)
                    elif s_ == 7:
                        emit_evac(block, 2 * th, 0)
                    elif s_ == 8:
                        emit_pmm(block, 2 * th, 1)
                    elif s_ == 9:
                        emit_h(block, 2 * th + 1)
                    elif s_ == 10:
                        emit_evac(block, 2 * th, 1)
                    elif s_ == 12:
                        emit_pmm(block, 2 * th + 1, 0)
                    elif s_ == 14:
                        emit_evac(block, 2 * th + 1, 0)
                        emit_pmm(block, 2 * th + 1, 1)
                    elif s_ == 15:
                        emit_evac(block, 2 * th + 1, 1)

            if pendc[0] is not None:
                def cflush(v, fin, pc=pendc[0]):
                    pdbs, ps_, ppar = pc
                    last = None
                    for k in range(4):
                        idx = DS * pdbs[k] + ps_
                        last = v.tensor_copy(carry[:, idx:idx + 1],
                                             h44[ppar][:, TH * (k + 1) - 1:TH * (k + 1)])
                    fin(last)
                sg.step("vector", [], cflush)
                pendc[0] = None

            def sty(s, fin, th=th):
                for db in range(NDI):
                    fin(s.dma_start(scr["s_yac"][db][:, TH * th:TH * (th + 1)],
                                    yh[:, TH * db:TH * (db + 1)]), 16)
            sg.step("sync", ["vector", "gpsimd"], sty)
            sg.barrier_wait("vector", ["sync"])
    c.pop()

    c.push()
    with nc.Block() as block:
        sg = Seg(c, block)
        t4 = c.stack.enter_context(nc.sbuf_tensor("t4X", [128, NDI * T], BF16))
        ut = c.stack.enter_context(nc.sbuf_tensor("utX3", [128, T], BF16))
        zt = c.stack.enter_context(nc.sbuf_tensor("ztX3", [128, T], BF16))
        yt = c.stack.enter_context(nc.sbuf_tensor("ytX3", [128, T], BF16))
        woutb = c.stack.enter_context(nc.sbuf_tensor("woutbX", [128, NDI * D], BF16))
        stg = c.stack.enter_context(nc.sbuf_tensor("stgX", [128, 2 * D], BF16))
        pso = [c.stack.enter_context(nc.psum_tensor(c.uname("psoX"), [128, 512], F32))
               for i in range(2)]

        def ldwo(g, fin):
            fin(g.dma_start(*ld3(woutb[:, :], io["w_WoutT"])), 16)
        sg.step("gpsimd", [], ldwo)

        for db in range(NDI):
            def ld(s, fin, db=db):
                fin(s.dma_start(ut[:, :], scr["s_u"][db]), 16)
                fin(s.dma_start(zt[:, :], scr["s_z"][db]), 16)
                fin(s.dma_start(yt[:, :], scr["s_yac"][db]), 16)
            sg.step("sync", ["vector", "scalar"], ld)

            def zs(s, fin, db=db):
                fin(s.activation(zt[:, :], zt[:, :], AF.Silu))
            sg.step("scalar", ["sync"], zs)

            def comb(v, fin, db=db):
                v.tensor_scalar(ut[:, :], ut[:, :], mD[:, db:db + 1], None, ALU.mult)
                v.tensor_tensor(ut[:, :], yt[:, :], ut[:, :], ALU.add)
                fin(v.tensor_tensor(t4[:, T * db:T * (db + 1)], ut[:, :], zt[:, :],
                                    ALU.mult))
            sg.step("vector", ["scalar", "sync"], comb)

        sg.barrier_wait("tensor", ["vector"])
        items = [(g, dsl) for g in range(NTC) for dsl in range(2)]

        def produce(t, i, sl):
            g, dsl = items[i]
            last = None
            for db in range(NDI):
                last = t.matmul(pso[sl][:, :],
                                t4[:, T * db + 128 * g:T * db + 128 * (g + 1)],
                                woutb[:, D * db + 512 * dsl:D * db + 512 * (dsl + 1)],
                                start=(db == 0), stop=(db == NDI - 1))
            return last

        def consume(v, i, sl, st):
            return v.tensor_copy(stg[:, 1024 * st:1024 * st + 512], pso[sl][:, :])

        def store(y, i, st):
            g, dsl = items[i]
            return y.dma_start(scr["s_mamba"][g][:, 512 * dsl:512 * (dsl + 1)],
                               stg[:, 1024 * st:1024 * st + 512])

        pipe3(sg, len(items), produce, consume, store, 2, 2)
    c.pop()


def seg_l1f_pre(c, io, scr):
    nc = c.nc
    sb = lambda n, shp, dt: c.stack.enter_context(nc.sbuf_tensor(n, shp, dt))
    ones1f = sb("ones1fL", [1, 128], F32)
    gcol = sb("gcolL", [128, 1], F32)
    identL = sb("identL", [128, 128], BF16)
    wfcTd = nc.dram_tensor("wfcT_d", [NDB, 128, FF], BF16).ap()
    wcd = nc.dram_tensor("wcomb_d", [NFB, 8, 128, 128], BF16).ap()
    wpd = nc.dram_tensor("wproj_d", [NFB, 128, D], BF16).ap()

    with nc.Block() as block:
        sg = Seg(c, block)

        def ld(s, fin):
            fin(s.dma_start(identL[:, :], io["c_ident"]), 16)
        sg.step("sync", [], ld)

        def ms(v, fin):
            fin(v.memset(ones1f[:, :], 1.0))
        sg.step("vector", [], ms)

    c.push()
    with nc.Block() as block:
        sg = Seg(c, block)
        psg = c.stack.enter_context(nc.psum_tensor(c.uname("psgL"), [128, 512], F32))
        grow = c.stack.enter_context(nc.sbuf_tensor("growL", [1, 1], F32))

        def ldg(s, fin):
            fin(s.dma_start(grow[:, :], io["v_lrgate"]), 16)
        sg.step("sync", [], ldg)

        def gmm(t, fin):
            fin(t.matmul(psg[:, 0:1], ones1f[:, :], grow[0:1, 0:1], start=True, stop=True))
        sg.step("tensor", ["sync", "vector"], gmm)

        def gev(v, fin):
            fin(v.tensor_copy(gcol[:, :], psg[:, 0:1]))
        sg.step("vector", ["tensor"], gev)
    c.pop()

    AFCopy = AF.Copy if hasattr(AF, "Copy") else AF.Identity

    def ternarize(io_key, nrow_tiles, ncol, store_fn):
        c.push()
        with nc.Block() as block:
            sg = Seg(c, block)
            rowt = [c.stack.enter_context(nc.sbuf_tensor(c.uname("rowtL"), [128, ncol], F32))
                    for i in range(2)]
            weffr = [c.stack.enter_context(nc.sbuf_tensor(c.uname("weffrL"), [128, ncol], BF16))
                     for i in range(2)]
            ge = c.stack.enter_context(nc.sbuf_tensor(c.uname("geL"), [128, ncol], BF16))
            le = c.stack.enter_context(nc.sbuf_tensor(c.uname("leL"), [128, ncol], BF16))
            scol = c.stack.enter_context(nc.sbuf_tensor(c.uname("scolL"), [128, 2], F32))
            stg = c.stack.enter_context(nc.sbuf_tensor(c.uname("stgL"), [128, ncol], BF16))
            pst = [c.stack.enter_context(nc.psum_tensor(c.uname("pstL"), [128, 128], BF16))
                   for i in range(2)]
            eyst = [0]

            for i in range(nrow_tiles):
                b = i % 2

                def ldr(s, fin, i=i, b=b):
                    fin(s.dma_start(rowt[b][:, :], io[io_key][128 * i:128 * (i + 1), :]), 16)
                sg.step("sync", ["vector"], ldr)

                def tern(v, fin, i=i, b=b):
                    v.tensor_reduce(scol[:, 0:1], rowt[b][:, :], AX.X, ALU.add,
                                    apply_absolute_value=True)
                    v.drain()
                    v.tensor_scalar(scol[:, 0:1], scol[:, 0:1], 1.0 / ncol, 1e-5,
                                    ALU.mult, ALU.max)
                    v.drain()
                    v.reciprocal(scol[:, 1:2], scol[:, 0:1])
                    v.drain()
                    v.tensor_scalar(ge[:, :], rowt[b][:, :], scol[:, 1:2], 0.5,
                                    ALU.mult, ALU.is_gt)
                    v.tensor_scalar(le[:, :], rowt[b][:, :], scol[:, 1:2], -0.5,
                                    ALU.mult, ALU.is_lt)
                    v.tensor_tensor(ge[:, :], ge[:, :], le[:, :], ALU.subtract)
                    v.drain()
                    fin(v.tensor_scalar(weffr[b][:, :], ge[:, :], scol[:, 0:1], None,
                                        ALU.mult))
                sg.step("vector", ["sync", "tensor"], tern)
                pw = [("eV", c.val("eV"))]
                if eyst[0] > 0:
                    pw.append(("eY", eyst[0]))

                def tp(t, k, sl, i=i, b=b):
                    return t.transpose(pst[sl][:, :], weffr[b][:, 128 * k:128 * (k + 1)],
                                       identL[:, :])

                def tc(s, k, sl, i=i):
                    return s.activation(stg[:, 128 * k:128 * (k + 1)], pst[sl][:, :],
                                        AFCopy)

                pipe(sg, ncol // 128, tp, tc, 2, ceng="scalar", pwait=pw)

                def stw(s, fin, i=i):
                    fin(store_fn(s, i, stg), 16)
                sg.step("sync", ["scalar"], stw)
                eyst[0] = c.val("eY")
        c.pop()

    def store_fc(s, i, stg):
        return s.dma_start(
            wfcTd[:, :, 128 * i:128 * (i + 1)].rearrange("j p f -> p j f"),
            stg[:, 0:D].rearrange("p (j f) -> p j f", j=NDB))

    def store_proj(s, i, stg):
        return s.dma_start(
            wpd[:, :, 128 * i:128 * (i + 1)].rearrange("fc p d -> p fc d"),
            stg[:, 0:FF].rearrange("p (fc d) -> p fc d", fc=NFB))

    ternarize("w_fc", NFB, D, store_fc)
    ternarize("w_proj", NDB, FF, store_proj)

    c.push()
    with nc.Block() as block:
        sg = Seg(c, block)
        fbT = c.stack.enter_context(nc.sbuf_tensor("fbTL", [16, 16], BF16))
        vb = c.stack.enter_context(nc.sbuf_tensor("vbL", [16, D], BF16))
        uTb = c.stack.enter_context(nc.sbuf_tensor("uTbL", [16, FF], BF16))
        m1t = c.stack.enter_context(nc.sbuf_tensor("m1tL", [16, D], BF16))
        wslc = [c.stack.enter_context(nc.sbuf_tensor(f"wslcL{i}", [128, 512], BF16))
                for i in range(2)]
        stg = c.stack.enter_context(nc.sbuf_tensor("stg2L", [128, 2 * 512], BF16))
        psl = [c.stack.enter_context(nc.psum_tensor(c.uname("pslL"), [128, 512], F32))
               for i in range(2)]

        def ldw(g, fin):
            fin(g.dma_start(fbT[:, :], io["w_fb_T"]), 16)
            fin(g.dma_start(vb[:, :], io["w_lrv"]), 16)
            fin(g.dma_start(uTb[:, :], io["w_lruT"]), 16)
        sg.step("gpsimd", [], ldw)

        def m1mm(t, fin):
            t.matmul(psl[0][0:16, :], fbT[:, :], vb[:, 0:512], start=True, stop=True)
            fin(t.matmul(psl[1][0:16, :], fbT[:, :], vb[:, 512:1024], start=True, stop=True))
        sg.step("tensor", ["gpsimd", "vector"], m1mm)

        def m1ev(v, fin):
            v.tensor_copy(m1t[:, 0:512], psl[0][0:16, :])
            fin(v.tensor_copy(m1t[:, 512:1024], psl[1][0:16, :]))
        sg.step("vector", ["tensor"], m1ev)

        items = [(dc, fsl) for dc in range(8) for fsl in range(8)]
        N64 = len(items)
        p0 = c.val("pp")
        q0 = c.val("cc")
        ev0 = c.val("eV")
        ey0 = c.val("eY")
        yld = {}
        yst = {}

        def _ld(s, i):
            dc, fsl = items[i]
            c.inc("eY", s.dma_start(wslc[i % 2][:, :],
                                    wfcTd[dc, :, 512 * fsl:512 * (fsl + 1)]), 16)
            yld[i] = c.val("eY")

        def syncsec(s):
            _ld(s, 0)
            _ld(s, 1)
            for i in range(N64):
                dc, fsl = items[i]
                c.wait(s, "cc", q0 + i + 1)
                c.inc("eY", s.dma_start(
                    wcd[4 * fsl:4 * (fsl + 1), dc].rearrange("ff p t -> p ff t"),
                    stg[:, 512 * (i % 2):512 * (i % 2 + 1)].rearrange(
                        "p (ff t) -> p ff t", ff=4)), 16)
                yst[i] = c.val("eY")
                if i + 2 < N64:
                    _ld(s, i + 2)
        block.sync(syncsec)

        def prodsec(t):
            t.wait_ge(c.sem("eV")[0], ev0)
            for i in range(N64):
                dc, fsl = items[i]
                w = q0 + i - 1
                if w > q0:
                    c.wait(t, "cc", w)
                c.inc("pp", t.matmul(psl[i % 2][:, :], m1t[:, 128 * dc:128 * (dc + 1)],
                                     uTb[:, 512 * fsl:512 * (fsl + 1)],
                                     start=True, stop=True))
        block.tensor(prodsec)

        def conssec(v):
            for i in range(N64):
                c.wait(v, "pp", p0 + i + 1)
                c.wait(v, "eY", yld[i])
                if i >= 2:
                    c.wait(v, "eY", yst[i - 2])
                c.inc("cc", v.scalar_tensor_tensor(
                    stg[:, 512 * (i % 2):512 * (i % 2 + 1)],
                    psl[i % 2][:, :], gcol[:, 0:1], wslc[i % 2][:, :],
                    ALU.mult, ALU.add))
        block.vector(conssec)

        def fini(s):
            c.wait(s, "eY")
        block.sync(fini)
    c.pop()

    return dict(wcd=wcd, wpd=wpd)


def seg_l1f_final(c, io, scr):
    nc = c.nc
    c.push()
    with nc.Block() as block:
        sg = Seg(c, block)
        x1t = [c.stack.enter_context(nc.sbuf_tensor(f"x1tF{i}", [128, D], F32))
               for i in range(2)]
        mbt = [c.stack.enter_context(nc.sbuf_tensor(f"mbtF{i}", [128, D], BF16))
               for i in range(2)]
        xnt = [c.stack.enter_context(nc.sbuf_tensor(f"xntF{i}", [128, D], BF16))
               for i in range(2)]
        ff2 = [c.stack.enter_context(nc.sbuf_tensor(f"ff2F{i}", [128, D], BF16))
               for i in range(2)]
        ot = [c.stack.enter_context(nc.sbuf_tensor(f"otF{i}", [128, D], F32))
              for i in range(2)]

        for g in range(NTC):
            b = g % 2

            def ldf(s, fin, g=g, b=b):
                fin(s.dma_start(x1t[b][:, :], scr["s_x1"][g]), 16)
                fin(s.dma_start(ff2[b][:, :], scr["s_ffn2"][g]), 16)
                fin(s.dma_start(mbt[b][:, :], scr["s_mamba"][g]), 16)
                fin(s.dma_start(xnt[b][:, :], scr["s_xn1"][g]), 16)
            sg.step("sync", ["vector"], ldf)

            def fsum(v, fin, g=g, b=b):
                v.tensor_tensor(ot[b][:, :], x1t[b][:, :], ff2[b][:, :], ALU.add)
                v.tensor_tensor(ot[b][:, :], ot[b][:, :], mbt[b][:, :], ALU.add)
                fin(v.tensor_tensor(ot[b][:, :], ot[b][:, :], xnt[b][:, :], ALU.add))
            sg.step("vector", ["sync"], fsum)

            def sto(s, fin, g=g, b=b):
                fin(s.dma_start(io["out"][128 * g:128 * (g + 1), :], ot[b][:, :]), 16)
            sg.step("sync", ["vector"], sto)
        sg.barrier_wait("vector", ["sync"])
    c.pop()
